# version 15
# baseline (speedup 1.0000x reference)
import numpy as np
from contextlib import ExitStack
import concourse.bass as bass
import concourse.mybir as mybir
from concourse.bass_utils import run_bass_kernel_spmd

F32 = mybir.dt.float32
BF16 = mybir.dt.bfloat16
F16 = mybir.dt.float16
F8 = mybir.dt.float8e4
I8 = mybir.dt.int8
I16 = mybir.dt.int16
ACT = mybir.ActivationFunctionType
ALU = mybir.AluOpType
AXX = mybir.AxisListType.X

D = 1024
NCH = 8
NOWN = 2048
NTOT = 3584
NOE = 3072
NEXT_ = 1024
NEG = -1.0e30
EPS = 1e-5


class Prog:
    ENGINES = ("tensor", "vector", "scalar", "sync", "gpsimd")

    def __init__(self):
        self.ops = {e: [] for e in self.ENGINES}
        self.cnt = {e: 0 for e in self.ENGINES}
        self.hwm = {e: {x: 0 for x in self.ENGINES} for e in self.ENGINES}

    def emit(self, eng, fn, deps=(), selfdep=False, inc=1):
        waits = []
        dd = [d for d in deps if d is not None]
        if selfdep and self.cnt[eng] > 0:
            dd.append((eng, self.cnt[eng]))
        for peng, pval in dd:
            if pval <= 0:
                continue
            if peng == eng:
                pass
            if self.hwm[eng][peng] < pval:
                self.hwm[eng][peng] = pval
                waits.append((peng, pval))
        self.ops[eng].append((waits, fn, inc))
        if fn is not None and inc:
            self.cnt[eng] += inc
            return (eng, self.cnt[eng])
        return None

    def barrier(self):
        snap = dict(self.cnt)
        for e in self.ENGINES:
            self.emit(e, None, deps=[(p, snap[p]) for p in self.ENGINES if p != e],
                      inc=0)

    def run(self, block, sems):
        mult = {e: (16 if e in ("sync", "gpsimd") else 1) for e in self.ENGINES}

        def mk(eng_name):
            def body(eng):
                for waits, fn, inc in self.ops[eng_name]:
                    for peng, pval in waits:
                        eng.wait_ge(sems[peng], pval * mult[peng])
                    if fn is not None:
                        inst = fn(eng)
                        if inc:
                            inst.then_inc(sems[eng_name], inc * mult[eng_name])
            return body
        block.tensor(mk("tensor"))
        block.vector(mk("vector"))
        block.scalar(mk("scalar"))
        block.sync(mk("sync"))
        block.gpsimd(mk("gpsimd"))


def build_nc(dbg=None):
    nc = bass.Bass("TRN2", target_bir_lowering=False)
    P = Prog()

    h_xin = nc.dram_tensor("xin", [D, NTOT], F32, kind="ExternalInput").ap()
    h_w = {}
    for nm in ("q0", "k0", "v0", "o0", "q1", "k1", "v1", "o1", "f0"):
        h_w[nm] = nc.dram_tensor("w_" + nm, [D, D], F32, kind="ExternalInput").ap()
    h_w1 = nc.dram_tensor("w_moe1", [8, D, 4096], F32, kind="ExternalInput").ap()
    h_w2 = nc.dram_tensor("w_moe2", [8, 4096, D], F32, kind="ExternalInput").ap()
    h_wg = nc.dram_tensor("w_gate", [D, 8], F32, kind="ExternalInput").ap()
    h_bg = nc.dram_tensor("b_gate", [8, 1], F32, kind="ExternalInput").ap()
    h_b1 = nc.dram_tensor("b_moe1", [128, 256], F32, kind="ExternalInput").ap()
    h_b2 = nc.dram_tensor("b_moe2", [8, D], F32, kind="ExternalInput").ap()
    h_b0 = nc.dram_tensor("b_ffn0", [128, 8], F32, kind="ExternalInput").ap()
    h_mq = nc.dram_tensor("mq_bf", [128, 1152], BF16, kind="ExternalInput").ap()
    h_bcm = nc.dram_tensor("bcm_bf", [128, 2048], BF16, kind="ExternalInput").ap()
    h_idf = nc.dram_tensor("ident_f", [128, 128], F32, kind="ExternalInput").ap()
    h_idb = nc.dram_tensor("ident_b", [128, 128], BF16, kind="ExternalInput").ap()
    h_onc = nc.dram_tensor("ones_c", [128, 1], F32, kind="ExternalInput").ap()
    h_onr = nc.dram_tensor("ones_r", [33, 128], F32, kind="ExternalInput").ap()
    h_out = nc.dram_tensor("out", [NOWN, D], F32, kind="ExternalOutput").ap()
    h_dbg = None
    if dbg is not None:
        h_dbg = nc.dram_tensor("dbgout", [D, NOE], F32, kind="ExternalOutput").ap()

    es = ExitStack()
    sbt = lambda nm, shp, dt: es.enter_context(nc.sbuf_tensor(nm, shp, dt))
    pst = lambda nm, shp: es.enter_context(nc.psum_tensor(nm, shp, F32))

    AR1 = sbt("AR1", [128, 24576], F32)
    AR2 = sbt("AR2", [128, 14336], F32)
    AR3 = sbt("AR3", [128, 3584], F32)
    WS = [sbt("ws0", [128, 2048], F32), sbt("ws1", [128, 2048], F32)]
    WBF = sbt("wbf", [128, 2048], BF16)
    MQ = sbt("mqbf", [128, 1152], BF16)
    IDF = sbt("identf", [128, 128], F32)
    IDB = sbt("identb", [128, 128], BF16)
    ONC = sbt("onesc", [128, 1], F32)
    ONR = sbt("onesr", [33, 128], F32)
    B1 = sbt("b1sb", [128, 256], F32)
    B0 = sbt("b0sb", [128, 8], F32)
    BG = sbt("bgsb", [8, 1], F32)
    QS1 = sbt("qs1", [128, 512], F32)
    QS2 = sbt("qs2", [128, 512], F32)
    QS3 = sbt("qs3", [128, 1024], F32)
    GT = QS3
    PNB = sbt("pnb", [128, 1024], BF16)
    PTB = sbt("ptb", [128, 1024], BF16)
    RMAX = sbt("rmax", [128, 8], F32)
    RSUM = sbt("rsum", [128, 4], F32)
    RINV = sbt("rinv", [128, 2], F32)
    Q8 = sbt("q8t", [128, 1024], F8)
    I8T = sbt("i8t", [128, 1024], I8)
    F16T = sbt("f16t", [128, 512], F16)
    I16T = sbt("i16t", [128, 512], I16)

    AR1B = AR1.bitcast(BF16)
    AR2B = AR2.bitcast(BF16)
    AR3B = AR3.bitcast(BF16)

    pA = pst("pA", [128, 512])
    pB = pst("pB", [128, 512])
    pC = pst("pC", [128, 512])
    pD = pst("pD", [128, 512])
    pE = pst("pE", [128, 512])
    pF = pst("pF", [128, 512])
    pG = pst("pG", [128, 128])
    pH = pst("pH", [128, 128])

    pFB = pF.bitcast(BF16)
    pDB = pD.bitcast(BF16)
    pEB = pE.bitcast(BF16)
    pGB = pG.bitcast(BF16)
    sems = {e: es.enter_context(nc.semaphore("sem_" + e)) for e in Prog.ENGINES}

    TT = lambda fn, deps=(): P.emit("tensor", fn, deps=deps)
    VV = lambda fn, deps=(): P.emit("vector", fn, deps=deps, selfdep=True)
    SS = lambda fn, deps=(): P.emit("scalar", fn, deps=deps, selfdep=True)
    DY = lambda fn, deps=(): P.emit("sync", fn, deps=deps)
    DG = lambda fn, deps=(): P.emit("gpsimd", fn, deps=deps)
    TSNAP = lambda: ("tensor", P.cnt["tensor"])
    VSNAP = lambda: ("vector", P.cnt["vector"])
    SSNAP = lambda: ("scalar", P.cnt["scalar"])

    pfree = {id(p): None for p in (pA, pB, pC, pD, pE, pF, pG, pH)}
    ws_last = [None, None]
    wbf_last = [None, None]
    wbf_tlast = [None, None]

    DG(lambda g: g.dma_start(out=MQ[:], in_=h_mq[:, :]))
    DG(lambda g: g.dma_start(out=AR2.bitcast(BF16)[:, 19456:21504], in_=h_bcm[:, :]))
    DG(lambda g: g.dma_start(out=IDF[:], in_=h_idf[:, :]))
    DG(lambda g: g.dma_start(out=IDB[:], in_=h_idb[:, :]))
    DG(lambda g: g.dma_start(out=ONC[:], in_=h_onc[:, :]))
    DG(lambda g: g.dma_start(out=ONR[:], in_=h_onr[:, :]))
    DG(lambda g: g.dma_start(out=B1[:], in_=h_b1[:, :]))
    DG(lambda g: g.dma_start(out=B0[:], in_=h_b0[:, :]))
    DG(lambda g: g.dma_start(out=BG[:], in_=h_bg[:, :]))
    P.barrier()

    src_cons = [None, None]

    def ln_stats(src, n, accx, accsq, rows=None):
        if rows is None:
            rows = AR3
        sq = (QS1[:, 0:512], QS2[:, 0:512])
        sq_cons = [None, None]
        for dc in range(NCH):
            ap, dtok = src(dc)
            if dc == 0:
                VV(lambda v, ap=ap: v.tensor_copy(out=accx[:, 0:n], in_=ap),
                   deps=[dtok])
            else:
                VV(lambda v, ap=ap: v.tensor_tensor(out=accx[:, 0:n], in0=accx[:, 0:n],
                                                    in1=ap, op=ALU.add), deps=[dtok])
            for tt in range((n + 511) // 512):
                c0 = tt * 512
                w = min(512, n - c0)
                qp = (dc + tt) % 2
                ts = SS(lambda s, ap=ap, c0=c0, w=w, qp=qp: s.activation(
                    out=sq[qp][:, 0:w], in_=ap[:, c0:c0 + w], func=ACT.Square),
                    deps=[sq_cons[qp], dtok])
                if dc == 0:
                    av = VV(lambda v, c0=c0, w=w, qp=qp: v.tensor_copy(
                        out=accsq[:, c0:c0 + w], in_=sq[qp][:, 0:w]), deps=[ts])
                else:
                    av = VV(lambda v, c0=c0, w=w, qp=qp: v.tensor_tensor(
                        out=accsq[:, c0:c0 + w], in0=accsq[:, c0:c0 + w],
                        in1=sq[qp][:, 0:w], op=ALU.add), deps=[ts])
                sq_cons[qp] = av
            src_cons[dc % 2] = av
        vacc = VSNAP()
        evs = []
        for tt in range((n + 511) // 512):
            c0 = tt * 512
            w = min(512, n - c0)
            m1 = TT(lambda t, c0=c0, w=w: t.matmul(
                out=pE[0:1, 0:w], lhsT=ONC[:, :], rhs=accx[:, c0:c0 + w],
                start=True, stop=True), deps=[vacc, pfree[id(pE)]])
            e1 = SS(lambda s, c0=c0, w=w: s.copy(out=rows[0:1, c0:c0 + w],
                                                 in_=pE[0:1, 0:w]), deps=[m1])
            pfree[id(pE)] = e1
            m2 = TT(lambda t, c0=c0, w=w: t.matmul(
                out=pF[0:1, 0:w], lhsT=ONC[:, :], rhs=accsq[:, c0:c0 + w],
                start=True, stop=True), deps=[vacc, pfree[id(pF)]])
            e2 = SS(lambda s, c0=c0, w=w: s.copy(out=rows[32:33, c0:c0 + w],
                                                 in_=pF[0:1, 0:w]), deps=[m2])
            pfree[id(pF)] = e2
            evs += [e1, e2]
        return evs + [SSNAP()]

    def bcast_stats(c0, w, stok, rows=None, dst1=None, dst2=None):
        if rows is None:
            rows = AR3
        if dst1 is None:
            dst1 = QS1[:, 0:w]
        if dst2 is None:
            dst2 = QS2[:, 0:w]
        war = [VSNAP(), TSNAP()]
        m1 = TT(lambda t, c0=c0, w=w: t.matmul(
            out=pE[:, 0:w], lhsT=ONR[0:1, :], rhs=rows[0:1, c0:c0 + w],
            start=True, stop=True), deps=list(stok) + [pfree[id(pE)]])
        e1 = SS(lambda s, w=w, dst1=dst1: s.copy(out=dst1, in_=pE[:, 0:w]),
                deps=[m1] + war)
        pfree[id(pE)] = e1
        m2 = TT(lambda t, c0=c0, w=w: t.matmul(
            out=pF[:, 0:w], lhsT=ONR[32:33, :], rhs=rows[32:33, c0:c0 + w],
            start=True, stop=True), deps=list(stok) + [pfree[id(pF)]])
        e2 = SS(lambda s, w=w, dst2=dst2: s.copy(out=dst2, in_=pF[:, 0:w]),
                deps=[m2])
        pfree[id(pF)] = e2
        tmp = QS3[:, 512:512 + w]
        VV(lambda v, dst1=dst1: v.tensor_scalar(out=dst1, in0=dst1,
                                                scalar1=-1.0 / D, scalar2=None,
                                                op0=ALU.mult), deps=[e1])
        VV(lambda v, dst2=dst2: v.tensor_scalar(out=dst2, in0=dst2,
                                                scalar1=1.0 / D, scalar2=EPS,
                                                op0=ALU.mult, op1=ALU.add),
           deps=[e2])
        VV(lambda v, dst1=dst1, tmp=tmp: v.tensor_tensor(out=tmp, in0=dst1,
                                                         in1=dst1, op=ALU.mult))
        tv = VV(lambda v, dst2=dst2, tmp=tmp: v.tensor_tensor(
            out=dst2, in0=dst2, in1=tmp, op=ALU.subtract))
        tq = SS(lambda s, dst2=dst2: s.activation(out=dst2, in_=dst2,
                                                  func=ACT.Sqrt), deps=[tv])
        tr = VV(lambda v, dst2=dst2: v.reciprocal(out=dst2, in_=dst2), deps=[tq])
        return [tr, e1]

    def load_wslice(hfn, par, nf32, prior=(), wb0=0):
        d = DY(lambda y, hfn=hfn, par=par, nf32=nf32: y.dma_start(
            out=WS[par][:, 0:nf32], in_=hfn()), deps=[ws_last[par]] + list(prior))
        if nf32 <= 1024:
            wdeps = [wbf_last[wb0 // 1024], wbf_tlast[wb0 // 1024]]
        else:
            wdeps = [wbf_last[0], wbf_last[1], wbf_tlast[0], wbf_tlast[1]]
        c = SS(lambda s, par=par, nf32=nf32, wb0=wb0: s.copy(
            out=WBF[:, wb0:wb0 + nf32], in_=WS[par][:, 0:nf32]),
            deps=[d] + wdeps)
        ws_last[par] = c
        return c

    att_st = {}

    def attn_reset():
        att_st.clear()
        for p in (0, 1):
            att_st[p] = dict(exp=None, nrm=None, tr=None, pte=None, avm=None,
                             oev=None, rmx=None, rcp=None)

    def attn_s1(n, qap, kap_fn, nt, mask_a, mask_b, v_fn, o_ap, cfg):
        p = n % 2
        st = att_st[p]
        wk = nt * 128
        w0 = min(wk, 512)
        wb = wk - w0
        sA = cfg["sA"][p]
        sB = cfg["sB"][p]
        m1 = TT(lambda t: t.matmul(out=sA[:, 0:w0], lhsT=qap, rhs=kap_fn(0, w0),
                                   start=True, stop=(mask_a is None)),
                deps=[st["exp"], st["rmx"]])
        if mask_a is not None:
            m1 = TT(lambda t: t.matmul(out=sA[:, 0:w0], lhsT=IDB[:, :], rhs=mask_a,
                                       start=False, stop=True))
        m2 = None
        if wb > 0:
            m2 = TT(lambda t: t.matmul(out=sB[:, 0:wb], lhsT=qap, rhs=kap_fn(w0, wb),
                                       start=True, stop=(mask_b is None)))
            if mask_b is not None:
                m2 = TT(lambda t: t.matmul(out=sB[:, 0:wb], lhsT=IDB[:, :],
                                           rhs=mask_b, start=False, stop=True))
        r1 = VV(lambda v: v.tensor_reduce(out=RMAX[:, 4 * p:4 * p + 1],
                                          in_=sA[:, 0:w0], axis=AXX,
                                          op=ALU.max, negate=True),
                deps=[m1, st["exp"]])
        rc = r1
        rcol = 4 * p
        if wb > 0:
            r2 = VV(lambda v: v.tensor_reduce(out=RMAX[:, 4 * p + 1:4 * p + 2],
                                              in_=sB[:, 0:wb], axis=AXX,
                                              op=ALU.max, negate=True), deps=[m2])
            rc = VV(lambda v: v.tensor_tensor(out=RMAX[:, 4 * p + 2:4 * p + 3],
                                              in0=RMAX[:, 4 * p:4 * p + 1],
                                              in1=RMAX[:, 4 * p + 1:4 * p + 2],
                                              op=ALU.min))
            rcol = 4 * p + 2
        st["rmx"] = rc
        return dict(n=n, nt=nt, wk=wk, w0=w0, wb=wb, vfn=v_fn, oap=o_ap, cfg=cfg,
                    rc=rc, rcol=rcol, m1=m1, m2=m2)

    def attn_s2a(cx):
        p = cx["n"] % 2
        st = att_st[p]
        cfg = cx["cfg"]
        e1 = SS(lambda s: s.activation(out=cfg["ea"][p][:, 0:cx["w0"]],
                                       in_=cfg["sA"][p][:, 0:cx["w0"]],
                                       func=ACT.Exp,
                                       bias=RMAX[:, cx["rcol"]:cx["rcol"] + 1],
                                       accum_out=RSUM[:, 2 * p:2 * p + 1]),
                deps=[cx["rc"], st["rcp"], st["nrm"]])
        elast = e1
        if cx["wb"] > 0:
            elast = SS(lambda s: s.activation(out=cfg["eb"][p][:, 0:cx["wb"]],
                                              in_=cfg["sB"][p][:, 0:cx["wb"]],
                                              func=ACT.Exp,
                                              bias=RMAX[:, cx["rcol"]:cx["rcol"] + 1],
                                              accum_out=RSUM[:, 2 * p + 1:2 * p + 2]))
        st["exp"] = elast
        cx["elast"] = elast

    def attn_s2b(cx):
        p = cx["n"] % 2
        st = att_st[p]
        if cx["wb"] > 0:
            sadd = VV(lambda v: v.tensor_tensor(out=RSUM[:, 2 * p:2 * p + 1],
                                                in0=RSUM[:, 2 * p:2 * p + 1],
                                                in1=RSUM[:, 2 * p + 1:2 * p + 2],
                                                op=ALU.add), deps=[cx["elast"]])
        else:
            sadd = cx["elast"]
        rcp = VV(lambda v: v.reciprocal(out=RINV[:, p:p + 1],
                                        in_=RSUM[:, 2 * p:2 * p + 1]), deps=[sadd])
        st["rcp"] = rcp
        cx["rcp"] = rcp

    def attn_s3(cx):
        p = cx["n"] % 2
        st = att_st[p]
        cfg = cx["cfg"]
        pnb = cfg["pnb"][p]
        trb = cfg["trb"][p]
        nt, wk, w0, wb = cx["nt"], cx["wk"], cx["w0"], cx["wb"]
        na = SS(lambda s: s.activation(out=pnb[:, 0:w0], in_=cfg["ea"][p][:, 0:w0],
                                       func=ACT.Copy, scale=RINV[:, p:p + 1]),
                deps=[cx["rcp"], st["tr"]])
        nrm = na
        if wb > 0:
            nrm = SS(lambda s: s.activation(out=pnb[:, w0:wk],
                                            in_=cfg["eb"][p][:, 0:wb],
                                            func=ACT.Copy, scale=RINV[:, p:p + 1]))
        st["nrm"] = nrm
        tts = []
        for d_ in range(nt):
            t = TT(lambda t_, d_=d_: t_.transpose(
                out=trb[:, d_ * 128:(d_ + 1) * 128],
                in_=pnb[:, d_ * 128:(d_ + 1) * 128], identity=IDB[:, :]),
                deps=[nrm, st["pte"]] if d_ == 0 else [nrm])
            tts.append(t)
        st["tr"] = tts[-1]
        if p == 0:
            pte = SS(lambda s: s.copy(out=cfg["ptb"][p][:, 0:wk],
                                      in_=trb[:, 0:wk]), deps=tts + [st["avm"]])
        else:
            pte = VV(lambda v: v.tensor_copy(out=cfg["ptb"][p][:, 0:wk],
                                             in_=trb[:, 0:wk]),
                     deps=tts + [st["avm"]])
        st["pte"] = pte
        cx["pte"] = pte

    def attn_s4(cx):
        p = cx["n"] % 2
        st = att_st[p]
        cfg = cx["cfg"]
        ptb = cfg["ptb"][p]
        avp = cfg["avs"][p]
        avm = None
        for d_ in range(cx["nt"]):
            avm = TT(lambda t_, d_=d_: t_.matmul(
                out=avp, lhsT=cx["vfn"](d_), rhs=ptb[:, d_ * 128:(d_ + 1) * 128],
                start=(d_ == 0), stop=(d_ == cx["nt"] - 1)),
                deps=[cx["pte"], st["oev"]] if d_ == 0 else [])
        st["avm"] = avm
        if p == 0:
            oev = VV(lambda v: v.tensor_copy(out=cx["oap"], in_=avp), deps=[avm])
        else:
            oev = SS(lambda s: s.copy(out=cx["oap"], in_=avp), deps=[avm])
        st["oev"] = oev

    def attn_drive(tiles):
        ctxs = []
        N = len(tiles)
        for i in range(N + 4):
            if i >= 4:
                attn_s4(ctxs[i - 4])
            if 3 <= i < N + 3:
                attn_s3(ctxs[i - 3])
            if 2 <= i < N + 2:
                attn_s2b(ctxs[i - 2])
            if 1 <= i < N + 1:
                attn_s2a(ctxs[i - 1])
            if i < N:
                ctxs.append(attn_s1(*tiles[i]))

    xln_ap = lambda dc, c0, w: AR1B[:, dc * 3584 + c0: dc * 3584 + c0 + w]
    xbuf = [AR1[:, 14336:17920], AR1[:, 17920:21504]]

    def srcA_final(dc):
        par = dc % 2
        tk = DG(lambda g, dc=dc, par=par: g.dma_start(
            out=xbuf[par][:, :], in_=h_xin[dc * 128:(dc + 1) * 128, :]),
            deps=[src_cons[par]])
        return xbuf[par], tk

    stok = ln_stats(srcA_final, NTOT, AR2[:, 0:3584], AR2[:, 3584:7168])
    P.barrier()
    MB = AR2[:, 0:3584]
    RB = AR2[:, 3584:7168]
    for tt in range(7):
        bcast_stats(tt * 512, 512, stok, dst1=MB[:, tt * 512:(tt + 1) * 512],
                    dst2=RB[:, tt * 512:(tt + 1) * 512])
    P.barrier()
    for dc in range(NCH):
        ap, tk = srcA_final(dc)
        q0 = (dc % 2) * 512
        for tt in range(7):
            c0 = tt * 512
            VV(lambda v, ap=ap, c0=c0, q0=q0: v.tensor_tensor(
                out=QS3[:, q0:q0 + 512], in0=ap[:, c0:c0 + 512],
                in1=MB[:, c0:c0 + 512], op=ALU.add), deps=[tk])
            lastap = VV(lambda v, dc=dc, c0=c0, q0=q0: v.tensor_tensor(
                out=xln_ap(dc, c0, 512), in0=QS3[:, q0:q0 + 512],
                in1=RB[:, c0:c0 + 512], op=ALU.mult))
        src_cons[dc % 2] = lastap
    P.barrier()

    def o_ap(ch):
        if ch < 6:
            return AR1B[:, 28672 + ch * 3072: 28672 + (ch + 1) * 3072]
        return AR2B[:, 13312 + (ch - 6) * 3072: 13312 + (ch - 5) * 3072]

    v_ap = lambda kt: AR3B[:, kt * 256:(kt + 1) * 256]

    for g in range(4):
        if g > 0:
            P.barrier()
        cw = load_wslice(lambda g=g: h_w["q0"][:, g * 256:(g + 1) * 256].rearrange(
            "(c p) m -> p c m", p=128), g % 2, 2048)
        for cc in range(2):
            for tt in range(6):
                ps = (pA, pB, pC, pD)[(cc * 6 + tt) % 4]
                for dc in range(NCH):
                    t = TT(lambda t_, dc=dc, cc=cc, tt=tt, ps=ps: t_.matmul(
                        out=ps[:, 0:512],
                        lhsT=WBF[:, dc * 256 + cc * 128: dc * 256 + cc * 128 + 128],
                        rhs=xln_ap(dc, 512 + tt * 512, 512),
                        start=(dc == 0), stop=(dc == NCH - 1)),
                        deps=[cw, pfree[id(ps)]] if dc == 0 else [])
                ev = SS(lambda s, cc=cc, tt=tt, ps=ps: s.activation(
                    out=AR2B[:, cc * 3072 + tt * 512: cc * 3072 + (tt + 1) * 512],
                    in_=ps[:, 0:512], func=ACT.Copy, scale=0.125), deps=[t])
                pfree[id(ps)] = ev
        wbf_last[0] = wbf_last[1] = SSNAP()
        wbf_tlast[0] = wbf_tlast[1] = TSNAP()
        cw = load_wslice(lambda g=g: h_w["k0"][:, g * 256:(g + 1) * 256].rearrange(
            "(c p) m -> p c m", p=128), (g + 1) % 2, 2048)
        for cc in range(2):
            for tt in range(7):
                ps = (pA, pB, pC, pD)[(cc * 7 + tt) % 4]
                for dc in range(NCH):
                    t = TT(lambda t_, dc=dc, cc=cc, tt=tt, ps=ps: t_.matmul(
                        out=ps[:, 0:512],
                        lhsT=WBF[:, dc * 256 + cc * 128: dc * 256 + cc * 128 + 128],
                        rhs=xln_ap(dc, tt * 512, 512),
                        start=(dc == 0), stop=(dc == NCH - 1)),
                        deps=[cw, pfree[id(ps)]] if dc == 0 else [])
                ev = SS(lambda s, cc=cc, tt=tt, ps=ps: s.copy(
                    out=AR2B[:, 6144 + cc * 3584 + tt * 512:
                             6144 + cc * 3584 + (tt + 1) * 512],
                    in_=ps[:, 0:512]), deps=[t])
                pfree[id(ps)] = ev
        wbf_last[0] = wbf_last[1] = SSNAP()
        wbf_tlast[0] = wbf_tlast[1] = TSNAP()
        cw = load_wslice(lambda g=g: h_w["v0"][:, g * 256:(g + 1) * 256].rearrange(
            "(c p) m -> p c m", p=128), g % 2, 2048)
        for kt in range(28):
            ps = (pA, pB, pC, pD)[kt % 4]
            for dc in range(NCH):
                t = TT(lambda t_, dc=dc, kt=kt, ps=ps: t_.matmul(
                    out=ps[:, 0:256], lhsT=xln_ap(dc, kt * 128, 128),
                    rhs=WBF[:, dc * 256:(dc + 1) * 256],
                    start=(dc == 0), stop=(dc == NCH - 1)),
                    deps=[cw, pfree[id(ps)]] if dc == 0 else [])
            ev = SS(lambda s, kt=kt, ps=ps: s.copy(out=v_ap(kt), in_=ps[:, 0:256]),
                    deps=[t])
            pfree[id(ps)] = ev
        wbf_last[0] = wbf_last[1] = SSNAP()
        wbf_tlast[0] = wbf_tlast[1] = TSNAP()

        proj_done = [TSNAP(), SSNAP()]
        attn_reset()
        BCMB = AR2B[:, 19456:21504]
        cfgL0 = dict(sA=(pC, pE), sB=(pA, pB),
                     ea=(QS3[:, 0:512], QS1[:, 0:512]),
                     eb=(QS3[:, 512:1024], QS2[:, 0:512]),
                     pnb=(PNB[:, 0:640], AR2B[:, 21504:22144]),
                     ptb=(PTB[:, 0:640], AR2B[:, 22144:22784]),
                     trb=(pFB[:, 0:1024], pDB[:, 0:1024]),
                     avs=(pH[0:64, 0:128], pH[64:128, 0:128]))
        tiles = []
        ntile = 0
        for hl in range(4):
            cc = hl // 2
            p0 = 64 * (hl % 2)
            ch = 2 * g + cc
            qh = lambda c0, w, cc=cc, p0=p0: AR2B[p0:p0 + 64,
                                                  cc * 3072 + c0: cc * 3072 + c0 + w]
            kh = lambda c0, w, cc=cc, p0=p0: AR2B[p0:p0 + 64,
                                                  6144 + cc * 3584 + c0:
                                                  6144 + cc * 3584 + c0 + w]
            for qt in range(16):
                ma = BCMB[:, qt * 512:(qt + 1) * 512] if qt <= 3 else MQ[:, 512:1024]
                tiles.append((ntile, qh(qt * 128, 128),
                              lambda c0, w, qt=qt, kh=kh: kh(qt * 128 + c0, w),
                              5, ma, MQ[:, 1024:1152],
                              lambda d, qt=qt, hl=hl: v_ap(qt + d)[:, hl * 64:(hl + 1) * 64],
                              o_ap(ch)[p0:p0 + 64, qt * 128:(qt + 1) * 128], cfgL0))
                ntile += 1
            for qt in range(8):
                nt = min(qt, 4) + 1
                j0 = max(0, qt - 4)
                wk = nt * 128
                if nt == 5:
                    ma, mb = MQ[:, 512:1024], MQ[:, 1024:1152]
                else:
                    ma, mb = MQ[:, 1152 - wk:1152], None
                tiles.append((ntile, qh(2048 + qt * 128, 128),
                              lambda c0, w, j0=j0, kh=kh: kh(2560 + j0 * 128 + c0, w),
                              nt, ma, mb,
                              lambda d, j0=j0, hl=hl: v_ap(20 + j0 + d)[:, hl * 64:(hl + 1) * 64],
                              o_ap(ch)[p0:p0 + 64, 2048 + qt * 128: 2048 + (qt + 1) * 128],
                              cfgL0))
                ntile += 1
        for p_ in (0, 1):
            att_st[p_]["exp"] = proj_done[1]
            att_st[p_]["rmx"] = proj_done[0]
        attn_drive(tiles)
    P.barrier()

    def o2_ap(ch):
        if ch < 4:
            return AR2B[:, ch * 3072:(ch + 1) * 3072]
        if ch < 6:
            return AR2B[:, 19456 + (ch - 4) * 3072: 19456 + (ch - 3) * 3072]
        return AR2B[:, 13312 + (ch - 6) * 3072: 13312 + (ch - 5) * 3072]

    def wo2_ap(dk):
        if dk < 3:
            return AR2B[:, 25600 + dk * 1024: 25600 + (dk + 1) * 1024]
        return AR3B[:, (dk - 3) * 1024: (dk - 2) * 1024]

    for ch in range(6):
        VV(lambda v, ch=ch: v.tensor_copy(out=o2_ap(ch), in_=o_ap(ch)))
    omove = VSNAP()
    for i in range(4):
        par = i % 2
        d = DY(lambda y, i=i, par=par: y.dma_start(
            out=WS[par][:, 0:2048],
            in_=h_w["o0"][256 * i:256 * (i + 1), :].rearrange(
                "(c p) m -> p c m", p=128)), deps=[ws_last[par], omove])
        c1 = SS(lambda s, i=i, par=par: s.copy(out=wo2_ap(2 * i),
                                               in_=WS[par][:, 0:1024]), deps=[d])
        c2 = SS(lambda s, i=i, par=par: s.copy(out=wo2_ap(2 * i + 1),
                                               in_=WS[par][:, 1024:2048]), deps=[d])
        ws_last[par] = c2
    P.barrier()
    xr = [QS1, QS2]
    xr_last = [None, None]
    for tt in range(6):
        for oc in range(NCH):
            ps = (pA, pB, pC, pD)[(tt * 8 + oc) % 4]
            par = oc % 2
            dx = DG(lambda g, oc=oc, tt=tt, par=par: g.dma_start(
                out=xr[par][:, 0:512],
                in_=h_xin[oc * 128:(oc + 1) * 128,
                          512 + tt * 512: 512 + (tt + 1) * 512]),
                deps=[xr_last[par]])
            for dk in range(NCH):
                t = TT(lambda t_, dk=dk, oc=oc, tt=tt, ps=ps: t_.matmul(
                    out=ps[:, 0:512], lhsT=wo2_ap(dk)[:, oc * 128:(oc + 1) * 128],
                    rhs=o2_ap(dk)[:, tt * 512:(tt + 1) * 512],
                    start=(dk == 0), stop=(dk == NCH - 1)),
                    deps=[pfree[id(ps)]] if dk == 0 else [])
            av = VV(lambda v, oc=oc, tt=tt, ps=ps, par=par: v.tensor_tensor(
                out=AR1[:, oc * 3072 + tt * 512: oc * 3072 + (tt + 1) * 512],
                in0=ps[:, 0:512], in1=xr[par][:, 0:512], op=ALU.add),
                deps=[t, dx])
            pfree[id(ps)] = av
            xr_last[par] = av
    P.barrier()
    if dbg == "x1":
        for dc in range(NCH):
            DG(lambda g, dc=dc: g.dma_start(
                out=h_dbg[dc * 128:(dc + 1) * 128, :],
                in_=AR1[:, dc * 3072:(dc + 1) * 3072]))

    stok = ln_stats(lambda dc: (AR1[:, dc * 3072:(dc + 1) * 3072], None), NOE,
                    AR2[:, 0:3072], AR2[:, 3072:6144])
    P.barrier()
    xq8_ap = lambda dc, c0, w: AR2B[:, dc * 3072 + c0: dc * 3072 + c0 + w]
    for tt in range(6):
        c0 = tt * 512
        btok = bcast_stats(c0, 512, stok)
        for dc in range(NCH):
            VV(lambda v, dc=dc, c0=c0: v.tensor_tensor(
                out=QS3[:, 0:512], in0=AR1[:, dc * 3072 + c0: dc * 3072 + c0 + 512],
                in1=QS1[:, 0:512], op=ALU.add), deps=btok)
            VV(lambda v: v.tensor_tensor(out=QS3[:, 0:512], in0=QS3[:, 0:512],
                                         in1=QS2[:, 0:512], op=ALU.mult))
            VV(lambda v: v.tensor_copy(out=Q8[:, 0:512], in_=QS3[:, 0:512]))
            VV(lambda v: v.tensor_copy(out=I8T[:, 0:512], in_=Q8[:, 0:512].bitcast(I8)))
            VV(lambda v, dc=dc, c0=c0: v.tensor_copy(
                out=xq8_ap(dc, c0, 512), in_=I8T[:, 0:512].bitcast(F8)))
    P.barrier()
    vlast_d = None
    for oc in range(NCH):
        par = oc % 2
        d = DY(lambda y, oc=oc, par=par: y.dma_start(
            out=WS[par][:, 0:1024],
            in_=h_w["f0"][:, oc * 128:(oc + 1) * 128].rearrange(
                "(c p) m -> p c m", p=128)), deps=[ws_last[par]])
        q1 = SS(lambda s, par=par: s.copy(out=Q8[:, 0:1024], in_=WS[par][:, 0:1024]),
                deps=[d, vlast_d, VSNAP()])
        q2 = SS(lambda s: s.copy(out=I8T[:, 0:1024], in_=Q8[:, 0:1024].bitcast(I8)))
        q3 = SS(lambda s: s.copy(out=WBF[:, 0:1024], in_=I8T[:, 0:1024].bitcast(F8)),
                deps=[wbf_last[0], wbf_tlast[0]])
        ws_last[par] = q3
        for tt in range(6):
            ps = (pA, pB, pC, pD)[(oc * 6 + tt) % 4]
            for dc in range(NCH):
                t = TT(lambda t_, dc=dc, tt=tt, ps=ps: t_.matmul(
                    out=ps[:, 0:512], lhsT=WBF[:, dc * 128:(dc + 1) * 128],
                    rhs=xq8_ap(dc, tt * 512, 512),
                    start=(dc == 0), stop=(dc == NCH - 1)),
                    deps=[q3, pfree[id(ps)]] if dc == 0 else [])
            f1 = SS(lambda s, ps=ps: s.copy(out=F16T[:, 0:512], in_=ps[:, 0:512]),
                    deps=[t, vlast_d])
            f2 = SS(lambda s: s.copy(out=I16T[:, 0:512],
                                     in_=F16T[:, 0:512].bitcast(I16)))
            pfree[id(ps)] = f2
            vlast_d = VV(lambda v, oc=oc, tt=tt: v.scalar_tensor_tensor(
                out=AR1[:, oc * 3072 + tt * 512: oc * 3072 + (tt + 1) * 512],
                in0=I16T[:, 0:512].bitcast(F16),
                scalar=B0[:, oc:oc + 1],
                in1=AR1[:, oc * 3072 + tt * 512: oc * 3072 + (tt + 1) * 512],
                op0=ALU.add, op1=ALU.add), deps=[f2])
        wbf_last[0] = TSNAP()
    P.barrier()
    if dbg == "x2":
        for dc in range(NCH):
            DG(lambda g, dc=dc: g.dma_start(
                out=h_dbg[dc * 128:(dc + 1) * 128, :],
                in_=AR1[:, dc * 3072:(dc + 1) * 3072]))

    stok = ln_stats(lambda dc: (AR1[:, dc * 3072 + 2048:(dc + 1) * 3072], None),
                    NEXT_, AR2[:, 8192:9216], AR2[:, 9216:10240])
    P.barrier()
    xln2e = lambda dc, c0, w: AR2B[:, dc * 1024 + c0: dc * 1024 + c0 + w]
    for tt in range(2):
        c0 = tt * 512
        btok = bcast_stats(c0, 512, stok)
        for dc in range(NCH):
            VV(lambda v, dc=dc, c0=c0: v.tensor_tensor(
                out=QS3[:, 0:512],
                in0=AR1[:, dc * 3072 + 2048 + c0: dc * 3072 + 2048 + c0 + 512],
                in1=QS1[:, 0:512], op=ALU.add), deps=btok)
            VV(lambda v, dc=dc, c0=c0: v.tensor_tensor(
                out=xln2e(dc, c0, 512), in0=QS3[:, 0:512], in1=QS2[:, 0:512],
                op=ALU.mult))
    P.barrier()
    k1_ap = lambda ch, c0, w: AR2B[:, 8192 + ch * 1024 + c0: 8192 + ch * 1024 + c0 + w]
    v1_ap = lambda kt: AR2B[:, 16384 + kt * 1024: 16384 + (kt + 1) * 1024]
    for ch in range(NCH):
        wb0 = (ch % 2) * 1024
        cw = load_wslice(lambda ch=ch: h_w["k1"][:, ch * 128:(ch + 1) * 128].rearrange(
            "(c p) m -> p c m", p=128), ch % 2, 1024, wb0=wb0)
        for tt in range(2):
            ps = (pA, pB, pC, pD)[(ch * 2 + tt) % 4]
            for dc in range(NCH):
                t = TT(lambda t_, dc=dc, tt=tt, ps=ps, wb0=wb0: t_.matmul(
                    out=ps[:, 0:512], lhsT=WBF[:, wb0 + dc * 128: wb0 + (dc + 1) * 128],
                    rhs=xln2e(dc, tt * 512, 512),
                    start=(dc == 0), stop=(dc == NCH - 1)),
                    deps=[cw, pfree[id(ps)]] if dc == 0 else [])
            ev = SS(lambda s, ch=ch, tt=tt, ps=ps: s.copy(
                out=k1_ap(ch, tt * 512, 512), in_=ps[:, 0:512]), deps=[t])
            pfree[id(ps)] = ev
        wbf_last[ch % 2] = SSNAP()
        wbf_tlast[ch % 2] = TSNAP()
    for vg in range(4):
        cw = load_wslice(lambda vg=vg: h_w["v1"][:, vg * 256:(vg + 1) * 256].rearrange(
            "(c p) m -> p c m", p=128), vg % 2, 2048)
        for kt in range(8):
            ps = (pA, pB, pC, pD)[(vg * 8 + kt) % 4]
            for dc in range(NCH):
                t = TT(lambda t_, dc=dc, kt=kt, ps=ps: t_.matmul(
                    out=ps[:, 0:256], lhsT=xln2e(dc, kt * 128, 128),
                    rhs=WBF[:, dc * 256:(dc + 1) * 256],
                    start=(dc == 0), stop=(dc == NCH - 1)),
                    deps=[cw, pfree[id(ps)]] if dc == 0 else [])
            ev = SS(lambda s, kt=kt, vg=vg, ps=ps: s.copy(
                out=v1_ap(kt)[:, vg * 256:(vg + 1) * 256], in_=ps[:, 0:256]),
                deps=[t])
            pfree[id(ps)] = ev
        wbf_last[0] = wbf_last[1] = SSNAP()
        wbf_tlast[0] = wbf_tlast[1] = TSNAP()
    P.barrier()
    stok = ln_stats(lambda dc: (AR1[:, dc * 3072: dc * 3072 + 2048], None), NOWN,
                    AR2[:, 0:2048], AR2[:, 2048:4096], rows=WS[0])
    P.barrier()

    def xo_ap(dc, c0, w):
        if dc < 4:
            return AR2B[:, dc * 2048 + c0: dc * 2048 + c0 + w]
        if dc < 6:
            return AR2B[:, 24576 + (dc - 4) * 2048 + c0:
                        24576 + (dc - 4) * 2048 + c0 + w]
        return AR3B[:, (dc - 6) * 2048 + c0: (dc - 6) * 2048 + c0 + w]

    for tt in range(4):
        c0 = tt * 512
        btok = bcast_stats(c0, 512, stok, rows=WS[0])
        for dc in range(NCH):
            VV(lambda v, dc=dc, c0=c0: v.tensor_tensor(
                out=QS3[:, 0:512], in0=AR1[:, dc * 3072 + c0: dc * 3072 + c0 + 512],
                in1=QS1[:, 0:512], op=ALU.add), deps=btok)
            VV(lambda v, dc=dc, c0=c0: v.tensor_tensor(
                out=xo_ap(dc, c0, 512), in0=QS3[:, 0:512], in1=QS2[:, 0:512],
                op=ALU.mult))
    ws_last[0] = TSNAP()
    P.barrier()
    q1b_ap = lambda ch, c0, w: AR1B[:, (ch * 3072 + 2048) * 2 + c0:
                                    (ch * 3072 + 2048) * 2 + c0 + w]
    for ch in range(NCH):
        _oc4 = ch
        wb0 = (ch % 2) * 1024
        cw = load_wslice(lambda ch=ch: h_w["q1"][:, ch * 128:(ch + 1) * 128].rearrange(
            "(c p) m -> p c m", p=128), ch % 2, 1024, wb0=wb0)
        for tt in range(4):
            ps = (pA, pB, pC, pD)[(_oc4 * 4 + tt) % 4]
            for dc in range(NCH):
                t = TT(lambda t_, dc=dc, tt=tt, ps=ps, wb0=wb0: t_.matmul(
                    out=ps[:, 0:512], lhsT=WBF[:, wb0 + dc * 128: wb0 + (dc + 1) * 128],
                    rhs=xo_ap(dc, tt * 512, 512),
                    start=(dc == 0), stop=(dc == NCH - 1)),
                    deps=[cw, pfree[id(ps)]] if dc == 0 else [])
            ev = SS(lambda s, ch=ch, tt=tt, ps=ps: s.activation(
                out=q1b_ap(ch, tt * 512, 512), in_=ps[:, 0:512], func=ACT.Copy,
                scale=0.125), deps=[t])
            pfree[id(ps)] = ev
        wbf_last[ch % 2] = SSNAP()
        wbf_tlast[ch % 2] = TSNAP()
    P.barrier()
    attn_reset()
    cfgL1 = dict(sA=(pC, pA), sB=(pD, pB),
                 ea=(QS3[:, 0:512], QS1[:, 0:512]),
                 eb=(QS3[:, 512:1024], QS2[:, 0:512]),
                 pnb=(PNB[:, 0:1024], AR3B[:, 4096:5120]),
                 ptb=(PTB[:, 0:1024], AR3B[:, 5120:6144]),
                 trb=(pFB[:, 0:1024], pEB[:, 0:1024]),
                 avs=(pH[0:64, 0:128], pH[64:128, 0:128]))
    tiles = []
    ntile = 0
    for h in range(16):
        ch = h // 2
        p0 = 64 * (h % 2)
        for qt in range(16):
            nt = (qt % 8) + 1
            wk = nt * 128
            if wk <= 512:
                ma, mb = MQ[:, 1152 - wk:1152], None
            else:
                ma, mb = None, MQ[:, 1152 - (wk - 512):1152]
            tiles.append((ntile,
                          q1b_ap(ch, qt * 128, 128)[p0:p0 + 64, :],
                          lambda c0, w, ch=ch, p0=p0: k1_ap(ch, c0, w)[p0:p0 + 64, :],
                          nt, ma, mb,
                          lambda d, h=h: v1_ap(d)[:, h * 64:(h + 1) * 64],
                          xo_ap(ch, qt * 128, 128)[p0:p0 + 64, :], cfgL1))
            ntile += 1
    attn_drive(tiles)
    P.barrier()
    for oc in range(NCH):
        _oc4 = oc
        wb0 = (oc % 2) * 1024
        cw = load_wslice(lambda oc=oc: h_w["o1"][:, oc * 128:(oc + 1) * 128].rearrange(
            "(c p) m -> p c m", p=128), oc % 2, 1024, wb0=wb0)
        for tt in range(4):
            ps = (pA, pB, pC, pD)[(_oc4 * 4 + tt) % 4]
            for dk in range(NCH):
                t = TT(lambda t_, dk=dk, tt=tt, ps=ps, wb0=wb0: t_.matmul(
                    out=ps[:, 0:512], lhsT=WBF[:, wb0 + dk * 128: wb0 + (dk + 1) * 128],
                    rhs=xo_ap(dk, tt * 512, 512),
                    start=(dk == 0), stop=(dk == NCH - 1)),
                    deps=[cw, pfree[id(ps)]] if dk == 0 else [])
            av = VV(lambda v, oc=oc, tt=tt, ps=ps: v.tensor_tensor(
                out=AR1[:, oc * 3072 + tt * 512: oc * 3072 + (tt + 1) * 512],
                in0=AR1[:, oc * 3072 + tt * 512: oc * 3072 + (tt + 1) * 512],
                in1=ps[:, 0:512], op=ALU.add), deps=[t])
            pfree[id(ps)] = av
        wbf_last[oc % 2] = VSNAP()
        wbf_tlast[oc % 2] = TSNAP()
    P.barrier()
    if dbg == "x3":
        for dc in range(NCH):
            DG(lambda g, dc=dc: g.dma_start(
                out=h_dbg[dc * 128:(dc + 1) * 128, :],
                in_=AR1[:, dc * 3072:(dc + 1) * 3072]))

    stok = ln_stats(lambda dc: (AR1[:, dc * 3072: dc * 3072 + 2048], None), NOWN,
                    AR2[:, 8192:10240], AR2[:, 10240:12288])
    P.barrier()
    xln3 = lambda dc, c0, w: AR2B[:, dc * 2048 + c0: dc * 2048 + c0 + w]
    for tt in range(4):
        c0 = tt * 512
        btok = bcast_stats(c0, 512, stok)
        for dc in range(NCH):
            VV(lambda v, dc=dc, c0=c0: v.tensor_tensor(
                out=QS3[:, 0:512], in0=AR1[:, dc * 3072 + c0: dc * 3072 + c0 + 512],
                in1=QS1[:, 0:512], op=ALU.add), deps=btok)
            VV(lambda v, dc=dc, c0=c0: v.tensor_tensor(
                out=xln3(dc, c0, 512), in0=QS3[:, 0:512], in1=QS2[:, 0:512],
                op=ALU.mult))
    P.barrier()
    dwg = DY(lambda y: y.dma_start(
        out=WS[0][:, 0:64], in_=h_wg[:, :].rearrange("(c p) m -> p c m", p=128)),
        deps=[ws_last[0]])
    cwg = SS(lambda s: s.copy(out=WBF[:, 0:64], in_=WS[0][:, 0:64]),
             deps=[dwg, wbf_last[0], wbf_tlast[0]])
    ws_last[0] = cwg
    for tt in range(4):
        for dc in range(NCH):
            t = TT(lambda t_, dc=dc, tt=tt: t_.matmul(
                out=pC[0:8, 0:512], lhsT=WBF[:, dc * 8:(dc + 1) * 8],
                rhs=xln3(dc, tt * 512, 512),
                start=(dc == 0), stop=(dc == NCH - 1)),
                deps=[cwg, pfree[id(pC)]] if dc == 0 else [])
        ev = VV(lambda v, tt=tt: v.tensor_scalar(
            out=AR3[32:40, tt * 512:(tt + 1) * 512], in0=pC[0:8, 0:512],
            scalar1=BG[0:8, 0:1], scalar2=None, op0=ALU.add), deps=[t])
        pfree[id(pC)] = ev
    lg_done = SSNAP()
    for ts in range(16):
        t = TT(lambda t_, ts=ts: t_.transpose(
            out=pG[:, 0:8], in_=AR3[32:40, ts * 128:(ts + 1) * 128],
            identity=IDF[32:40, 32:40]), deps=[lg_done, pfree[id(pG)]])
        ev = SS(lambda s, ts=ts: s.copy(out=QS3[:, 128 + ts * 8: 128 + (ts + 1) * 8],
                                        in_=pG[:, 0:8]), deps=[t])
        pfree[id(pG)] = ev
        r1 = VV(lambda v, ts=ts: v.tensor_reduce(
            out=RMAX[:, 0:1], in_=QS3[:, 128 + ts * 8: 128 + (ts + 1) * 8], axis=AXX,
            op=ALU.max, negate=True), deps=[ev, SSNAP()])
        r2 = SS(lambda s, ts=ts: s.activation(
            out=QS3[:, ts * 8:(ts + 1) * 8], in_=QS3[:, 128 + ts * 8: 128 + (ts + 1) * 8],
            func=ACT.Exp, bias=RMAX[:, 0:1], accum_out=RSUM[:, 0:1]), deps=[r1])
        VV(lambda v, ts=ts: v.reciprocal(out=QS3[:, 272 + ts: 273 + ts], in_=RSUM[:, 0:1]),
           deps=[r2])
    sgate = SSNAP()
    VV(lambda v: v.tensor_scalar(out=QS3[:, 0:128], in0=QS3[:, 0:128], scalar1=1.0,
                                 scalar2=None, op0=ALU.is_ge), deps=[sgate])
    VV(lambda v: v.tensor_copy(out=QS3[:, 256:272],
                               in_=bass.AP(QS3, 0, [[1024, 128], [8, 16]])))
    for e in range(1, 8):
        ge = bass.AP(QS3, e, [[1024, 128], [8, 16]])
        VV(lambda v: v.tensor_scalar(out=QS1[:, 0:16], in0=QS3[:, 256:272],
                                     scalar1=-1.0, scalar2=1.0,
                                     op0=ALU.mult, op1=ALU.add))
        VV(lambda v, ge=ge: v.tensor_tensor(out=ge, in0=ge, in1=QS1[:, 0:16],
                                            op=ALU.mult))
        VV(lambda v, ge=ge: v.tensor_tensor(out=QS3[:, 256:272], in0=QS3[:, 256:272],
                                            in1=ge, op=ALU.add))
    for e in range(8):
        ge = bass.AP(QS3, e, [[1024, 128], [8, 16]])
        VV(lambda v, ge=ge: v.tensor_tensor(out=ge, in0=ge, in1=QS3[:, 272:288],
                                            op=ALU.mult))
    gt_done = VSNAP()
    P.barrier()

    def acc_ap(ts, c0, w):
        if ts < 6:
            return AR2[:, 8192 + ts * 1024 + c0: 8192 + ts * 1024 + c0 + w]
        if ts < 14:
            base = (ts - 6) * 3072 + 2048
            return AR1[:, base + c0: base + c0 + w]
        return AR3[:, (ts - 14) * 1024 + c0: (ts - 14) * 1024 + c0 + w]

    db2 = DY(lambda y: y.dma_start(out=WS[1][0:8, 0:1024], in_=h_b2[:, :]),
             deps=[ws_last[1]])
    qs1g_last = None
    for ts in range(16):
        tg = TT(lambda t_, ts=ts: t_.transpose(
            out=pG[0:8, 0:128], in_=QS3[:, ts * 8:(ts + 1) * 8], identity=IDF[:, :]),
            deps=[gt_done, pfree[id(pG)]])
        eg = SS(lambda s: s.copy(out=QS1[0:8, 0:128], in_=pG[0:8, 0:128]),
                deps=[tg, qs1g_last, VSNAP()])
        pfree[id(pG)] = eg
        for oh in range(2):
            ps = pC if oh == 0 else pD
            psx = pE if oh == 0 else pF
            mb = TT(lambda t_, oh=oh, ps=ps: t_.matmul(
                out=ps[:, 0:512], lhsT=QS1[0:8, 0:128],
                rhs=WS[1][0:8, oh * 512:(oh + 1) * 512], start=True, stop=True),
                deps=[eg, db2, pfree[id(ps)]])
            qs1g_last = mb
            trs = []
            for k in range(4):
                dc = oh * 4 + k
                t = TT(lambda t_, dc=dc, ts=ts, k=k, psx=psx: t_.transpose(
                    out=psx[:, k * 128:(k + 1) * 128],
                    in_=AR1[:, dc * 3072 + ts * 128: dc * 3072 + (ts + 1) * 128],
                    identity=IDF[:, :]),
                    deps=[pfree[id(psx)]] if k == 0 else [])
                trs.append(t)
            eb2 = SS(lambda s, ps=ps: s.copy(out=QS3[:, 512:1024], in_=ps[:, 0:512]),
                     deps=[mb, qs1g_last, VSNAP()])
            av = VV(lambda v, ts=ts, oh=oh, psx=psx: v.tensor_tensor(
                out=acc_ap(ts, oh * 512, 512), in0=psx[:, 0:512], in1=QS3[:, 512:1024],
                op=ALU.add), deps=trs + [eb2])
            pfree[id(ps)] = av
            pfree[id(psx)] = av
    P.barrier()

    def w1bf_ap(dc):
        r = dc // 4
        return AR1B[:, r * 6144 + (dc % 4) * 1024: r * 6144 + (dc % 4) * 1024 + 1024]

    def w2bf_ap(ms):
        r = 2 + ms // 4
        return AR1B[:, r * 6144 + (ms % 4) * 1024: r * 6144 + (ms % 4) * 1024 + 1024]

    def h_ap(ms, tq):
        idx = ms * 4 + tq
        r = 4 + idx // 8
        return AR1B[:, r * 6144 + (idx % 8) * 512: r * 6144 + (idx % 8) * 512 + 512]

    w1_last = None
    w2_last = None
    for e in range(8):
        for mc in range(4):
            c1toks = []
            for i in range(4):
                par = i % 2
                d = DY(lambda y, e=e, mc=mc, i=i, par=par: y.dma_start(
                    out=WS[par][:, 0:2048],
                    in_=h_w1[e, 256 * i:256 * (i + 1),
                             mc * 1024:(mc + 1) * 1024].rearrange(
                        "(c p) m -> p c m", p=128)),
                    deps=[ws_last[par], w1_last])
                dst0 = (2 * i // 4) * 6144 + ((2 * i) % 4) * 1024
                c = SS(lambda s, par=par, dst0=dst0: s.copy(
                    out=AR1B[:, dst0:dst0 + 2048], in_=WS[par][:, 0:2048]), deps=[d])
                ws_last[par] = c
                c1toks.append(c)
            c2toks = []
            for i in range(4):
                par = i % 2
                d = DY(lambda y, e=e, mc=mc, i=i, par=par: y.dma_start(
                    out=WS[par][:, 0:2048],
                    in_=h_w2[e, mc * 1024 + 256 * i: mc * 1024 + 256 * (i + 1),
                             :].rearrange("(c p) m -> p c m", p=128)),
                    deps=[ws_last[par], w2_last])
                dst0 = (2 + (2 * i) // 4) * 6144 + ((2 * i) % 4) * 1024
                c = SS(lambda s, par=par, dst0=dst0: s.copy(
                    out=AR1B[:, dst0:dst0 + 2048], in_=WS[par][:, 0:2048]), deps=[d])
                ws_last[par] = c
                c2toks.append(c)
            h_last = None
            for ms in range(8):
                for tq in range(4):
                    ps = (pA, pB, pF)[(ms * 4 + tq) % 3]
                    dep0 = c1toks + [pfree[id(ps)], w2_last]
                    for dc in range(NCH):
                        t = TT(lambda t_, dc=dc, ms=ms, tq=tq, ps=ps: t_.matmul(
                            out=ps[:, 0:512],
                            lhsT=w1bf_ap(dc)[:, ms * 128:(ms + 1) * 128],
                            rhs=xln3(dc, tq * 512, 512),
                            start=(dc == 0), stop=(dc == NCH - 1)),
                            deps=dep0 if dc == 0 else [])
                    ev = SS(lambda s, e=e, mc=mc, ms=ms, tq=tq, ps=ps: s.activation(
                        out=h_ap(ms, tq), in_=ps[:, 0:512], func=ACT.Relu,
                        bias=B1[:, e * 32 + mc * 8 + ms: e * 32 + mc * 8 + ms + 1]),
                        deps=[t, w2_last] if (ms == 0 and tq == 0) else [t])
                    pfree[id(ps)] = ev
                    h_last = ev
            w1_last = h_last
            psl = [pC, pD, pE]
            for ts in range(16):
                for oh in range(2):
                    ps = psl[(ts * 2 + oh) % 3]
                    dep0 = c2toks + [pfree[id(ps)], h_last]
                    for ms in range(8):
                        t = TT(lambda t_, ms=ms, ts=ts, oh=oh, ps=ps: t_.matmul(
                            out=ps[:, 0:512],
                            lhsT=h_ap(ms, ts // 4)[:, (ts % 4) * 128:
                                                   (ts % 4 + 1) * 128],
                            rhs=w2bf_ap(ms)[:, oh * 512:(oh + 1) * 512],
                            start=(ms == 0), stop=(ms == 7)),
                            deps=dep0 if ms == 0 else [])
                    w2_last = t
                    av = VV(lambda v, e=e, ts=ts, oh=oh, ps=ps: v.scalar_tensor_tensor(
                        out=acc_ap(ts, oh * 512, 512), in0=ps[:, 0:512],
                        scalar=QS3[:, ts * 8 + e: ts * 8 + e + 1],
                        in1=acc_ap(ts, oh * 512, 512),
                        op0=ALU.mult, op1=ALU.add), deps=[t])
                    pfree[id(ps)] = av
    P.barrier()

    for ts in range(16):
        DG(lambda g, ts=ts: g.dma_start(
            out=h_out[ts * 128:(ts + 1) * 128, :], in_=acc_ap(ts, 0, 1024)))
    P.emit("gpsimd", None, deps=[("gpsimd", P.cnt["gpsimd"])], inc=0)

    with nc.Block() as block:
        P.run(block, sems)
    es.close()
    return nc


_NC_CACHE = {}


def _get_nc(dbg=None):
    if dbg not in _NC_CACHE:
        _NC_CACHE[dbg] = build_nc(dbg)
    return _NC_CACHE[dbg]


def make_inputs(inputs):
    f = np.float32
    import ml_dtypes
    x = np.asarray(inputs["L0_attn_x"], dtype=f)
    import ml_dtypes as _mld
    idx = np.arange(128)
    m_su = np.where(idx[None, :] > idx[:, None], 0.0, NEG).astype(f)
    m_li = np.where(idx[None, :] <= idx[:, None], 0.0, NEG).astype(f)
    mask_own = np.concatenate([m_su, np.zeros((128, 384), f), m_li], axis=1)
    mq = np.concatenate([np.zeros((128, 512), f), mask_own], axis=1)
    common = {
        "w_q0": np.ascontiguousarray(inputs["L0_attn_Wq"].T.astype(f)),
        "w_k0": np.ascontiguousarray(inputs["L0_attn_Wk"].T.astype(f)),
        "w_v0": np.ascontiguousarray(inputs["L0_attn_Wv"].T.astype(f)),
        "w_o0": np.ascontiguousarray(inputs["L0_attn_Wo"].T.astype(f)),
        "w_q1": np.ascontiguousarray(inputs["L1_attn_Wq"].T.astype(f)),
        "w_k1": np.ascontiguousarray(inputs["L1_attn_Wk"].T.astype(f)),
        "w_v1": np.ascontiguousarray(inputs["L1_attn_Wv"].T.astype(f)),
        "w_o1": np.ascontiguousarray(inputs["L1_attn_Wo"].T.astype(f)),
        "w_f0": np.ascontiguousarray(inputs["L0_ffn_weight"].T.astype(f)),
        "w_moe1": np.ascontiguousarray(inputs["L1_ffn_W1"].astype(f)),
        "w_moe2": np.ascontiguousarray(inputs["L1_ffn_W2"].astype(f)),
        "w_gate": np.ascontiguousarray(inputs["L1_ffn_w_gate"].astype(f)),
        "b_gate": np.ascontiguousarray(inputs["L1_ffn_b_gate"].astype(f)
                                       .reshape(8, 1)),
        "b_moe1": np.ascontiguousarray(
            inputs["L1_ffn_b1"].astype(f).reshape(8, 32, 128).transpose(2, 0, 1)
            .reshape(128, 256)),
        "b_moe2": np.ascontiguousarray(inputs["L1_ffn_b2"].astype(f)),
        "b_ffn0": np.ascontiguousarray(inputs["L0_ffn_bias"].astype(f)
                                       .reshape(8, 128).T),
        "mq_bf": mq.astype(_mld.bfloat16),
        "ident_f": np.eye(128, dtype=f),
        "ident_b": np.eye(128, dtype=f).astype(ml_dtypes.bfloat16),
        "ones_c": np.ones((128, 1), f),
        "ones_r": np.ones((33, 128), f),
    }
    in_maps = []
    for c in range(8):
        b, hh = c // 2, c % 2
        xb = x[b]
        own = xb[hh * 2048:(hh + 1) * 2048]
        halo = xb[hh * 2048 - 512: hh * 2048] if hh else np.zeros((512, D), f)
        ext = xb[0:1024]
        m = dict(common)
        m["xin"] = np.ascontiguousarray(np.concatenate([halo, own, ext], axis=0).T)
        bcm = np.empty((128, 2048), f)
        for qt in range(4):
            v = mask_own[:, 0:512].copy()
            if hh == 0:
                v[:, 0:512 - 128 * qt] += NEG
            bcm[:, qt * 512:(qt + 1) * 512] = v
        m["bcm_bf"] = np.ascontiguousarray(bcm.astype(_mld.bfloat16))
        in_maps.append(m)
    return in_maps


def kernel(**inputs):
    nc = _get_nc()
    in_maps = make_inputs(inputs)
    res = run_bass_kernel_spmd(nc, in_maps, core_ids=list(range(8)))
    out = np.empty((4, 4096, D), dtype=np.float32)
    for c in range(8):
        b, hh = c // 2, c % 2
        out[b, hh * 2048:(hh + 1) * 2048] = res.results[c]["out"]
    return out


# version 17
# speedup vs baseline: 1.0874x; 1.0874x over previous
import numpy as np
from contextlib import ExitStack
import concourse.bass as bass
import concourse.mybir as mybir
from concourse.bass_utils import run_bass_kernel_spmd

F32 = mybir.dt.float32
BF16 = mybir.dt.bfloat16
F16 = mybir.dt.float16
F8 = mybir.dt.float8e4
I8 = mybir.dt.int8
I16 = mybir.dt.int16
ACT = mybir.ActivationFunctionType
ALU = mybir.AluOpType
AXX = mybir.AxisListType.X

D = 1024
NCH = 8
NOWN = 2048
NTOT = 3584
NOE = 3072
NEXT_ = 1024
NEG = -1.0e30
EPS = 1e-5


class Prog:
    ENGINES = ("tensor", "vector", "scalar", "sync", "gpsimd")
    VSEMS = ("cst", "xs0", "xs1", "xr0", "xr1", "wy0", "wy1")
    DMAS = ("sync", "gpsimd", "cst", "xs0", "xs1", "xr0", "xr1", "wy0", "wy1")

    def __init__(self):
        self.ops = {e: [] for e in self.ENGINES}
        allsems = self.ENGINES + self.VSEMS
        self.cnt = {e: 0 for e in allsems}
        self.hwm = {e: {x: 0 for x in allsems} for e in self.ENGINES}

    def emit(self, eng, fn, deps=(), selfdep=False, inc=1, sem=None):
        sem = sem or eng
        waits = []
        dd = [d for d in deps if d is not None]
        if selfdep and self.cnt[eng] > 0:
            dd.append((eng, self.cnt[eng]))
        for peng, pval in dd:
            if pval <= 0:
                continue
            if self.hwm[eng][peng] < pval:
                self.hwm[eng][peng] = pval
                waits.append((peng, pval))
        self.ops[eng].append((waits, fn, inc, sem))
        if fn is not None and inc:
            self.cnt[sem] += inc
            return (sem, self.cnt[sem])
        return None

    def barrier(self):
        snap = dict(self.cnt)
        allsems = self.ENGINES + self.VSEMS
        for e in self.ENGINES:
            self.emit(e, None, deps=[(p, snap[p]) for p in allsems if p != e],
                      inc=0)

    def run(self, block, sems):
        mult = {e: (16 if e in self.DMAS else 1) for e in self.ENGINES + self.VSEMS}

        def mk(eng_name):
            def body(eng):
                for waits, fn, inc, sem in self.ops[eng_name]:
                    for peng, pval in waits:
                        eng.wait_ge(sems[peng], pval * mult[peng])
                    if fn is not None:
                        inst = fn(eng)
                        if inc:
                            inst.then_inc(sems[sem], inc * mult[sem])
            return body
        block.tensor(mk("tensor"))
        block.vector(mk("vector"))
        block.scalar(mk("scalar"))
        block.sync(mk("sync"))
        block.gpsimd(mk("gpsimd"))


def build_nc(dbg=None):
    nc = bass.Bass("TRN2", target_bir_lowering=False)
    P = Prog()

    h_xin = nc.dram_tensor("xin", [D, NTOT], F32, kind="ExternalInput").ap()
    h_w = {}
    for nm in ("q0", "k0", "v0", "o0", "q1", "k1", "v1", "o1", "f0"):
        h_w[nm] = nc.dram_tensor("w_" + nm, [D, D], F32, kind="ExternalInput").ap()
    h_w1 = nc.dram_tensor("w_moe1", [8, D, 4096], F32, kind="ExternalInput").ap()
    h_w2 = nc.dram_tensor("w_moe2", [8, 4096, D], F32, kind="ExternalInput").ap()
    h_wg = nc.dram_tensor("w_gate", [D, 8], F32, kind="ExternalInput").ap()
    h_bg = nc.dram_tensor("b_gate", [8, 1], F32, kind="ExternalInput").ap()
    h_b1 = nc.dram_tensor("b_moe1", [128, 256], F32, kind="ExternalInput").ap()
    h_b2 = nc.dram_tensor("b_moe2", [8, D], F32, kind="ExternalInput").ap()
    h_b0 = nc.dram_tensor("b_ffn0", [128, 8], F32, kind="ExternalInput").ap()
    h_mq = nc.dram_tensor("mq_bf", [128, 1152], BF16, kind="ExternalInput").ap()
    h_bcm = nc.dram_tensor("bcm_bf", [128, 2048], BF16, kind="ExternalInput").ap()
    h_idf = nc.dram_tensor("ident_f", [128, 128], F32, kind="ExternalInput").ap()
    h_idb = nc.dram_tensor("ident_b", [128, 128], BF16, kind="ExternalInput").ap()
    h_onc = nc.dram_tensor("ones_c", [128, 1], F32, kind="ExternalInput").ap()
    h_onr = nc.dram_tensor("ones_r", [33, 128], F32, kind="ExternalInput").ap()
    h_out = nc.dram_tensor("out", [NOWN, D], F32, kind="ExternalOutput").ap()
    h_dbg = None
    if dbg is not None:
        h_dbg = nc.dram_tensor("dbgout", [D, NOE], F32, kind="ExternalOutput").ap()

    es = ExitStack()
    sbt = lambda nm, shp, dt: es.enter_context(nc.sbuf_tensor(nm, shp, dt))
    pst = lambda nm, shp: es.enter_context(nc.psum_tensor(nm, shp, F32))

    AR1 = sbt("AR1", [128, 24576], F32)
    AR2 = sbt("AR2", [128, 14336], F32)
    AR3 = sbt("AR3", [128, 3584], F32)
    WS = [sbt("ws0", [128, 2048], F32), sbt("ws1", [128, 2048], F32)]
    WBF = sbt("wbf", [128, 2048], BF16)
    MQ = sbt("mqbf", [128, 1152], BF16)
    IDF = sbt("identf", [128, 128], F32)
    IDB = sbt("identb", [128, 128], BF16)
    ONC = sbt("onesc", [128, 1], F32)
    ONR = sbt("onesr", [33, 128], F32)
    B1 = sbt("b1sb", [128, 256], F32)
    B0 = sbt("b0sb", [128, 8], F32)
    BG = sbt("bgsb", [8, 1], F32)
    QS1 = sbt("qs1", [128, 512], F32)
    QS2 = sbt("qs2", [128, 512], F32)
    QS3 = sbt("qs3", [128, 1024], F32)
    GT = QS3
    PNB = sbt("pnb", [128, 1024], BF16)
    PTB = sbt("ptb", [128, 1024], BF16)
    RMAX = sbt("rmax", [128, 8], F32)
    RSUM = sbt("rsum", [128, 4], F32)
    RINV = sbt("rinv", [128, 2], F32)
    Q8 = sbt("q8t", [128, 1024], F8)
    I8T = sbt("i8t", [128, 1024], I8)
    F16T = sbt("f16t", [128, 512], F16)
    I16T = sbt("i16t", [128, 512], I16)

    AR1B = AR1.bitcast(BF16)
    AR2B = AR2.bitcast(BF16)
    AR3B = AR3.bitcast(BF16)

    pA = pst("pA", [128, 512])
    pB = pst("pB", [128, 512])
    pC = pst("pC", [128, 512])
    pD = pst("pD", [128, 512])
    pE = pst("pE", [128, 512])
    pF = pst("pF", [128, 512])
    pG = pst("pG", [128, 128])
    pH = pst("pH", [128, 128])

    pFB = pF.bitcast(BF16)
    pDB = pD.bitcast(BF16)
    pEB = pE.bitcast(BF16)
    pGB = pG.bitcast(BF16)
    sems = {e: es.enter_context(nc.semaphore("sem_" + e))
            for e in Prog.ENGINES + Prog.VSEMS}

    TT = lambda fn, deps=(): P.emit("tensor", fn, deps=deps)
    VV = lambda fn, deps=(): P.emit("vector", fn, deps=deps, selfdep=True)
    SS = lambda fn, deps=(): P.emit("scalar", fn, deps=deps, selfdep=True)
    DY = lambda fn, deps=(): P.emit("sync", fn, deps=deps)
    DG = lambda fn, deps=(): P.emit("gpsimd", fn, deps=deps)
    TSNAP = lambda: ("tensor", P.cnt["tensor"])
    VSNAP = lambda: ("vector", P.cnt["vector"])
    SSNAP = lambda: ("scalar", P.cnt["scalar"])

    pfree = {id(p): None for p in (pA, pB, pC, pD, pE, pF, pG, pH)}
    ws_last = [None, None]
    wbf_last = [None, None]
    wbf_tlast = [None, None]

    P.emit("gpsimd", lambda g: g.dma_start(out=MQ[:], in_=h_mq[:, :]), sem="cst")
    P.emit("gpsimd", lambda g: g.dma_start(out=AR2.bitcast(BF16)[:, 19456:21504], in_=h_bcm[:, :]), sem="cst")
    P.emit("gpsimd", lambda g: g.dma_start(out=IDF[:], in_=h_idf[:, :]), sem="cst")
    P.emit("gpsimd", lambda g: g.dma_start(out=IDB[:], in_=h_idb[:, :]), sem="cst")
    P.emit("gpsimd", lambda g: g.dma_start(out=ONC[:], in_=h_onc[:, :]), sem="cst")
    P.emit("gpsimd", lambda g: g.dma_start(out=ONR[:], in_=h_onr[:, :]), sem="cst")
    P.emit("gpsimd", lambda g: g.dma_start(out=B1[:], in_=h_b1[:, :]), sem="cst")
    P.emit("gpsimd", lambda g: g.dma_start(out=B0[:], in_=h_b0[:, :]), sem="cst")
    P.emit("gpsimd", lambda g: g.dma_start(out=BG[:], in_=h_bg[:, :]), sem="cst")
    P.barrier()

    src_cons = [None, None]

    def ln_stats(src, n, accx, accsq, rows=None):
        if rows is None:
            rows = AR3
        sq = (QS1[:, 0:512], QS2[:, 0:512])
        sq_cons = [None, None]
        for dc in range(NCH):
            ap, dtok = src(dc)
            if dc == 0:
                VV(lambda v, ap=ap: v.tensor_copy(out=accx[:, 0:n], in_=ap),
                   deps=[dtok])
            else:
                VV(lambda v, ap=ap: v.tensor_tensor(out=accx[:, 0:n], in0=accx[:, 0:n],
                                                    in1=ap, op=ALU.add), deps=[dtok])
            for tt in range((n + 511) // 512):
                c0 = tt * 512
                w = min(512, n - c0)
                qp = (dc + tt) % 2
                ts = SS(lambda s, ap=ap, c0=c0, w=w, qp=qp: s.activation(
                    out=sq[qp][:, 0:w], in_=ap[:, c0:c0 + w], func=ACT.Square),
                    deps=[sq_cons[qp], dtok])
                if dc == 0:
                    av = VV(lambda v, c0=c0, w=w, qp=qp: v.tensor_copy(
                        out=accsq[:, c0:c0 + w], in_=sq[qp][:, 0:w]), deps=[ts])
                else:
                    av = VV(lambda v, c0=c0, w=w, qp=qp: v.tensor_tensor(
                        out=accsq[:, c0:c0 + w], in0=accsq[:, c0:c0 + w],
                        in1=sq[qp][:, 0:w], op=ALU.add), deps=[ts])
                sq_cons[qp] = av
            src_cons[dc % 2] = av
        vacc = VSNAP()
        evs = []
        for tt in range((n + 511) // 512):
            c0 = tt * 512
            w = min(512, n - c0)
            m1 = TT(lambda t, c0=c0, w=w: t.matmul(
                out=pE[0:1, 0:w], lhsT=ONC[:, :], rhs=accx[:, c0:c0 + w],
                start=True, stop=True), deps=[vacc, pfree[id(pE)]])
            e1 = SS(lambda s, c0=c0, w=w: s.copy(out=rows[0:1, c0:c0 + w],
                                                 in_=pE[0:1, 0:w]), deps=[m1])
            pfree[id(pE)] = e1
            m2 = TT(lambda t, c0=c0, w=w: t.matmul(
                out=pF[0:1, 0:w], lhsT=ONC[:, :], rhs=accsq[:, c0:c0 + w],
                start=True, stop=True), deps=[vacc, pfree[id(pF)]])
            e2 = SS(lambda s, c0=c0, w=w: s.copy(out=rows[32:33, c0:c0 + w],
                                                 in_=pF[0:1, 0:w]), deps=[m2])
            pfree[id(pF)] = e2
            evs += [e1, e2]
        return evs + [SSNAP()]

    def bcast_stats(c0, w, stok, rows=None, dst1=None, dst2=None):
        if rows is None:
            rows = AR3
        if dst1 is None:
            dst1 = QS1[:, 0:w]
        if dst2 is None:
            dst2 = QS2[:, 0:w]
        war = [VSNAP(), TSNAP()]
        m1 = TT(lambda t, c0=c0, w=w: t.matmul(
            out=pE[:, 0:w], lhsT=ONR[0:1, :], rhs=rows[0:1, c0:c0 + w],
            start=True, stop=True), deps=list(stok) + [pfree[id(pE)]])
        e1 = SS(lambda s, w=w, dst1=dst1: s.copy(out=dst1, in_=pE[:, 0:w]),
                deps=[m1] + war)
        pfree[id(pE)] = e1
        m2 = TT(lambda t, c0=c0, w=w: t.matmul(
            out=pF[:, 0:w], lhsT=ONR[32:33, :], rhs=rows[32:33, c0:c0 + w],
            start=True, stop=True), deps=list(stok) + [pfree[id(pF)]])
        e2 = SS(lambda s, w=w, dst2=dst2: s.copy(out=dst2, in_=pF[:, 0:w]),
                deps=[m2])
        pfree[id(pF)] = e2
        tmp = QS3[:, 512:512 + w]
        VV(lambda v, dst1=dst1: v.tensor_scalar(out=dst1, in0=dst1,
                                                scalar1=-1.0 / D, scalar2=None,
                                                op0=ALU.mult), deps=[e1])
        VV(lambda v, dst2=dst2: v.tensor_scalar(out=dst2, in0=dst2,
                                                scalar1=1.0 / D, scalar2=EPS,
                                                op0=ALU.mult, op1=ALU.add),
           deps=[e2])
        VV(lambda v, dst1=dst1, tmp=tmp: v.tensor_tensor(out=tmp, in0=dst1,
                                                         in1=dst1, op=ALU.mult))
        tv = VV(lambda v, dst2=dst2, tmp=tmp: v.tensor_tensor(
            out=dst2, in0=dst2, in1=tmp, op=ALU.subtract))
        tq = SS(lambda s, dst2=dst2: s.activation(out=dst2, in_=dst2,
                                                  func=ACT.Sqrt), deps=[tv])
        tr = VV(lambda v, dst2=dst2: v.reciprocal(out=dst2, in_=dst2), deps=[tq])
        return [tr, e1]

    def load_wslice(hfn, par, nf32, prior=(), wb0=0):
        d = P.emit("sync", lambda y, hfn=hfn, par=par, nf32=nf32: y.dma_start(
            out=WS[par][:, 0:nf32], in_=hfn()), deps=[ws_last[par]] + list(prior),
            sem="wy" + str(par))
        if nf32 <= 1024:
            wdeps = [wbf_last[wb0 // 1024], wbf_tlast[wb0 // 1024]]
        else:
            wdeps = [wbf_last[0], wbf_last[1], wbf_tlast[0], wbf_tlast[1]]
        c = SS(lambda s, par=par, nf32=nf32, wb0=wb0: s.copy(
            out=WBF[:, wb0:wb0 + nf32], in_=WS[par][:, 0:nf32]),
            deps=[d] + wdeps)
        ws_last[par] = c
        return c

    att_st = {}

    def attn_reset():
        att_st.clear()
        for p in (0, 1):
            att_st[p] = dict(exp=None, nrm=None, tr=None, pte=None, avm=None,
                             oev=None, rmx=None, rcp=None)

    def attn_s1(n, qap, kap_fn, nt, mask_a, mask_b, v_fn, o_ap, cfg):
        p = n % 2
        st = att_st[p]
        wk = nt * 128
        w0 = min(wk, 512)
        wb = wk - w0
        sA = cfg["sA"][p]
        sB = cfg["sB"][p]
        m1 = TT(lambda t: t.matmul(out=sA[:, 0:w0], lhsT=qap, rhs=kap_fn(0, w0),
                                   start=True, stop=(mask_a is None)),
                deps=[st["exp"], st["rmx"]])
        if mask_a is not None:
            m1 = TT(lambda t: t.matmul(out=sA[:, 0:w0], lhsT=IDB[:, :], rhs=mask_a,
                                       start=False, stop=True))
        m2 = None
        if wb > 0:
            m2 = TT(lambda t: t.matmul(out=sB[:, 0:wb], lhsT=qap, rhs=kap_fn(w0, wb),
                                       start=True, stop=(mask_b is None)))
            if mask_b is not None:
                m2 = TT(lambda t: t.matmul(out=sB[:, 0:wb], lhsT=IDB[:, :],
                                           rhs=mask_b, start=False, stop=True))
        r1 = VV(lambda v: v.tensor_reduce(out=RMAX[:, 4 * p:4 * p + 1],
                                          in_=sA[:, 0:w0], axis=AXX,
                                          op=ALU.max, negate=True),
                deps=[m1, st["exp"]])
        rc = r1
        rcol = 4 * p
        if wb > 0:
            r2 = VV(lambda v: v.tensor_reduce(out=RMAX[:, 4 * p + 1:4 * p + 2],
                                              in_=sB[:, 0:wb], axis=AXX,
                                              op=ALU.max, negate=True), deps=[m2])
            rc = VV(lambda v: v.tensor_tensor(out=RMAX[:, 4 * p + 2:4 * p + 3],
                                              in0=RMAX[:, 4 * p:4 * p + 1],
                                              in1=RMAX[:, 4 * p + 1:4 * p + 2],
                                              op=ALU.min))
            rcol = 4 * p + 2
        st["rmx"] = rc
        return dict(n=n, nt=nt, wk=wk, w0=w0, wb=wb, vfn=v_fn, oap=o_ap, cfg=cfg,
                    rc=rc, rcol=rcol, m1=m1, m2=m2)

    def attn_s2a(cx):
        p = cx["n"] % 2
        st = att_st[p]
        cfg = cx["cfg"]
        e1 = SS(lambda s: s.activation(out=cfg["ea"][p][:, 0:cx["w0"]],
                                       in_=cfg["sA"][p][:, 0:cx["w0"]],
                                       func=ACT.Exp,
                                       bias=RMAX[:, cx["rcol"]:cx["rcol"] + 1],
                                       accum_out=RSUM[:, 2 * p:2 * p + 1]),
                deps=[cx["rc"], st["rcp"], st["nrm"]])
        elast = e1
        if cx["wb"] > 0:
            elast = SS(lambda s: s.activation(out=cfg["eb"][p][:, 0:cx["wb"]],
                                              in_=cfg["sB"][p][:, 0:cx["wb"]],
                                              func=ACT.Exp,
                                              bias=RMAX[:, cx["rcol"]:cx["rcol"] + 1],
                                              accum_out=RSUM[:, 2 * p + 1:2 * p + 2]))
        st["exp"] = elast
        cx["elast"] = elast

    def attn_s2b(cx):
        p = cx["n"] % 2
        st = att_st[p]
        if cx["wb"] > 0:
            sadd = VV(lambda v: v.tensor_tensor(out=RSUM[:, 2 * p:2 * p + 1],
                                                in0=RSUM[:, 2 * p:2 * p + 1],
                                                in1=RSUM[:, 2 * p + 1:2 * p + 2],
                                                op=ALU.add), deps=[cx["elast"]])
        else:
            sadd = cx["elast"]
        rcp = VV(lambda v: v.reciprocal(out=RINV[:, p:p + 1],
                                        in_=RSUM[:, 2 * p:2 * p + 1]), deps=[sadd])
        st["rcp"] = rcp
        cx["rcp"] = rcp

    def attn_s3(cx):
        p = cx["n"] % 2
        st = att_st[p]
        cfg = cx["cfg"]
        pnb = cfg["pnb"][p]
        trb = cfg["trb"][p]
        nt, wk, w0, wb = cx["nt"], cx["wk"], cx["w0"], cx["wb"]
        na = SS(lambda s: s.activation(out=pnb[:, 0:w0], in_=cfg["ea"][p][:, 0:w0],
                                       func=ACT.Copy, scale=RINV[:, p:p + 1]),
                deps=[cx["rcp"], st["tr"]])
        nrm = na
        if wb > 0:
            nrm = SS(lambda s: s.activation(out=pnb[:, w0:wk],
                                            in_=cfg["eb"][p][:, 0:wb],
                                            func=ACT.Copy, scale=RINV[:, p:p + 1]))
        st["nrm"] = nrm
        tts = []
        for d_ in range(nt):
            t = TT(lambda t_, d_=d_: t_.transpose(
                out=trb[:, d_ * 128:(d_ + 1) * 128],
                in_=pnb[:, d_ * 128:(d_ + 1) * 128], identity=IDB[:, :]),
                deps=[nrm, st["pte"]] if d_ == 0 else [nrm])
            tts.append(t)
        st["tr"] = tts[-1]
        pte = VV(lambda v: v.tensor_copy(out=cfg["ptb"][p][:, 0:wk],
                                         in_=trb[:, 0:wk]),
                 deps=tts + [st["avm"]])
        st["pte"] = pte
        cx["pte"] = pte

    def attn_s4(cx):
        p = cx["n"] % 2
        st = att_st[p]
        cfg = cx["cfg"]
        ptb = cfg["ptb"][p]
        avp = cfg["avs"][p]
        avm = None
        for d_ in range(cx["nt"]):
            avm = TT(lambda t_, d_=d_: t_.matmul(
                out=avp, lhsT=cx["vfn"](d_), rhs=ptb[:, d_ * 128:(d_ + 1) * 128],
                start=(d_ == 0), stop=(d_ == cx["nt"] - 1)),
                deps=[cx["pte"], st["oev"]] if d_ == 0 else [])
        st["avm"] = avm
        oev = VV(lambda v: v.tensor_copy(out=cx["oap"], in_=avp), deps=[avm])
        st["oev"] = oev

    def attn_drive(tiles):
        ctxs = []
        N = len(tiles)
        for i in range(N + 4):
            if i >= 4:
                attn_s4(ctxs[i - 4])
            if 3 <= i < N + 3:
                attn_s3(ctxs[i - 3])
            if 2 <= i < N + 2:
                attn_s2b(ctxs[i - 2])
            if 1 <= i < N + 1:
                attn_s2a(ctxs[i - 1])
            if i < N:
                ctxs.append(attn_s1(*tiles[i]))

    xln_ap = lambda dc, c0, w: AR1B[:, dc * 3584 + c0: dc * 3584 + c0 + w]
    xbuf = [AR1[:, 14336:17920], AR1[:, 17920:21504]]

    def srcA_final(dc):
        par = dc % 2
        tk = P.emit("gpsimd", lambda g, dc=dc, par=par: g.dma_start(
            out=xbuf[par][:, :], in_=h_xin[dc * 128:(dc + 1) * 128, :]),
            deps=[src_cons[par]], sem="xs" + str(par))
        return xbuf[par], tk

    stok = ln_stats(srcA_final, NTOT, AR2[:, 0:3584], AR2[:, 3584:7168])
    P.barrier()
    MB = AR2[:, 0:3584]
    RB = AR2[:, 3584:7168]
    for tt in range(7):
        bcast_stats(tt * 512, 512, stok, dst1=MB[:, tt * 512:(tt + 1) * 512],
                    dst2=RB[:, tt * 512:(tt + 1) * 512])
    P.barrier()
    for dc in range(NCH):
        ap, tk = srcA_final(dc)
        q0 = (dc % 2) * 512
        for tt in range(7):
            c0 = tt * 512
            VV(lambda v, ap=ap, c0=c0, q0=q0: v.tensor_tensor(
                out=QS3[:, q0:q0 + 512], in0=ap[:, c0:c0 + 512],
                in1=MB[:, c0:c0 + 512], op=ALU.add), deps=[tk])
            lastap = VV(lambda v, dc=dc, c0=c0, q0=q0: v.tensor_tensor(
                out=xln_ap(dc, c0, 512), in0=QS3[:, q0:q0 + 512],
                in1=RB[:, c0:c0 + 512], op=ALU.mult))
        src_cons[dc % 2] = lastap
    P.barrier()

    def o_ap(ch):
        if ch < 6:
            return AR1B[:, 28672 + ch * 3072: 28672 + (ch + 1) * 3072]
        return AR2B[:, 13312 + (ch - 6) * 3072: 13312 + (ch - 5) * 3072]

    v_ap = lambda kt: AR3B[:, kt * 256:(kt + 1) * 256]

    for g in range(4):
        if g > 0:
            P.barrier()
        cw = load_wslice(lambda g=g: h_w["q0"][:, g * 256:(g + 1) * 256].rearrange(
            "(c p) m -> p c m", p=128), g % 2, 2048)
        for cc in range(2):
            for tt in range(6):
                ps = (pA, pB, pC, pD)[(cc * 6 + tt) % 4]
                for dc in range(NCH):
                    t = TT(lambda t_, dc=dc, cc=cc, tt=tt, ps=ps: t_.matmul(
                        out=ps[:, 0:512],
                        lhsT=WBF[:, dc * 256 + cc * 128: dc * 256 + cc * 128 + 128],
                        rhs=xln_ap(dc, 512 + tt * 512, 512),
                        start=(dc == 0), stop=(dc == NCH - 1)),
                        deps=[cw, pfree[id(ps)]] if dc == 0 else [])
                ev = SS(lambda s, cc=cc, tt=tt, ps=ps: s.activation(
                    out=AR2B[:, cc * 3072 + tt * 512: cc * 3072 + (tt + 1) * 512],
                    in_=ps[:, 0:512], func=ACT.Copy, scale=0.125), deps=[t])
                pfree[id(ps)] = ev
        wbf_last[0] = wbf_last[1] = SSNAP()
        wbf_tlast[0] = wbf_tlast[1] = TSNAP()
        cw = load_wslice(lambda g=g: h_w["k0"][:, g * 256:(g + 1) * 256].rearrange(
            "(c p) m -> p c m", p=128), (g + 1) % 2, 2048)
        for cc in range(2):
            for tt in range(7):
                ps = (pA, pB, pC, pD)[(cc * 7 + tt) % 4]
                for dc in range(NCH):
                    t = TT(lambda t_, dc=dc, cc=cc, tt=tt, ps=ps: t_.matmul(
                        out=ps[:, 0:512],
                        lhsT=WBF[:, dc * 256 + cc * 128: dc * 256 + cc * 128 + 128],
                        rhs=xln_ap(dc, tt * 512, 512),
                        start=(dc == 0), stop=(dc == NCH - 1)),
                        deps=[cw, pfree[id(ps)]] if dc == 0 else [])
                ev = SS(lambda s, cc=cc, tt=tt, ps=ps: s.copy(
                    out=AR2B[:, 6144 + cc * 3584 + tt * 512:
                             6144 + cc * 3584 + (tt + 1) * 512],
                    in_=ps[:, 0:512]), deps=[t])
                pfree[id(ps)] = ev
        wbf_last[0] = wbf_last[1] = SSNAP()
        wbf_tlast[0] = wbf_tlast[1] = TSNAP()
        cw = load_wslice(lambda g=g: h_w["v0"][:, g * 256:(g + 1) * 256].rearrange(
            "(c p) m -> p c m", p=128), g % 2, 2048)
        for kt in range(28):
            ps = (pA, pB, pC, pD)[kt % 4]
            for dc in range(NCH):
                t = TT(lambda t_, dc=dc, kt=kt, ps=ps: t_.matmul(
                    out=ps[:, 0:256], lhsT=xln_ap(dc, kt * 128, 128),
                    rhs=WBF[:, dc * 256:(dc + 1) * 256],
                    start=(dc == 0), stop=(dc == NCH - 1)),
                    deps=[cw, pfree[id(ps)]] if dc == 0 else [])
            ev = SS(lambda s, kt=kt, ps=ps: s.copy(out=v_ap(kt), in_=ps[:, 0:256]),
                    deps=[t])
            pfree[id(ps)] = ev
        wbf_last[0] = wbf_last[1] = SSNAP()
        wbf_tlast[0] = wbf_tlast[1] = TSNAP()

        proj_done = [TSNAP(), SSNAP()]
        attn_reset()
        BCMB = AR2B[:, 19456:21504]
        cfgL0 = dict(sA=(pC, pE), sB=(pA, pB),
                     ea=(QS3[:, 0:512], QS1[:, 0:512]),
                     eb=(QS3[:, 512:1024], QS2[:, 0:512]),
                     pnb=(PNB[:, 0:640], AR2B[:, 21504:22144]),
                     ptb=(PTB[:, 0:640], AR2B[:, 22144:22784]),
                     trb=(pFB[:, 0:1024], pDB[:, 0:1024]),
                     avs=(pH[0:64, 0:128], pH[64:128, 0:128]))
        tiles = []
        ntile = 0
        for hl in range(4):
            cc = hl // 2
            p0 = 64 * (hl % 2)
            ch = 2 * g + cc
            qh = lambda c0, w, cc=cc, p0=p0: AR2B[p0:p0 + 64,
                                                  cc * 3072 + c0: cc * 3072 + c0 + w]
            kh = lambda c0, w, cc=cc, p0=p0: AR2B[p0:p0 + 64,
                                                  6144 + cc * 3584 + c0:
                                                  6144 + cc * 3584 + c0 + w]
            for qt in range(16):
                ma = BCMB[:, qt * 512:(qt + 1) * 512] if qt <= 3 else MQ[:, 512:1024]
                tiles.append((ntile, qh(qt * 128, 128),
                              lambda c0, w, qt=qt, kh=kh: kh(qt * 128 + c0, w),
                              5, ma, MQ[:, 1024:1152],
                              lambda d, qt=qt, hl=hl: v_ap(qt + d)[:, hl * 64:(hl + 1) * 64],
                              o_ap(ch)[p0:p0 + 64, qt * 128:(qt + 1) * 128], cfgL0))
                ntile += 1
            for qt in range(8):
                nt = min(qt, 4) + 1
                j0 = max(0, qt - 4)
                wk = nt * 128
                if nt == 5:
                    ma, mb = MQ[:, 512:1024], MQ[:, 1024:1152]
                else:
                    ma, mb = MQ[:, 1152 - wk:1152], None
                tiles.append((ntile, qh(2048 + qt * 128, 128),
                              lambda c0, w, j0=j0, kh=kh: kh(2560 + j0 * 128 + c0, w),
                              nt, ma, mb,
                              lambda d, j0=j0, hl=hl: v_ap(20 + j0 + d)[:, hl * 64:(hl + 1) * 64],
                              o_ap(ch)[p0:p0 + 64, 2048 + qt * 128: 2048 + (qt + 1) * 128],
                              cfgL0))
                ntile += 1
        for p_ in (0, 1):
            att_st[p_]["exp"] = proj_done[1]
            att_st[p_]["rmx"] = proj_done[0]
        attn_drive(tiles)
    P.barrier()

    def o2_ap(ch):
        if ch < 4:
            return AR2B[:, ch * 3072:(ch + 1) * 3072]
        if ch < 6:
            return AR2B[:, 19456 + (ch - 4) * 3072: 19456 + (ch - 3) * 3072]
        return AR2B[:, 13312 + (ch - 6) * 3072: 13312 + (ch - 5) * 3072]

    def wo2_ap(dk):
        if dk < 3:
            return AR2B[:, 25600 + dk * 1024: 25600 + (dk + 1) * 1024]
        return AR3B[:, (dk - 3) * 1024: (dk - 2) * 1024]

    for ch in range(6):
        VV(lambda v, ch=ch: v.tensor_copy(out=o2_ap(ch), in_=o_ap(ch)))
    omove = VSNAP()
    for i in range(4):
        par = i % 2
        d = P.emit("sync", lambda y, i=i, par=par: y.dma_start(
            out=WS[par][:, 0:2048],
            in_=h_w["o0"][256 * i:256 * (i + 1), :].rearrange(
                "(c p) m -> p c m", p=128)), deps=[ws_last[par], omove],
            sem="wy" + str(par))
        c1 = SS(lambda s, i=i, par=par: s.copy(out=wo2_ap(2 * i),
                                               in_=WS[par][:, 0:1024]), deps=[d])
        c2 = SS(lambda s, i=i, par=par: s.copy(out=wo2_ap(2 * i + 1),
                                               in_=WS[par][:, 1024:2048]), deps=[d])
        ws_last[par] = c2
    P.barrier()
    xr = [QS1, QS2]
    xr_last = [None, None]
    for tt in range(6):
        for oc in range(NCH):
            ps = (pA, pB, pC, pD)[(tt * 8 + oc) % 4]
            par = oc % 2
            dx = P.emit("gpsimd", lambda g, oc=oc, tt=tt, par=par: g.dma_start(
                out=xr[par][:, 0:512],
                in_=h_xin[oc * 128:(oc + 1) * 128,
                          512 + tt * 512: 512 + (tt + 1) * 512]),
                deps=[xr_last[par]], sem="xr" + str(par))
            for dk in range(NCH):
                t = TT(lambda t_, dk=dk, oc=oc, tt=tt, ps=ps: t_.matmul(
                    out=ps[:, 0:512], lhsT=wo2_ap(dk)[:, oc * 128:(oc + 1) * 128],
                    rhs=o2_ap(dk)[:, tt * 512:(tt + 1) * 512],
                    start=(dk == 0), stop=(dk == NCH - 1)),
                    deps=[pfree[id(ps)]] if dk == 0 else [])
            av = VV(lambda v, oc=oc, tt=tt, ps=ps, par=par: v.tensor_tensor(
                out=AR1[:, oc * 3072 + tt * 512: oc * 3072 + (tt + 1) * 512],
                in0=ps[:, 0:512], in1=xr[par][:, 0:512], op=ALU.add),
                deps=[t, dx])
            pfree[id(ps)] = av
            xr_last[par] = av
    P.barrier()
    if dbg == "x1":
        for dc in range(NCH):
            DG(lambda g, dc=dc: g.dma_start(
                out=h_dbg[dc * 128:(dc + 1) * 128, :],
                in_=AR1[:, dc * 3072:(dc + 1) * 3072]))

    stok = ln_stats(lambda dc: (AR1[:, dc * 3072:(dc + 1) * 3072], None), NOE,
                    AR2[:, 0:3072], AR2[:, 3072:6144])
    P.barrier()
    xq8_ap = lambda dc, c0, w: AR2B[:, dc * 3072 + c0: dc * 3072 + c0 + w]
    for tt in range(6):
        c0 = tt * 512
        btok = bcast_stats(c0, 512, stok)
        for dc in range(NCH):
            VV(lambda v, dc=dc, c0=c0: v.tensor_tensor(
                out=QS3[:, 0:512], in0=AR1[:, dc * 3072 + c0: dc * 3072 + c0 + 512],
                in1=QS1[:, 0:512], op=ALU.add), deps=btok)
            VV(lambda v: v.tensor_tensor(out=QS3[:, 0:512], in0=QS3[:, 0:512],
                                         in1=QS2[:, 0:512], op=ALU.mult))
            VV(lambda v: v.tensor_copy(out=Q8[:, 0:512], in_=QS3[:, 0:512]))
            VV(lambda v: v.tensor_copy(out=I8T[:, 0:512], in_=Q8[:, 0:512].bitcast(I8)))
            VV(lambda v, dc=dc, c0=c0: v.tensor_copy(
                out=xq8_ap(dc, c0, 512), in_=I8T[:, 0:512].bitcast(F8)))
    P.barrier()
    vlast_d = None
    for oc in range(NCH):
        par = oc % 2
        d = P.emit("sync", lambda y, oc=oc, par=par: y.dma_start(
            out=WS[par][:, 0:1024],
            in_=h_w["f0"][:, oc * 128:(oc + 1) * 128].rearrange(
                "(c p) m -> p c m", p=128)), deps=[ws_last[par]],
            sem="wy" + str(par))
        q1 = SS(lambda s, par=par: s.copy(out=Q8[:, 0:1024], in_=WS[par][:, 0:1024]),
                deps=[d, vlast_d, VSNAP()])
        q2 = SS(lambda s: s.copy(out=I8T[:, 0:1024], in_=Q8[:, 0:1024].bitcast(I8)))
        q3 = SS(lambda s: s.copy(out=WBF[:, 0:1024], in_=I8T[:, 0:1024].bitcast(F8)),
                deps=[wbf_last[0], wbf_tlast[0]])
        ws_last[par] = q3
        for tt in range(6):
            ps = (pA, pB, pC, pD)[(oc * 6 + tt) % 4]
            for dc in range(NCH):
                t = TT(lambda t_, dc=dc, tt=tt, ps=ps: t_.matmul(
                    out=ps[:, 0:512], lhsT=WBF[:, dc * 128:(dc + 1) * 128],
                    rhs=xq8_ap(dc, tt * 512, 512),
                    start=(dc == 0), stop=(dc == NCH - 1)),
                    deps=[q3, pfree[id(ps)]] if dc == 0 else [])
            f1 = SS(lambda s, ps=ps: s.copy(out=F16T[:, 0:512], in_=ps[:, 0:512]),
                    deps=[t, vlast_d])
            f2 = SS(lambda s: s.copy(out=I16T[:, 0:512],
                                     in_=F16T[:, 0:512].bitcast(I16)))
            pfree[id(ps)] = f2
            vlast_d = VV(lambda v, oc=oc, tt=tt: v.scalar_tensor_tensor(
                out=AR1[:, oc * 3072 + tt * 512: oc * 3072 + (tt + 1) * 512],
                in0=I16T[:, 0:512].bitcast(F16),
                scalar=B0[:, oc:oc + 1],
                in1=AR1[:, oc * 3072 + tt * 512: oc * 3072 + (tt + 1) * 512],
                op0=ALU.add, op1=ALU.add), deps=[f2])
        wbf_last[0] = TSNAP()
    P.barrier()
    if dbg == "x2":
        for dc in range(NCH):
            DG(lambda g, dc=dc: g.dma_start(
                out=h_dbg[dc * 128:(dc + 1) * 128, :],
                in_=AR1[:, dc * 3072:(dc + 1) * 3072]))

    stok = ln_stats(lambda dc: (AR1[:, dc * 3072 + 2048:(dc + 1) * 3072], None),
                    NEXT_, AR2[:, 8192:9216], AR2[:, 9216:10240])
    P.barrier()
    xln2e = lambda dc, c0, w: AR2B[:, dc * 1024 + c0: dc * 1024 + c0 + w]
    for tt in range(2):
        c0 = tt * 512
        btok = bcast_stats(c0, 512, stok)
        for dc in range(NCH):
            VV(lambda v, dc=dc, c0=c0: v.tensor_tensor(
                out=QS3[:, 0:512],
                in0=AR1[:, dc * 3072 + 2048 + c0: dc * 3072 + 2048 + c0 + 512],
                in1=QS1[:, 0:512], op=ALU.add), deps=btok)
            VV(lambda v, dc=dc, c0=c0: v.tensor_tensor(
                out=xln2e(dc, c0, 512), in0=QS3[:, 0:512], in1=QS2[:, 0:512],
                op=ALU.mult))
    P.barrier()
    k1_ap = lambda ch, c0, w: AR2B[:, 8192 + ch * 1024 + c0: 8192 + ch * 1024 + c0 + w]
    v1_ap = lambda kt: AR2B[:, 16384 + kt * 1024: 16384 + (kt + 1) * 1024]
    for ch in range(NCH):
        wb0 = (ch % 2) * 1024
        cw = load_wslice(lambda ch=ch: h_w["k1"][:, ch * 128:(ch + 1) * 128].rearrange(
            "(c p) m -> p c m", p=128), ch % 2, 1024, wb0=wb0)
        for tt in range(2):
            ps = (pA, pB, pC, pD)[(ch * 2 + tt) % 4]
            for dc in range(NCH):
                t = TT(lambda t_, dc=dc, tt=tt, ps=ps, wb0=wb0: t_.matmul(
                    out=ps[:, 0:512], lhsT=WBF[:, wb0 + dc * 128: wb0 + (dc + 1) * 128],
                    rhs=xln2e(dc, tt * 512, 512),
                    start=(dc == 0), stop=(dc == NCH - 1)),
                    deps=[cw, pfree[id(ps)]] if dc == 0 else [])
            ev = SS(lambda s, ch=ch, tt=tt, ps=ps: s.copy(
                out=k1_ap(ch, tt * 512, 512), in_=ps[:, 0:512]), deps=[t])
            pfree[id(ps)] = ev
        wbf_last[ch % 2] = SSNAP()
        wbf_tlast[ch % 2] = TSNAP()
    for vg in range(4):
        cw = load_wslice(lambda vg=vg: h_w["v1"][:, vg * 256:(vg + 1) * 256].rearrange(
            "(c p) m -> p c m", p=128), vg % 2, 2048)
        for kt in range(8):
            ps = (pA, pB, pC, pD)[(vg * 8 + kt) % 4]
            for dc in range(NCH):
                t = TT(lambda t_, dc=dc, kt=kt, ps=ps: t_.matmul(
                    out=ps[:, 0:256], lhsT=xln2e(dc, kt * 128, 128),
                    rhs=WBF[:, dc * 256:(dc + 1) * 256],
                    start=(dc == 0), stop=(dc == NCH - 1)),
                    deps=[cw, pfree[id(ps)]] if dc == 0 else [])
            ev = SS(lambda s, kt=kt, vg=vg, ps=ps: s.copy(
                out=v1_ap(kt)[:, vg * 256:(vg + 1) * 256], in_=ps[:, 0:256]),
                deps=[t])
            pfree[id(ps)] = ev
        wbf_last[0] = wbf_last[1] = SSNAP()
        wbf_tlast[0] = wbf_tlast[1] = TSNAP()
    P.barrier()
    stok = ln_stats(lambda dc: (AR1[:, dc * 3072: dc * 3072 + 2048], None), NOWN,
                    AR2[:, 0:2048], AR2[:, 2048:4096], rows=WS[0])
    P.barrier()

    def xo_ap(dc, c0, w):
        if dc < 4:
            return AR2B[:, dc * 2048 + c0: dc * 2048 + c0 + w]
        if dc < 6:
            return AR2B[:, 24576 + (dc - 4) * 2048 + c0:
                        24576 + (dc - 4) * 2048 + c0 + w]
        return AR3B[:, (dc - 6) * 2048 + c0: (dc - 6) * 2048 + c0 + w]

    for tt in range(4):
        c0 = tt * 512
        btok = bcast_stats(c0, 512, stok, rows=WS[0])
        for dc in range(NCH):
            VV(lambda v, dc=dc, c0=c0: v.tensor_tensor(
                out=QS3[:, 0:512], in0=AR1[:, dc * 3072 + c0: dc * 3072 + c0 + 512],
                in1=QS1[:, 0:512], op=ALU.add), deps=btok)
            VV(lambda v, dc=dc, c0=c0: v.tensor_tensor(
                out=xo_ap(dc, c0, 512), in0=QS3[:, 0:512], in1=QS2[:, 0:512],
                op=ALU.mult))
    ws_last[0] = TSNAP()
    P.barrier()
    q1b_ap = lambda ch, c0, w: AR1B[:, (ch * 3072 + 2048) * 2 + c0:
                                    (ch * 3072 + 2048) * 2 + c0 + w]
    for ch in range(NCH):
        _oc4 = ch
        wb0 = (ch % 2) * 1024
        cw = load_wslice(lambda ch=ch: h_w["q1"][:, ch * 128:(ch + 1) * 128].rearrange(
            "(c p) m -> p c m", p=128), ch % 2, 1024, wb0=wb0)
        for tt in range(4):
            ps = (pA, pB, pC, pD)[(_oc4 * 4 + tt) % 4]
            for dc in range(NCH):
                t = TT(lambda t_, dc=dc, tt=tt, ps=ps, wb0=wb0: t_.matmul(
                    out=ps[:, 0:512], lhsT=WBF[:, wb0 + dc * 128: wb0 + (dc + 1) * 128],
                    rhs=xo_ap(dc, tt * 512, 512),
                    start=(dc == 0), stop=(dc == NCH - 1)),
                    deps=[cw, pfree[id(ps)]] if dc == 0 else [])
            ev = SS(lambda s, ch=ch, tt=tt, ps=ps: s.activation(
                out=q1b_ap(ch, tt * 512, 512), in_=ps[:, 0:512], func=ACT.Copy,
                scale=0.125), deps=[t])
            pfree[id(ps)] = ev
        wbf_last[ch % 2] = SSNAP()
        wbf_tlast[ch % 2] = TSNAP()
    P.barrier()
    attn_reset()
    cfgL1 = dict(sA=(pC, pA), sB=(pD, pB),
                 ea=(QS3[:, 0:512], QS1[:, 0:512]),
                 eb=(QS3[:, 512:1024], QS2[:, 0:512]),
                 pnb=(PNB[:, 0:1024], AR3B[:, 4096:5120]),
                 ptb=(PTB[:, 0:1024], AR3B[:, 5120:6144]),
                 trb=(pFB[:, 0:1024], pEB[:, 0:1024]),
                 avs=(pH[0:64, 0:128], pH[64:128, 0:128]))
    tiles = []
    ntile = 0
    for h in range(16):
        ch = h // 2
        p0 = 64 * (h % 2)
        for qt in range(16):
            nt = (qt % 8) + 1
            wk = nt * 128
            if wk <= 512:
                ma, mb = MQ[:, 1152 - wk:1152], None
            else:
                ma, mb = None, MQ[:, 1152 - (wk - 512):1152]
            tiles.append((ntile,
                          q1b_ap(ch, qt * 128, 128)[p0:p0 + 64, :],
                          lambda c0, w, ch=ch, p0=p0: k1_ap(ch, c0, w)[p0:p0 + 64, :],
                          nt, ma, mb,
                          lambda d, h=h: v1_ap(d)[:, h * 64:(h + 1) * 64],
                          xo_ap(ch, qt * 128, 128)[p0:p0 + 64, :], cfgL1))
            ntile += 1
    attn_drive(tiles)
    P.barrier()
    for oc in range(NCH):
        _oc4 = oc
        wb0 = (oc % 2) * 1024
        cw = load_wslice(lambda oc=oc: h_w["o1"][:, oc * 128:(oc + 1) * 128].rearrange(
            "(c p) m -> p c m", p=128), oc % 2, 1024, wb0=wb0)
        for tt in range(4):
            ps = (pA, pB, pC, pD)[(_oc4 * 4 + tt) % 4]
            for dk in range(NCH):
                t = TT(lambda t_, dk=dk, tt=tt, ps=ps, wb0=wb0: t_.matmul(
                    out=ps[:, 0:512], lhsT=WBF[:, wb0 + dk * 128: wb0 + (dk + 1) * 128],
                    rhs=xo_ap(dk, tt * 512, 512),
                    start=(dk == 0), stop=(dk == NCH - 1)),
                    deps=[cw, pfree[id(ps)]] if dk == 0 else [])
            av = VV(lambda v, oc=oc, tt=tt, ps=ps: v.tensor_tensor(
                out=AR1[:, oc * 3072 + tt * 512: oc * 3072 + (tt + 1) * 512],
                in0=AR1[:, oc * 3072 + tt * 512: oc * 3072 + (tt + 1) * 512],
                in1=ps[:, 0:512], op=ALU.add), deps=[t])
            pfree[id(ps)] = av
        wbf_last[oc % 2] = VSNAP()
        wbf_tlast[oc % 2] = TSNAP()
    P.barrier()
    if dbg == "x3":
        for dc in range(NCH):
            DG(lambda g, dc=dc: g.dma_start(
                out=h_dbg[dc * 128:(dc + 1) * 128, :],
                in_=AR1[:, dc * 3072:(dc + 1) * 3072]))

    stok = ln_stats(lambda dc: (AR1[:, dc * 3072: dc * 3072 + 2048], None), NOWN,
                    AR2[:, 8192:10240], AR2[:, 10240:12288])
    P.barrier()
    xln3 = lambda dc, c0, w: AR2B[:, dc * 2048 + c0: dc * 2048 + c0 + w]
    for tt in range(4):
        c0 = tt * 512
        btok = bcast_stats(c0, 512, stok)
        for dc in range(NCH):
            VV(lambda v, dc=dc, c0=c0: v.tensor_tensor(
                out=QS3[:, 0:512], in0=AR1[:, dc * 3072 + c0: dc * 3072 + c0 + 512],
                in1=QS1[:, 0:512], op=ALU.add), deps=btok)
            VV(lambda v, dc=dc, c0=c0: v.tensor_tensor(
                out=xln3(dc, c0, 512), in0=QS3[:, 0:512], in1=QS2[:, 0:512],
                op=ALU.mult))
    P.barrier()
    dwg = P.emit("sync", lambda y: y.dma_start(
        out=WS[0][:, 0:64], in_=h_wg[:, :].rearrange("(c p) m -> p c m", p=128)),
        deps=[ws_last[0]], sem="wy0")
    cwg = SS(lambda s: s.copy(out=WBF[:, 0:64], in_=WS[0][:, 0:64]),
             deps=[dwg, wbf_last[0], wbf_tlast[0]])
    ws_last[0] = cwg
    for tt in range(4):
        for dc in range(NCH):
            t = TT(lambda t_, dc=dc, tt=tt: t_.matmul(
                out=pC[0:8, 0:512], lhsT=WBF[:, dc * 8:(dc + 1) * 8],
                rhs=xln3(dc, tt * 512, 512),
                start=(dc == 0), stop=(dc == NCH - 1)),
                deps=[cwg, pfree[id(pC)]] if dc == 0 else [])
        ev = VV(lambda v, tt=tt: v.tensor_scalar(
            out=AR3[32:40, tt * 512:(tt + 1) * 512], in0=pC[0:8, 0:512],
            scalar1=BG[0:8, 0:1], scalar2=None, op0=ALU.add), deps=[t])
        pfree[id(pC)] = ev
    lg_done = SSNAP()
    for ts in range(16):
        t = TT(lambda t_, ts=ts: t_.transpose(
            out=pG[:, 0:8], in_=AR3[32:40, ts * 128:(ts + 1) * 128],
            identity=IDF[32:40, 32:40]), deps=[lg_done, pfree[id(pG)]])
        ev = SS(lambda s, ts=ts: s.copy(out=QS3[:, 128 + ts * 8: 128 + (ts + 1) * 8],
                                        in_=pG[:, 0:8]), deps=[t])
        pfree[id(pG)] = ev
        r1 = VV(lambda v, ts=ts: v.tensor_reduce(
            out=RMAX[:, 0:1], in_=QS3[:, 128 + ts * 8: 128 + (ts + 1) * 8], axis=AXX,
            op=ALU.max, negate=True), deps=[ev, SSNAP()])
        r2 = SS(lambda s, ts=ts: s.activation(
            out=QS3[:, ts * 8:(ts + 1) * 8], in_=QS3[:, 128 + ts * 8: 128 + (ts + 1) * 8],
            func=ACT.Exp, bias=RMAX[:, 0:1], accum_out=RSUM[:, 0:1]), deps=[r1])
        VV(lambda v, ts=ts: v.reciprocal(out=QS3[:, 272 + ts: 273 + ts], in_=RSUM[:, 0:1]),
           deps=[r2])
    sgate = SSNAP()
    VV(lambda v: v.tensor_scalar(out=QS3[:, 0:128], in0=QS3[:, 0:128], scalar1=1.0,
                                 scalar2=None, op0=ALU.is_ge), deps=[sgate])
    VV(lambda v: v.tensor_copy(out=QS3[:, 256:272],
                               in_=bass.AP(QS3, 0, [[1024, 128], [8, 16]])))
    for e in range(1, 8):
        ge = bass.AP(QS3, e, [[1024, 128], [8, 16]])
        VV(lambda v: v.tensor_scalar(out=QS1[:, 0:16], in0=QS3[:, 256:272],
                                     scalar1=-1.0, scalar2=1.0,
                                     op0=ALU.mult, op1=ALU.add))
        VV(lambda v, ge=ge: v.tensor_tensor(out=ge, in0=ge, in1=QS1[:, 0:16],
                                            op=ALU.mult))
        VV(lambda v, ge=ge: v.tensor_tensor(out=QS3[:, 256:272], in0=QS3[:, 256:272],
                                            in1=ge, op=ALU.add))
    for e in range(8):
        ge = bass.AP(QS3, e, [[1024, 128], [8, 16]])
        VV(lambda v, ge=ge: v.tensor_tensor(out=ge, in0=ge, in1=QS3[:, 272:288],
                                            op=ALU.mult))
    gt_done = VSNAP()
    P.barrier()

    def acc_ap(ts, c0, w):
        if ts < 6:
            return AR2[:, 8192 + ts * 1024 + c0: 8192 + ts * 1024 + c0 + w]
        if ts < 14:
            base = (ts - 6) * 3072 + 2048
            return AR1[:, base + c0: base + c0 + w]
        return AR3[:, (ts - 14) * 1024 + c0: (ts - 14) * 1024 + c0 + w]

    db2 = P.emit("sync", lambda y: y.dma_start(out=WS[1][0:8, 0:1024], in_=h_b2[:, :]),
                 deps=[ws_last[1]], sem="wy1")
    qs1g_last = None
    for ts in range(16):
        tg = TT(lambda t_, ts=ts: t_.transpose(
            out=pG[0:8, 0:128], in_=QS3[:, ts * 8:(ts + 1) * 8], identity=IDF[:, :]),
            deps=[gt_done, pfree[id(pG)]])
        eg = SS(lambda s: s.copy(out=QS1[0:8, 0:128], in_=pG[0:8, 0:128]),
                deps=[tg, qs1g_last, VSNAP()])
        pfree[id(pG)] = eg
        for oh in range(2):
            ps = pC if oh == 0 else pD
            psx = pE if oh == 0 else pF
            mb = TT(lambda t_, oh=oh, ps=ps: t_.matmul(
                out=ps[:, 0:512], lhsT=QS1[0:8, 0:128],
                rhs=WS[1][0:8, oh * 512:(oh + 1) * 512], start=True, stop=True),
                deps=[eg, db2, pfree[id(ps)]])
            qs1g_last = mb
            trs = []
            for k in range(4):
                dc = oh * 4 + k
                t = TT(lambda t_, dc=dc, ts=ts, k=k, psx=psx: t_.transpose(
                    out=psx[:, k * 128:(k + 1) * 128],
                    in_=AR1[:, dc * 3072 + ts * 128: dc * 3072 + (ts + 1) * 128],
                    identity=IDF[:, :]),
                    deps=[pfree[id(psx)]] if k == 0 else [])
                trs.append(t)
            eb2 = SS(lambda s, ps=ps: s.copy(out=QS3[:, 512:1024], in_=ps[:, 0:512]),
                     deps=[mb, qs1g_last, VSNAP()])
            av = VV(lambda v, ts=ts, oh=oh, psx=psx: v.tensor_tensor(
                out=acc_ap(ts, oh * 512, 512), in0=psx[:, 0:512], in1=QS3[:, 512:1024],
                op=ALU.add), deps=trs + [eb2])
            pfree[id(ps)] = av
            pfree[id(psx)] = av
    P.barrier()

    def w1bf_ap(dc):
        r = dc // 4
        return AR1B[:, r * 6144 + (dc % 4) * 1024: r * 6144 + (dc % 4) * 1024 + 1024]

    def w2bf_ap(ms):
        r = 2 + ms // 4
        return AR1B[:, r * 6144 + (ms % 4) * 1024: r * 6144 + (ms % 4) * 1024 + 1024]

    def h_ap(ms, tq):
        idx = ms * 4 + tq
        r = 4 + idx // 8
        return AR1B[:, r * 6144 + (idx % 8) * 512: r * 6144 + (idx % 8) * 512 + 512]

    w1_last = None
    w2_last = None
    for e in range(8):
        for mc in range(4):
            c1toks = []
            for i in range(4):
                par = i % 2
                d = P.emit("sync", lambda y, e=e, mc=mc, i=i, par=par: y.dma_start(
                    out=WS[par][:, 0:2048],
                    in_=h_w1[e, 256 * i:256 * (i + 1),
                             mc * 1024:(mc + 1) * 1024].rearrange(
                        "(c p) m -> p c m", p=128)),
                    deps=[ws_last[par], w1_last], sem="wy" + str(par))
                dst0 = (2 * i // 4) * 6144 + ((2 * i) % 4) * 1024
                c = SS(lambda s, par=par, dst0=dst0: s.copy(
                    out=AR1B[:, dst0:dst0 + 2048], in_=WS[par][:, 0:2048]), deps=[d])
                ws_last[par] = c
                c1toks.append(c)
            c2toks = []
            for i in range(4):
                par = i % 2
                d = P.emit("sync", lambda y, e=e, mc=mc, i=i, par=par: y.dma_start(
                    out=WS[par][:, 0:2048],
                    in_=h_w2[e, mc * 1024 + 256 * i: mc * 1024 + 256 * (i + 1),
                             :].rearrange("(c p) m -> p c m", p=128)),
                    deps=[ws_last[par], w2_last], sem="wy" + str(par))
                dst0 = (2 + (2 * i) // 4) * 6144 + ((2 * i) % 4) * 1024
                c = SS(lambda s, par=par, dst0=dst0: s.copy(
                    out=AR1B[:, dst0:dst0 + 2048], in_=WS[par][:, 0:2048]), deps=[d])
                ws_last[par] = c
                c2toks.append(c)
            h_last = None
            for ms in range(8):
                for tq in range(4):
                    ps = (pA, pB, pF)[(ms * 4 + tq) % 3]
                    dep0 = c1toks + [pfree[id(ps)], w2_last]
                    for dc in range(NCH):
                        t = TT(lambda t_, dc=dc, ms=ms, tq=tq, ps=ps: t_.matmul(
                            out=ps[:, 0:512],
                            lhsT=w1bf_ap(dc)[:, ms * 128:(ms + 1) * 128],
                            rhs=xln3(dc, tq * 512, 512),
                            start=(dc == 0), stop=(dc == NCH - 1)),
                            deps=dep0 if dc == 0 else [])
                    ev = SS(lambda s, e=e, mc=mc, ms=ms, tq=tq, ps=ps: s.activation(
                        out=h_ap(ms, tq), in_=ps[:, 0:512], func=ACT.Relu,
                        bias=B1[:, e * 32 + mc * 8 + ms: e * 32 + mc * 8 + ms + 1]),
                        deps=[t, w2_last] if (ms == 0 and tq == 0) else [t])
                    pfree[id(ps)] = ev
                    h_last = ev
            w1_last = h_last
            psl = [pC, pD, pE]
            for ts in range(16):
                for oh in range(2):
                    ps = psl[(ts * 2 + oh) % 3]
                    dep0 = c2toks + [pfree[id(ps)], h_last]
                    for ms in range(8):
                        t = TT(lambda t_, ms=ms, ts=ts, oh=oh, ps=ps: t_.matmul(
                            out=ps[:, 0:512],
                            lhsT=h_ap(ms, ts // 4)[:, (ts % 4) * 128:
                                                   (ts % 4 + 1) * 128],
                            rhs=w2bf_ap(ms)[:, oh * 512:(oh + 1) * 512],
                            start=(ms == 0), stop=(ms == 7)),
                            deps=dep0 if ms == 0 else [])
                    w2_last = t
                    av = VV(lambda v, e=e, ts=ts, oh=oh, ps=ps: v.scalar_tensor_tensor(
                        out=acc_ap(ts, oh * 512, 512), in0=ps[:, 0:512],
                        scalar=QS3[:, ts * 8 + e: ts * 8 + e + 1],
                        in1=acc_ap(ts, oh * 512, 512),
                        op0=ALU.mult, op1=ALU.add), deps=[t])
                    pfree[id(ps)] = av
    P.barrier()

    for ts in range(16):
        DG(lambda g, ts=ts: g.dma_start(
            out=h_out[ts * 128:(ts + 1) * 128, :], in_=acc_ap(ts, 0, 1024)))
    P.emit("gpsimd", None, deps=[("gpsimd", P.cnt["gpsimd"])], inc=0)

    with nc.Block() as block:
        P.run(block, sems)
    es.close()
    return nc


_NC_CACHE = {}


def _get_nc(dbg=None):
    if dbg not in _NC_CACHE:
        _NC_CACHE[dbg] = build_nc(dbg)
    return _NC_CACHE[dbg]


def make_inputs(inputs):
    f = np.float32
    import ml_dtypes
    x = np.asarray(inputs["L0_attn_x"], dtype=f)
    import ml_dtypes as _mld
    idx = np.arange(128)
    m_su = np.where(idx[None, :] > idx[:, None], 0.0, NEG).astype(f)
    m_li = np.where(idx[None, :] <= idx[:, None], 0.0, NEG).astype(f)
    mask_own = np.concatenate([m_su, np.zeros((128, 384), f), m_li], axis=1)
    mq = np.concatenate([np.zeros((128, 512), f), mask_own], axis=1)
    common = {
        "w_q0": np.ascontiguousarray(inputs["L0_attn_Wq"].T.astype(f)),
        "w_k0": np.ascontiguousarray(inputs["L0_attn_Wk"].T.astype(f)),
        "w_v0": np.ascontiguousarray(inputs["L0_attn_Wv"].T.astype(f)),
        "w_o0": np.ascontiguousarray(inputs["L0_attn_Wo"].T.astype(f)),
        "w_q1": np.ascontiguousarray(inputs["L1_attn_Wq"].T.astype(f)),
        "w_k1": np.ascontiguousarray(inputs["L1_attn_Wk"].T.astype(f)),
        "w_v1": np.ascontiguousarray(inputs["L1_attn_Wv"].T.astype(f)),
        "w_o1": np.ascontiguousarray(inputs["L1_attn_Wo"].T.astype(f)),
        "w_f0": np.ascontiguousarray(inputs["L0_ffn_weight"].T.astype(f)),
        "w_moe1": np.ascontiguousarray(inputs["L1_ffn_W1"].astype(f)),
        "w_moe2": np.ascontiguousarray(inputs["L1_ffn_W2"].astype(f)),
        "w_gate": np.ascontiguousarray(inputs["L1_ffn_w_gate"].astype(f)),
        "b_gate": np.ascontiguousarray(inputs["L1_ffn_b_gate"].astype(f)
                                       .reshape(8, 1)),
        "b_moe1": np.ascontiguousarray(
            inputs["L1_ffn_b1"].astype(f).reshape(8, 32, 128).transpose(2, 0, 1)
            .reshape(128, 256)),
        "b_moe2": np.ascontiguousarray(inputs["L1_ffn_b2"].astype(f)),
        "b_ffn0": np.ascontiguousarray(inputs["L0_ffn_bias"].astype(f)
                                       .reshape(8, 128).T),
        "mq_bf": mq.astype(_mld.bfloat16),
        "ident_f": np.eye(128, dtype=f),
        "ident_b": np.eye(128, dtype=f).astype(ml_dtypes.bfloat16),
        "ones_c": np.ones((128, 1), f),
        "ones_r": np.ones((33, 128), f),
    }
    in_maps = []
    for c in range(8):
        b, hh = c // 2, c % 2
        xb = x[b]
        own = xb[hh * 2048:(hh + 1) * 2048]
        halo = xb[hh * 2048 - 512: hh * 2048] if hh else np.zeros((512, D), f)
        ext = xb[0:1024]
        m = dict(common)
        m["xin"] = np.ascontiguousarray(np.concatenate([halo, own, ext], axis=0).T)
        bcm = np.empty((128, 2048), f)
        for qt in range(4):
            v = mask_own[:, 0:512].copy()
            if hh == 0:
                v[:, 0:512 - 128 * qt] += NEG
            bcm[:, qt * 512:(qt + 1) * 512] = v
        m["bcm_bf"] = np.ascontiguousarray(bcm.astype(_mld.bfloat16))
        in_maps.append(m)
    return in_maps


def kernel(**inputs):
    nc = _get_nc()
    in_maps = make_inputs(inputs)
    res = run_bass_kernel_spmd(nc, in_maps, core_ids=list(range(8)))
    out = np.empty((4, 4096, D), dtype=np.float32)
    for c in range(8):
        b, hh = c // 2, c % 2
        out[b, hh * 2048:(hh + 1) * 2048] = res.results[c]["out"]
    return out


# version 18
# speedup vs baseline: 1.1473x; 1.0551x over previous
import numpy as np
from contextlib import ExitStack
import concourse.bass as bass
import concourse.mybir as mybir
from concourse.bass_utils import run_bass_kernel_spmd

F32 = mybir.dt.float32
BF16 = mybir.dt.bfloat16
F16 = mybir.dt.float16
F8 = mybir.dt.float8e4
I8 = mybir.dt.int8
I16 = mybir.dt.int16
ACT = mybir.ActivationFunctionType
ALU = mybir.AluOpType
AXX = mybir.AxisListType.X

D = 1024
NCH = 8
NOWN = 2048
NTOT = 3584
NOE = 3072
NEXT_ = 1024
NEG = -1.0e30
EPS = 1e-5


class Prog:
    ENGINES = ("tensor", "vector", "scalar", "sync", "gpsimd")
    VSEMS = ("cst", "xs0", "xs1", "xr0", "xr1", "wy0", "wy1")
    DMAS = ("sync", "gpsimd", "cst", "xs0", "xs1", "xr0", "xr1", "wy0", "wy1")

    def __init__(self):
        self.ops = {e: [] for e in self.ENGINES}
        allsems = self.ENGINES + self.VSEMS
        self.cnt = {e: 0 for e in allsems}
        self.hwm = {e: {x: 0 for x in allsems} for e in self.ENGINES}

    def emit(self, eng, fn, deps=(), selfdep=False, inc=1, sem=None):
        sem = sem or eng
        waits = []
        dd = [d for d in deps if d is not None]
        if selfdep and self.cnt[eng] > 0:
            dd.append((eng, self.cnt[eng]))
        for peng, pval in dd:
            if pval <= 0:
                continue
            if self.hwm[eng][peng] < pval:
                self.hwm[eng][peng] = pval
                waits.append((peng, pval))
        self.ops[eng].append((waits, fn, inc, sem))
        if fn is not None and inc:
            self.cnt[sem] += inc
            return (sem, self.cnt[sem])
        return None

    def barrier(self):
        snap = dict(self.cnt)
        allsems = self.ENGINES + self.VSEMS
        for e in self.ENGINES:
            self.emit(e, None, deps=[(p, snap[p]) for p in allsems if p != e],
                      inc=0)

    def run(self, block, sems):
        mult = {e: (16 if e in self.DMAS else 1) for e in self.ENGINES + self.VSEMS}

        def mk(eng_name):
            def body(eng):
                for waits, fn, inc, sem in self.ops[eng_name]:
                    for peng, pval in waits:
                        eng.wait_ge(sems[peng], pval * mult[peng])
                    if fn is not None:
                        inst = fn(eng)
                        if inc:
                            inst.then_inc(sems[sem], inc * mult[sem])
            return body
        block.tensor(mk("tensor"))
        block.vector(mk("vector"))
        block.scalar(mk("scalar"))
        block.sync(mk("sync"))
        block.gpsimd(mk("gpsimd"))


def build_nc(dbg=None):
    nc = bass.Bass("TRN2", target_bir_lowering=False)
    P = Prog()

    h_xin = nc.dram_tensor("xin", [D, NTOT], F32, kind="ExternalInput").ap()
    h_w = {}
    for nm in ("q0", "k0", "v0", "o0", "q1", "k1", "v1", "o1", "f0"):
        h_w[nm] = nc.dram_tensor("w_" + nm, [D, D], F32, kind="ExternalInput").ap()
    h_w1 = nc.dram_tensor("w_moe1", [8, D, 4096], F32, kind="ExternalInput").ap()
    h_w2 = nc.dram_tensor("w_moe2", [8, 4096, D], F32, kind="ExternalInput").ap()
    h_wg = nc.dram_tensor("w_gate", [D, 8], F32, kind="ExternalInput").ap()
    h_bg = nc.dram_tensor("b_gate", [8, 1], F32, kind="ExternalInput").ap()
    h_b1 = nc.dram_tensor("b_moe1", [128, 256], F32, kind="ExternalInput").ap()
    h_b2 = nc.dram_tensor("b_moe2", [8, D], F32, kind="ExternalInput").ap()
    h_b0 = nc.dram_tensor("b_ffn0", [128, 8], F32, kind="ExternalInput").ap()
    h_mq = nc.dram_tensor("mq_bf", [128, 1152], BF16, kind="ExternalInput").ap()
    h_bcm = nc.dram_tensor("bcm_bf", [128, 2048], BF16, kind="ExternalInput").ap()
    h_idf = nc.dram_tensor("ident_f", [128, 128], F32, kind="ExternalInput").ap()
    h_idb = nc.dram_tensor("ident_b", [128, 128], BF16, kind="ExternalInput").ap()
    h_onc = nc.dram_tensor("ones_c", [128, 1], F32, kind="ExternalInput").ap()
    h_onr = nc.dram_tensor("ones_r", [33, 128], F32, kind="ExternalInput").ap()
    h_out = nc.dram_tensor("out", [NOWN, D], F32, kind="ExternalOutput").ap()
    h_dbg = None
    if dbg is not None:
        h_dbg = nc.dram_tensor("dbgout", [D, NOE], F32, kind="ExternalOutput").ap()

    es = ExitStack()
    sbt = lambda nm, shp, dt: es.enter_context(nc.sbuf_tensor(nm, shp, dt))
    pst = lambda nm, shp: es.enter_context(nc.psum_tensor(nm, shp, F32))

    AR1 = sbt("AR1", [128, 24576], F32)
    AR2 = sbt("AR2", [128, 14336], F32)
    AR3 = sbt("AR3", [128, 3584], F32)
    WS = [sbt("ws0", [128, 2048], F32), sbt("ws1", [128, 2048], F32)]
    WBF = sbt("wbf", [128, 2048], BF16)
    MQ = sbt("mqbf", [128, 1152], BF16)
    IDF = sbt("identf", [128, 128], F32)
    IDB = sbt("identb", [128, 128], BF16)
    ONC = sbt("onesc", [128, 1], F32)
    ONR = sbt("onesr", [33, 128], F32)
    B1 = sbt("b1sb", [128, 256], F32)
    B0 = sbt("b0sb", [128, 8], F32)
    BG = sbt("bgsb", [8, 1], F32)
    QS1 = sbt("qs1", [128, 512], F32)
    QS2 = sbt("qs2", [128, 512], F32)
    QS3 = sbt("qs3", [128, 1024], F32)
    GT = QS3
    PNB = sbt("pnb", [128, 1024], BF16)
    PTB = sbt("ptb", [128, 1024], BF16)
    RMAX = sbt("rmax", [128, 8], F32)
    RSUM = sbt("rsum", [128, 4], F32)
    RINV = sbt("rinv", [128, 2], F32)
    Q8 = sbt("q8t", [128, 1024], F8)
    I8T = sbt("i8t", [128, 1024], I8)
    F16T = sbt("f16t", [128, 512], F16)
    I16T = sbt("i16t", [128, 512], I16)

    AR1B = AR1.bitcast(BF16)
    AR2B = AR2.bitcast(BF16)
    AR3B = AR3.bitcast(BF16)

    pA = pst("pA", [128, 512])
    pB = pst("pB", [128, 512])
    pC = pst("pC", [128, 512])
    pD = pst("pD", [128, 512])
    pE = pst("pE", [128, 512])
    pF = pst("pF", [128, 512])
    pG = pst("pG", [128, 128])
    pH = pst("pH", [128, 128])

    pFB = pF.bitcast(BF16)
    pDB = pD.bitcast(BF16)
    pEB = pE.bitcast(BF16)
    pGB = pG.bitcast(BF16)
    sems = {e: es.enter_context(nc.semaphore("sem_" + e))
            for e in Prog.ENGINES + Prog.VSEMS}

    TT = lambda fn, deps=(): P.emit("tensor", fn, deps=deps)
    VV = lambda fn, deps=(): P.emit("vector", fn, deps=deps, selfdep=True)
    SS = lambda fn, deps=(): P.emit("scalar", fn, deps=deps, selfdep=True)
    DY = lambda fn, deps=(): P.emit("sync", fn, deps=deps)
    DG = lambda fn, deps=(): P.emit("gpsimd", fn, deps=deps)
    TSNAP = lambda: ("tensor", P.cnt["tensor"])
    VSNAP = lambda: ("vector", P.cnt["vector"])
    SSNAP = lambda: ("scalar", P.cnt["scalar"])

    pfree = {id(p): None for p in (pA, pB, pC, pD, pE, pF, pG, pH)}
    ws_last = [None, None]
    wbf_last = [None, None]
    wbf_tlast = [None, None]

    P.emit("gpsimd", lambda g: g.dma_start(out=MQ[:], in_=h_mq[:, :]), sem="cst")
    P.emit("gpsimd", lambda g: g.dma_start(out=AR2.bitcast(BF16)[:, 19456:21504], in_=h_bcm[:, :]), sem="cst")
    P.emit("gpsimd", lambda g: g.dma_start(out=IDF[:], in_=h_idf[:, :]), sem="cst")
    P.emit("gpsimd", lambda g: g.dma_start(out=IDB[:], in_=h_idb[:, :]), sem="cst")
    P.emit("gpsimd", lambda g: g.dma_start(out=ONC[:], in_=h_onc[:, :]), sem="cst")
    P.emit("gpsimd", lambda g: g.dma_start(out=ONR[:], in_=h_onr[:, :]), sem="cst")
    P.emit("gpsimd", lambda g: g.dma_start(out=B1[:], in_=h_b1[:, :]), sem="cst")
    P.emit("gpsimd", lambda g: g.dma_start(out=B0[:], in_=h_b0[:, :]), sem="cst")
    P.emit("gpsimd", lambda g: g.dma_start(out=BG[:], in_=h_bg[:, :]), sem="cst")
    P.barrier()

    src_cons = [None, None]

    def ln_stats(src, n, accx, accsq, rows=None):
        if rows is None:
            rows = AR3
        sq = (QS1[:, 0:512], QS2[:, 0:512])
        sq_cons = [None, None]
        for dc in range(NCH):
            ap, dtok = src(dc)
            if dc == 0:
                VV(lambda v, ap=ap: v.tensor_copy(out=accx[:, 0:n], in_=ap),
                   deps=[dtok])
            else:
                VV(lambda v, ap=ap: v.tensor_tensor(out=accx[:, 0:n], in0=accx[:, 0:n],
                                                    in1=ap, op=ALU.add), deps=[dtok])
            for tt in range((n + 511) // 512):
                c0 = tt * 512
                w = min(512, n - c0)
                qp = (dc + tt) % 2
                ts = SS(lambda s, ap=ap, c0=c0, w=w, qp=qp: s.activation(
                    out=sq[qp][:, 0:w], in_=ap[:, c0:c0 + w], func=ACT.Square),
                    deps=[sq_cons[qp], dtok])
                if dc == 0:
                    av = VV(lambda v, c0=c0, w=w, qp=qp: v.tensor_copy(
                        out=accsq[:, c0:c0 + w], in_=sq[qp][:, 0:w]), deps=[ts])
                else:
                    av = VV(lambda v, c0=c0, w=w, qp=qp: v.tensor_tensor(
                        out=accsq[:, c0:c0 + w], in0=accsq[:, c0:c0 + w],
                        in1=sq[qp][:, 0:w], op=ALU.add), deps=[ts])
                sq_cons[qp] = av
            src_cons[dc % 2] = av
        vacc = VSNAP()
        evs = []
        for tt in range((n + 511) // 512):
            c0 = tt * 512
            w = min(512, n - c0)
            m1 = TT(lambda t, c0=c0, w=w: t.matmul(
                out=pE[0:1, 0:w], lhsT=ONC[:, :], rhs=accx[:, c0:c0 + w],
                start=True, stop=True), deps=[vacc, pfree[id(pE)]])
            e1 = SS(lambda s, c0=c0, w=w: s.copy(out=rows[0:1, c0:c0 + w],
                                                 in_=pE[0:1, 0:w]), deps=[m1])
            pfree[id(pE)] = e1
            m2 = TT(lambda t, c0=c0, w=w: t.matmul(
                out=pF[0:1, 0:w], lhsT=ONC[:, :], rhs=accsq[:, c0:c0 + w],
                start=True, stop=True), deps=[vacc, pfree[id(pF)]])
            e2 = SS(lambda s, c0=c0, w=w: s.copy(out=rows[32:33, c0:c0 + w],
                                                 in_=pF[0:1, 0:w]), deps=[m2])
            pfree[id(pF)] = e2
            evs += [e1, e2]
        return evs + [SSNAP()]

    def bcast_stats(c0, w, stok, rows=None, dst1=None, dst2=None):
        if rows is None:
            rows = AR3
        if dst1 is None:
            dst1 = QS1[:, 0:w]
        if dst2 is None:
            dst2 = QS2[:, 0:w]
        war = [VSNAP(), TSNAP()]
        m1 = TT(lambda t, c0=c0, w=w: t.matmul(
            out=pE[:, 0:w], lhsT=ONR[0:1, :], rhs=rows[0:1, c0:c0 + w],
            start=True, stop=True), deps=list(stok) + [pfree[id(pE)]])
        e1 = SS(lambda s, w=w, dst1=dst1: s.copy(out=dst1, in_=pE[:, 0:w]),
                deps=[m1] + war)
        pfree[id(pE)] = e1
        m2 = TT(lambda t, c0=c0, w=w: t.matmul(
            out=pF[:, 0:w], lhsT=ONR[32:33, :], rhs=rows[32:33, c0:c0 + w],
            start=True, stop=True), deps=list(stok) + [pfree[id(pF)]])
        e2 = SS(lambda s, w=w, dst2=dst2: s.copy(out=dst2, in_=pF[:, 0:w]),
                deps=[m2])
        pfree[id(pF)] = e2
        tmp = QS3[:, 512:512 + w]
        VV(lambda v, dst1=dst1: v.tensor_scalar(out=dst1, in0=dst1,
                                                scalar1=-1.0 / D, scalar2=None,
                                                op0=ALU.mult), deps=[e1])
        VV(lambda v, dst2=dst2: v.tensor_scalar(out=dst2, in0=dst2,
                                                scalar1=1.0 / D, scalar2=EPS,
                                                op0=ALU.mult, op1=ALU.add),
           deps=[e2])
        VV(lambda v, dst1=dst1, tmp=tmp: v.tensor_tensor(out=tmp, in0=dst1,
                                                         in1=dst1, op=ALU.mult))
        tv = VV(lambda v, dst2=dst2, tmp=tmp: v.tensor_tensor(
            out=dst2, in0=dst2, in1=tmp, op=ALU.subtract))
        tq = SS(lambda s, dst2=dst2: s.activation(out=dst2, in_=dst2,
                                                  func=ACT.Sqrt), deps=[tv])
        tr = VV(lambda v, dst2=dst2: v.reciprocal(out=dst2, in_=dst2), deps=[tq])
        return [tr, e1]

    def load_wslice(hfn, par, nf32, prior=(), wb0=0):
        d = P.emit("sync", lambda y, hfn=hfn, par=par, nf32=nf32: y.dma_start(
            out=WS[par][:, 0:nf32], in_=hfn()), deps=[ws_last[par]] + list(prior),
            sem="wy" + str(par))
        if nf32 <= 1024:
            wdeps = [wbf_last[wb0 // 1024], wbf_tlast[wb0 // 1024]]
        else:
            wdeps = [wbf_last[0], wbf_last[1], wbf_tlast[0], wbf_tlast[1]]
        c = SS(lambda s, par=par, nf32=nf32, wb0=wb0: s.copy(
            out=WBF[:, wb0:wb0 + nf32], in_=WS[par][:, 0:nf32]),
            deps=[d] + wdeps)
        ws_last[par] = c
        return c

    att_st = {}

    def attn_reset():
        att_st.clear()
        for p in (0, 1):
            att_st[p] = dict(exp=None, nrm=None, tr=None, pte=None, avm=None,
                             oev=None, rmx=None, rcp=None)

    def attn_s1(n, qap, kap_fn, nt, mask_a, mask_b, v_fn, o_ap, cfg):
        p = n % 2
        st = att_st[p]
        wk = nt * 128
        w0 = min(wk, 512)
        wb = wk - w0
        sA = cfg["sA"][p]
        sB = cfg["sB"][p]
        m1 = TT(lambda t: t.matmul(out=sA[:, 0:w0], lhsT=qap, rhs=kap_fn(0, w0),
                                   start=True, stop=(mask_a is None)),
                deps=[st["exp"], st["rmx"]])
        if mask_a is not None:
            m1 = TT(lambda t: t.matmul(out=sA[:, 0:w0], lhsT=IDB[:, :], rhs=mask_a,
                                       start=False, stop=True))
        m2 = None
        if wb > 0:
            m2 = TT(lambda t: t.matmul(out=sB[:, 0:wb], lhsT=qap, rhs=kap_fn(w0, wb),
                                       start=True, stop=(mask_b is None)))
            if mask_b is not None:
                m2 = TT(lambda t: t.matmul(out=sB[:, 0:wb], lhsT=IDB[:, :],
                                           rhs=mask_b, start=False, stop=True))
        r1 = VV(lambda v: v.tensor_reduce(out=RMAX[:, 4 * p:4 * p + 1],
                                          in_=sA[:, 0:w0], axis=AXX,
                                          op=ALU.max, negate=True),
                deps=[m1, st["exp"]])
        rc = r1
        rcol = 4 * p
        if wb > 0:
            r2 = VV(lambda v: v.tensor_reduce(out=RMAX[:, 4 * p + 1:4 * p + 2],
                                              in_=sB[:, 0:wb], axis=AXX,
                                              op=ALU.max, negate=True), deps=[m2])
            rc = VV(lambda v: v.tensor_tensor(out=RMAX[:, 4 * p + 2:4 * p + 3],
                                              in0=RMAX[:, 4 * p:4 * p + 1],
                                              in1=RMAX[:, 4 * p + 1:4 * p + 2],
                                              op=ALU.min))
            rcol = 4 * p + 2
        st["rmx"] = rc
        return dict(n=n, nt=nt, wk=wk, w0=w0, wb=wb, vfn=v_fn, oap=o_ap, cfg=cfg,
                    rc=rc, rcol=rcol, m1=m1, m2=m2)

    def attn_s2a(cx):
        p = cx["n"] % 2
        st = att_st[p]
        cfg = cx["cfg"]
        e1 = SS(lambda s: s.activation(out=cfg["ea"][p][:, 0:cx["w0"]],
                                       in_=cfg["sA"][p][:, 0:cx["w0"]],
                                       func=ACT.Exp,
                                       bias=RMAX[:, cx["rcol"]:cx["rcol"] + 1],
                                       accum_out=RSUM[:, 2 * p:2 * p + 1]),
                deps=[cx["rc"], st["rcp"], st["nrm"]])
        elast = e1
        if cx["wb"] > 0:
            elast = SS(lambda s: s.activation(out=cfg["eb"][p][:, 0:cx["wb"]],
                                              in_=cfg["sB"][p][:, 0:cx["wb"]],
                                              func=ACT.Exp,
                                              bias=RMAX[:, cx["rcol"]:cx["rcol"] + 1],
                                              accum_out=RSUM[:, 2 * p + 1:2 * p + 2]))
        st["exp"] = elast
        cx["elast"] = elast

    def attn_s2b(cx):
        p = cx["n"] % 2
        st = att_st[p]
        if cx["wb"] > 0:
            sadd = VV(lambda v: v.tensor_tensor(out=RSUM[:, 2 * p:2 * p + 1],
                                                in0=RSUM[:, 2 * p:2 * p + 1],
                                                in1=RSUM[:, 2 * p + 1:2 * p + 2],
                                                op=ALU.add), deps=[cx["elast"]])
        else:
            sadd = cx["elast"]
        rcp = VV(lambda v: v.reciprocal(out=RINV[:, p:p + 1],
                                        in_=RSUM[:, 2 * p:2 * p + 1]), deps=[sadd])
        st["rcp"] = rcp
        cx["rcp"] = rcp

    def attn_s3(cx):
        p = cx["n"] % 2
        st = att_st[p]
        cfg = cx["cfg"]
        pnb = cfg["pnb"][p]
        trb = cfg["trb"][p]
        nt, wk, w0, wb = cx["nt"], cx["wk"], cx["w0"], cx["wb"]
        na = SS(lambda s: s.activation(out=pnb[:, 0:w0], in_=cfg["ea"][p][:, 0:w0],
                                       func=ACT.Copy, scale=RINV[:, p:p + 1]),
                deps=[cx["rcp"], st["tr"]])
        nrm = na
        if wb > 0:
            nrm = SS(lambda s: s.activation(out=pnb[:, w0:wk],
                                            in_=cfg["eb"][p][:, 0:wb],
                                            func=ACT.Copy, scale=RINV[:, p:p + 1]))
        st["nrm"] = nrm
        tts = []
        for d_ in range(nt):
            t = TT(lambda t_, d_=d_: t_.transpose(
                out=trb[:, d_ * 128:(d_ + 1) * 128],
                in_=pnb[:, d_ * 128:(d_ + 1) * 128], identity=IDB[:, :]),
                deps=[nrm, st["pte"]] if d_ == 0 else [nrm])
            tts.append(t)
        st["tr"] = tts[-1]
        pte = VV(lambda v: v.tensor_copy(out=cfg["ptb"][p][:, 0:wk],
                                         in_=trb[:, 0:wk]),
                 deps=tts + [st["avm"]])
        st["pte"] = pte
        cx["pte"] = pte

    def attn_s4(cx):
        p = cx["n"] % 2
        st = att_st[p]
        cfg = cx["cfg"]
        ptb = cfg["ptb"][p]
        avp = cfg["avs"][p]
        avm = None
        for d_ in range(cx["nt"]):
            avm = TT(lambda t_, d_=d_: t_.matmul(
                out=avp, lhsT=cx["vfn"](d_), rhs=ptb[:, d_ * 128:(d_ + 1) * 128],
                start=(d_ == 0), stop=(d_ == cx["nt"] - 1)),
                deps=[cx["pte"], st["oev"]] if d_ == 0 else [])
        st["avm"] = avm
        oev = VV(lambda v: v.tensor_copy(out=cx["oap"], in_=avp), deps=[avm])
        st["oev"] = oev

    def attn_drive(tiles):
        ctxs = []
        N = len(tiles)
        for i in range(N + 4):
            if i >= 4:
                attn_s4(ctxs[i - 4])
            if 3 <= i < N + 3:
                attn_s3(ctxs[i - 3])
            if 2 <= i < N + 2:
                attn_s2b(ctxs[i - 2])
            if 1 <= i < N + 1:
                attn_s2a(ctxs[i - 1])
            if i < N:
                ctxs.append(attn_s1(*tiles[i]))

    xln_ap = lambda dc, c0, w: AR1B[:, dc * 3584 + c0: dc * 3584 + c0 + w]
    xbuf = [AR1[:, 14336:17920], AR1[:, 17920:21504]]

    def srcA_final(dc):
        par = dc % 2
        tk = P.emit("gpsimd", lambda g, dc=dc, par=par: g.dma_start(
            out=xbuf[par][:, :], in_=h_xin[dc * 128:(dc + 1) * 128, :]),
            deps=[src_cons[par]], sem="xs" + str(par))
        return xbuf[par], tk

    stok = ln_stats(srcA_final, NTOT, AR2[:, 0:3584], AR2[:, 3584:7168])
    P.barrier()
    MB = AR2[:, 0:3584]
    RB = AR2[:, 3584:7168]
    for tt in range(7):
        bcast_stats(tt * 512, 512, stok, dst1=MB[:, tt * 512:(tt + 1) * 512],
                    dst2=RB[:, tt * 512:(tt + 1) * 512])
    P.barrier()
    for dc in range(NCH):
        ap, tk = srcA_final(dc)
        q0 = (dc % 2) * 512
        for tt in range(7):
            c0 = tt * 512
            VV(lambda v, ap=ap, c0=c0, q0=q0: v.tensor_tensor(
                out=QS3[:, q0:q0 + 512], in0=ap[:, c0:c0 + 512],
                in1=MB[:, c0:c0 + 512], op=ALU.add), deps=[tk])
            lastap = VV(lambda v, dc=dc, c0=c0, q0=q0: v.tensor_tensor(
                out=xln_ap(dc, c0, 512), in0=QS3[:, q0:q0 + 512],
                in1=RB[:, c0:c0 + 512], op=ALU.mult))
        src_cons[dc % 2] = lastap
    P.barrier()

    def o_ap(ch):
        if ch < 6:
            return AR1B[:, 28672 + ch * 3072: 28672 + (ch + 1) * 3072]
        return AR2B[:, 13312 + (ch - 6) * 3072: 13312 + (ch - 5) * 3072]

    v_ap = lambda kt: AR3B[:, kt * 256:(kt + 1) * 256]

    for g in range(4):
        if g > 0:
            P.barrier()
        cw = load_wslice(lambda g=g: h_w["q0"][:, g * 256:(g + 1) * 256].rearrange(
            "(c p) m -> p c m", p=128), g % 2, 2048)
        for cc in range(2):
            for tt in range(6):
                ps = (pA, pB, pC, pD)[(cc * 6 + tt) % 4]
                for dc in range(NCH):
                    t = TT(lambda t_, dc=dc, cc=cc, tt=tt, ps=ps: t_.matmul(
                        out=ps[:, 0:512],
                        lhsT=WBF[:, dc * 256 + cc * 128: dc * 256 + cc * 128 + 128],
                        rhs=xln_ap(dc, 512 + tt * 512, 512),
                        start=(dc == 0), stop=(dc == NCH - 1)),
                        deps=[cw, pfree[id(ps)]] if dc == 0 else [])
                ev = SS(lambda s, cc=cc, tt=tt, ps=ps: s.activation(
                    out=AR2B[:, cc * 3072 + tt * 512: cc * 3072 + (tt + 1) * 512],
                    in_=ps[:, 0:512], func=ACT.Copy, scale=0.125), deps=[t])
                pfree[id(ps)] = ev
        wbf_last[0] = wbf_last[1] = SSNAP()
        wbf_tlast[0] = wbf_tlast[1] = TSNAP()
        cw = load_wslice(lambda g=g: h_w["k0"][:, g * 256:(g + 1) * 256].rearrange(
            "(c p) m -> p c m", p=128), (g + 1) % 2, 2048)
        for cc in range(2):
            for tt in range(7):
                ps = (pA, pB, pC, pD)[(cc * 7 + tt) % 4]
                for dc in range(NCH):
                    t = TT(lambda t_, dc=dc, cc=cc, tt=tt, ps=ps: t_.matmul(
                        out=ps[:, 0:512],
                        lhsT=WBF[:, dc * 256 + cc * 128: dc * 256 + cc * 128 + 128],
                        rhs=xln_ap(dc, tt * 512, 512),
                        start=(dc == 0), stop=(dc == NCH - 1)),
                        deps=[cw, pfree[id(ps)]] if dc == 0 else [])
                ev = SS(lambda s, cc=cc, tt=tt, ps=ps: s.copy(
                    out=AR2B[:, 6144 + cc * 3584 + tt * 512:
                             6144 + cc * 3584 + (tt + 1) * 512],
                    in_=ps[:, 0:512]), deps=[t])
                pfree[id(ps)] = ev
        wbf_last[0] = wbf_last[1] = SSNAP()
        wbf_tlast[0] = wbf_tlast[1] = TSNAP()
        cw = load_wslice(lambda g=g: h_w["v0"][:, g * 256:(g + 1) * 256].rearrange(
            "(c p) m -> p c m", p=128), g % 2, 2048)
        for kt in range(28):
            ps = (pA, pB, pC, pD)[kt % 4]
            for dc in range(NCH):
                t = TT(lambda t_, dc=dc, kt=kt, ps=ps: t_.matmul(
                    out=ps[:, 0:256], lhsT=xln_ap(dc, kt * 128, 128),
                    rhs=WBF[:, dc * 256:(dc + 1) * 256],
                    start=(dc == 0), stop=(dc == NCH - 1)),
                    deps=[cw, pfree[id(ps)]] if dc == 0 else [])
            ev = SS(lambda s, kt=kt, ps=ps: s.copy(out=v_ap(kt), in_=ps[:, 0:256]),
                    deps=[t])
            pfree[id(ps)] = ev
        wbf_last[0] = wbf_last[1] = SSNAP()
        wbf_tlast[0] = wbf_tlast[1] = TSNAP()

        proj_done = [TSNAP(), SSNAP()]
        attn_reset()
        BCMB = AR2B[:, 19456:21504]
        cfgL0 = dict(sA=(pC, pE), sB=(pA, pB),
                     ea=(QS3[:, 0:512], QS1[:, 0:512]),
                     eb=(QS3[:, 512:1024], QS2[:, 0:512]),
                     pnb=(PNB[:, 0:640], AR2B[:, 21504:22144]),
                     ptb=(PTB[:, 0:640], AR2B[:, 22144:22784]),
                     trb=(pFB[:, 0:1024], pDB[:, 0:1024]),
                     avs=(pH[0:64, 0:128], pH[64:128, 0:128]))
        tiles = []
        ntile = 0
        for hl in range(4):
            cc = hl // 2
            p0 = 64 * (hl % 2)
            ch = 2 * g + cc
            qh = lambda c0, w, cc=cc, p0=p0: AR2B[p0:p0 + 64,
                                                  cc * 3072 + c0: cc * 3072 + c0 + w]
            kh = lambda c0, w, cc=cc, p0=p0: AR2B[p0:p0 + 64,
                                                  6144 + cc * 3584 + c0:
                                                  6144 + cc * 3584 + c0 + w]
            for qt in range(16):
                ma = BCMB[:, qt * 512:(qt + 1) * 512] if qt <= 3 else MQ[:, 512:1024]
                tiles.append((ntile, qh(qt * 128, 128),
                              lambda c0, w, qt=qt, kh=kh: kh(qt * 128 + c0, w),
                              5, ma, MQ[:, 1024:1152],
                              lambda d, qt=qt, hl=hl: v_ap(qt + d)[:, hl * 64:(hl + 1) * 64],
                              o_ap(ch)[p0:p0 + 64, qt * 128:(qt + 1) * 128], cfgL0))
                ntile += 1
            for qt in range(8):
                nt = min(qt, 4) + 1
                j0 = max(0, qt - 4)
                wk = nt * 128
                if nt == 5:
                    ma, mb = MQ[:, 512:1024], MQ[:, 1024:1152]
                else:
                    ma, mb = MQ[:, 1152 - wk:1152], None
                tiles.append((ntile, qh(2048 + qt * 128, 128),
                              lambda c0, w, j0=j0, kh=kh: kh(2560 + j0 * 128 + c0, w),
                              nt, ma, mb,
                              lambda d, j0=j0, hl=hl: v_ap(20 + j0 + d)[:, hl * 64:(hl + 1) * 64],
                              o_ap(ch)[p0:p0 + 64, 2048 + qt * 128: 2048 + (qt + 1) * 128],
                              cfgL0))
                ntile += 1
        for p_ in (0, 1):
            att_st[p_]["exp"] = proj_done[1]
            att_st[p_]["rmx"] = proj_done[0]
        attn_drive(tiles)
    P.barrier()

    def o2_ap(ch):
        if ch < 4:
            return AR2B[:, ch * 3072:(ch + 1) * 3072]
        if ch < 6:
            return AR2B[:, 19456 + (ch - 4) * 3072: 19456 + (ch - 3) * 3072]
        return AR2B[:, 13312 + (ch - 6) * 3072: 13312 + (ch - 5) * 3072]

    def wo2_ap(dk):
        if dk < 3:
            return AR2B[:, 25600 + dk * 1024: 25600 + (dk + 1) * 1024]
        return AR3B[:, (dk - 3) * 1024: (dk - 2) * 1024]

    for ch in range(6):
        VV(lambda v, ch=ch: v.tensor_copy(out=o2_ap(ch), in_=o_ap(ch)))
    omove = VSNAP()
    for i in range(4):
        par = i % 2
        d = P.emit("sync", lambda y, i=i, par=par: y.dma_start(
            out=WS[par][:, 0:2048],
            in_=h_w["o0"][256 * i:256 * (i + 1), :].rearrange(
                "(c p) m -> p c m", p=128)), deps=[ws_last[par], omove],
            sem="wy" + str(par))
        c1 = SS(lambda s, i=i, par=par: s.copy(out=wo2_ap(2 * i),
                                               in_=WS[par][:, 0:1024]), deps=[d])
        c2 = SS(lambda s, i=i, par=par: s.copy(out=wo2_ap(2 * i + 1),
                                               in_=WS[par][:, 1024:2048]), deps=[d])
        ws_last[par] = c2
    P.barrier()
    xr = [QS1, QS2]
    xr_last = [None, None]
    for tt in range(6):
        for oc in range(NCH):
            ps = (pA, pB, pC, pD)[(tt * 8 + oc) % 4]
            par = oc % 2
            dx = P.emit("gpsimd", lambda g, oc=oc, tt=tt, par=par: g.dma_start(
                out=xr[par][:, 0:512],
                in_=h_xin[oc * 128:(oc + 1) * 128,
                          512 + tt * 512: 512 + (tt + 1) * 512]),
                deps=[xr_last[par]], sem="xr" + str(par))
            for dk in range(NCH):
                t = TT(lambda t_, dk=dk, oc=oc, tt=tt, ps=ps: t_.matmul(
                    out=ps[:, 0:512], lhsT=wo2_ap(dk)[:, oc * 128:(oc + 1) * 128],
                    rhs=o2_ap(dk)[:, tt * 512:(tt + 1) * 512],
                    start=(dk == 0), stop=(dk == NCH - 1)),
                    deps=[pfree[id(ps)]] if dk == 0 else [])
            av = VV(lambda v, oc=oc, tt=tt, ps=ps, par=par: v.tensor_tensor(
                out=AR1[:, oc * 3072 + tt * 512: oc * 3072 + (tt + 1) * 512],
                in0=ps[:, 0:512], in1=xr[par][:, 0:512], op=ALU.add),
                deps=[t, dx])
            pfree[id(ps)] = av
            xr_last[par] = av
    P.barrier()
    if dbg == "x1":
        for dc in range(NCH):
            DG(lambda g, dc=dc: g.dma_start(
                out=h_dbg[dc * 128:(dc + 1) * 128, :],
                in_=AR1[:, dc * 3072:(dc + 1) * 3072]))

    stok = ln_stats(lambda dc: (AR1[:, dc * 3072:(dc + 1) * 3072], None), NOE,
                    AR2[:, 0:3072], AR2[:, 3072:6144])
    P.barrier()
    xq8_ap = lambda dc, c0, w: AR2B[:, dc * 3072 + c0: dc * 3072 + c0 + w]
    for tt in range(6):
        c0 = tt * 512
        btok = bcast_stats(c0, 512, stok)
        for dc in range(NCH):
            VV(lambda v, dc=dc, c0=c0: v.tensor_tensor(
                out=QS3[:, 0:512], in0=AR1[:, dc * 3072 + c0: dc * 3072 + c0 + 512],
                in1=QS1[:, 0:512], op=ALU.add), deps=btok)
            VV(lambda v: v.tensor_tensor(out=QS3[:, 0:512], in0=QS3[:, 0:512],
                                         in1=QS2[:, 0:512], op=ALU.mult))
            VV(lambda v: v.tensor_copy(out=Q8[:, 0:512], in_=QS3[:, 0:512]))
            VV(lambda v: v.tensor_copy(out=I8T[:, 0:512], in_=Q8[:, 0:512].bitcast(I8)))
            VV(lambda v, dc=dc, c0=c0: v.tensor_copy(
                out=xq8_ap(dc, c0, 512), in_=I8T[:, 0:512].bitcast(F8)))
    P.barrier()
    vlast_d = None
    for oc in range(NCH):
        par = oc % 2
        d = P.emit("sync", lambda y, oc=oc, par=par: y.dma_start(
            out=WS[par][:, 0:1024],
            in_=h_w["f0"][:, oc * 128:(oc + 1) * 128].rearrange(
                "(c p) m -> p c m", p=128)), deps=[ws_last[par]],
            sem="wy" + str(par))
        q1 = SS(lambda s, par=par: s.copy(out=Q8[:, 0:1024], in_=WS[par][:, 0:1024]),
                deps=[d, vlast_d, VSNAP()])
        q2 = SS(lambda s: s.copy(out=I8T[:, 0:1024], in_=Q8[:, 0:1024].bitcast(I8)))
        q3 = SS(lambda s: s.copy(out=WBF[:, 0:1024], in_=I8T[:, 0:1024].bitcast(F8)),
                deps=[wbf_last[0], wbf_tlast[0]])
        ws_last[par] = q3
        for tt in range(6):
            ps = (pA, pB, pC, pD)[(oc * 6 + tt) % 4]
            for dc in range(NCH):
                t = TT(lambda t_, dc=dc, tt=tt, ps=ps: t_.matmul(
                    out=ps[:, 0:512], lhsT=WBF[:, dc * 128:(dc + 1) * 128],
                    rhs=xq8_ap(dc, tt * 512, 512),
                    start=(dc == 0), stop=(dc == NCH - 1)),
                    deps=[q3, pfree[id(ps)]] if dc == 0 else [])
            f1 = SS(lambda s, ps=ps: s.copy(out=F16T[:, 0:512], in_=ps[:, 0:512]),
                    deps=[t, vlast_d])
            f2 = SS(lambda s: s.copy(out=I16T[:, 0:512],
                                     in_=F16T[:, 0:512].bitcast(I16)))
            pfree[id(ps)] = f2
            vlast_d = VV(lambda v, oc=oc, tt=tt: v.scalar_tensor_tensor(
                out=AR1[:, oc * 3072 + tt * 512: oc * 3072 + (tt + 1) * 512],
                in0=I16T[:, 0:512].bitcast(F16),
                scalar=B0[:, oc:oc + 1],
                in1=AR1[:, oc * 3072 + tt * 512: oc * 3072 + (tt + 1) * 512],
                op0=ALU.add, op1=ALU.add), deps=[f2])
        wbf_last[0] = TSNAP()
    P.barrier()
    if dbg == "x2":
        for dc in range(NCH):
            DG(lambda g, dc=dc: g.dma_start(
                out=h_dbg[dc * 128:(dc + 1) * 128, :],
                in_=AR1[:, dc * 3072:(dc + 1) * 3072]))

    stok = ln_stats(lambda dc: (AR1[:, dc * 3072 + 2048:(dc + 1) * 3072], None),
                    NEXT_, AR2[:, 8192:9216], AR2[:, 9216:10240])
    P.barrier()
    xln2e = lambda dc, c0, w: AR2B[:, dc * 1024 + c0: dc * 1024 + c0 + w]
    for tt in range(2):
        c0 = tt * 512
        btok = bcast_stats(c0, 512, stok)
        for dc in range(NCH):
            VV(lambda v, dc=dc, c0=c0: v.tensor_tensor(
                out=QS3[:, 0:512],
                in0=AR1[:, dc * 3072 + 2048 + c0: dc * 3072 + 2048 + c0 + 512],
                in1=QS1[:, 0:512], op=ALU.add), deps=btok)
            VV(lambda v, dc=dc, c0=c0: v.tensor_tensor(
                out=xln2e(dc, c0, 512), in0=QS3[:, 0:512], in1=QS2[:, 0:512],
                op=ALU.mult))
    P.barrier()
    k1_ap = lambda ch, c0, w: AR2B[:, 8192 + ch * 1024 + c0: 8192 + ch * 1024 + c0 + w]
    v1_ap = lambda kt: AR2B[:, 16384 + kt * 1024: 16384 + (kt + 1) * 1024]
    for ch in range(NCH):
        wb0 = (ch % 2) * 1024
        cw = load_wslice(lambda ch=ch: h_w["k1"][:, ch * 128:(ch + 1) * 128].rearrange(
            "(c p) m -> p c m", p=128), ch % 2, 1024, wb0=wb0)
        for tt in range(2):
            ps = (pA, pB, pC, pD)[(ch * 2 + tt) % 4]
            for dc in range(NCH):
                t = TT(lambda t_, dc=dc, tt=tt, ps=ps, wb0=wb0: t_.matmul(
                    out=ps[:, 0:512], lhsT=WBF[:, wb0 + dc * 128: wb0 + (dc + 1) * 128],
                    rhs=xln2e(dc, tt * 512, 512),
                    start=(dc == 0), stop=(dc == NCH - 1)),
                    deps=[cw, pfree[id(ps)]] if dc == 0 else [])
            ev = SS(lambda s, ch=ch, tt=tt, ps=ps: s.copy(
                out=k1_ap(ch, tt * 512, 512), in_=ps[:, 0:512]), deps=[t])
            pfree[id(ps)] = ev
        wbf_last[ch % 2] = SSNAP()
        wbf_tlast[ch % 2] = TSNAP()
    for vg in range(4):
        cw = load_wslice(lambda vg=vg: h_w["v1"][:, vg * 256:(vg + 1) * 256].rearrange(
            "(c p) m -> p c m", p=128), vg % 2, 2048)
        for kt in range(8):
            ps = (pA, pB, pC, pD)[(vg * 8 + kt) % 4]
            for dc in range(NCH):
                t = TT(lambda t_, dc=dc, kt=kt, ps=ps: t_.matmul(
                    out=ps[:, 0:256], lhsT=xln2e(dc, kt * 128, 128),
                    rhs=WBF[:, dc * 256:(dc + 1) * 256],
                    start=(dc == 0), stop=(dc == NCH - 1)),
                    deps=[cw, pfree[id(ps)]] if dc == 0 else [])
            ev = SS(lambda s, kt=kt, vg=vg, ps=ps: s.copy(
                out=v1_ap(kt)[:, vg * 256:(vg + 1) * 256], in_=ps[:, 0:256]),
                deps=[t])
            pfree[id(ps)] = ev
        wbf_last[0] = wbf_last[1] = SSNAP()
        wbf_tlast[0] = wbf_tlast[1] = TSNAP()
    P.barrier()
    stok = ln_stats(lambda dc: (AR1[:, dc * 3072: dc * 3072 + 2048], None), NOWN,
                    AR2[:, 0:2048], AR2[:, 2048:4096], rows=WS[0])
    P.barrier()

    def xo_ap(dc, c0, w):
        if dc < 4:
            return AR2B[:, dc * 2048 + c0: dc * 2048 + c0 + w]
        if dc < 6:
            return AR2B[:, 24576 + (dc - 4) * 2048 + c0:
                        24576 + (dc - 4) * 2048 + c0 + w]
        return AR3B[:, (dc - 6) * 2048 + c0: (dc - 6) * 2048 + c0 + w]

    for tt in range(4):
        c0 = tt * 512
        btok = bcast_stats(c0, 512, stok, rows=WS[0])
        for dc in range(NCH):
            VV(lambda v, dc=dc, c0=c0: v.tensor_tensor(
                out=QS3[:, 0:512], in0=AR1[:, dc * 3072 + c0: dc * 3072 + c0 + 512],
                in1=QS1[:, 0:512], op=ALU.add), deps=btok)
            VV(lambda v, dc=dc, c0=c0: v.tensor_tensor(
                out=xo_ap(dc, c0, 512), in0=QS3[:, 0:512], in1=QS2[:, 0:512],
                op=ALU.mult))
    ws_last[0] = TSNAP()
    P.barrier()
    q1b_ap = lambda ch, c0, w: AR1B[:, (ch * 3072 + 2048) * 2 + c0:
                                    (ch * 3072 + 2048) * 2 + c0 + w]
    for ch in range(NCH):
        _oc4 = ch
        wb0 = (ch % 2) * 1024
        cw = load_wslice(lambda ch=ch: h_w["q1"][:, ch * 128:(ch + 1) * 128].rearrange(
            "(c p) m -> p c m", p=128), ch % 2, 1024, wb0=wb0)
        for tt in range(4):
            ps = (pA, pB, pC, pD)[(_oc4 * 4 + tt) % 4]
            for dc in range(NCH):
                t = TT(lambda t_, dc=dc, tt=tt, ps=ps, wb0=wb0: t_.matmul(
                    out=ps[:, 0:512], lhsT=WBF[:, wb0 + dc * 128: wb0 + (dc + 1) * 128],
                    rhs=xo_ap(dc, tt * 512, 512),
                    start=(dc == 0), stop=(dc == NCH - 1)),
                    deps=[cw, pfree[id(ps)]] if dc == 0 else [])
            ev = SS(lambda s, ch=ch, tt=tt, ps=ps: s.activation(
                out=q1b_ap(ch, tt * 512, 512), in_=ps[:, 0:512], func=ACT.Copy,
                scale=0.125), deps=[t])
            pfree[id(ps)] = ev
        wbf_last[ch % 2] = SSNAP()
        wbf_tlast[ch % 2] = TSNAP()
    P.barrier()
    attn_reset()
    cfgL1 = dict(sA=(pC, pA), sB=(pD, pB),
                 ea=(QS3[:, 0:512], QS1[:, 0:512]),
                 eb=(QS3[:, 512:1024], QS2[:, 0:512]),
                 pnb=(PNB[:, 0:1024], AR3B[:, 4096:5120]),
                 ptb=(PTB[:, 0:1024], AR3B[:, 5120:6144]),
                 trb=(pFB[:, 0:1024], pEB[:, 0:1024]),
                 avs=(pH[0:64, 0:128], pH[64:128, 0:128]))
    tiles = []
    ntile = 0
    for h in range(16):
        ch = h // 2
        p0 = 64 * (h % 2)
        for qt in range(16):
            nt = (qt % 8) + 1
            wk = nt * 128
            if wk <= 512:
                ma, mb = MQ[:, 1152 - wk:1152], None
            else:
                ma, mb = None, MQ[:, 1152 - (wk - 512):1152]
            tiles.append((ntile,
                          q1b_ap(ch, qt * 128, 128)[p0:p0 + 64, :],
                          lambda c0, w, ch=ch, p0=p0: k1_ap(ch, c0, w)[p0:p0 + 64, :],
                          nt, ma, mb,
                          lambda d, h=h: v1_ap(d)[:, h * 64:(h + 1) * 64],
                          xo_ap(ch, qt * 128, 128)[p0:p0 + 64, :], cfgL1))
            ntile += 1
    attn_drive(tiles)
    P.barrier()
    for oc in range(NCH):
        _oc4 = oc
        wb0 = (oc % 2) * 1024
        cw = load_wslice(lambda oc=oc: h_w["o1"][:, oc * 128:(oc + 1) * 128].rearrange(
            "(c p) m -> p c m", p=128), oc % 2, 1024, wb0=wb0)
        for tt in range(4):
            ps = (pA, pB, pC, pD)[(_oc4 * 4 + tt) % 4]
            for dk in range(NCH):
                t = TT(lambda t_, dk=dk, tt=tt, ps=ps, wb0=wb0: t_.matmul(
                    out=ps[:, 0:512], lhsT=WBF[:, wb0 + dk * 128: wb0 + (dk + 1) * 128],
                    rhs=xo_ap(dk, tt * 512, 512),
                    start=(dk == 0), stop=(dk == NCH - 1)),
                    deps=[cw, pfree[id(ps)]] if dk == 0 else [])
            av = VV(lambda v, oc=oc, tt=tt, ps=ps: v.tensor_tensor(
                out=AR1[:, oc * 3072 + tt * 512: oc * 3072 + (tt + 1) * 512],
                in0=AR1[:, oc * 3072 + tt * 512: oc * 3072 + (tt + 1) * 512],
                in1=ps[:, 0:512], op=ALU.add), deps=[t])
            pfree[id(ps)] = av
        wbf_last[oc % 2] = VSNAP()
        wbf_tlast[oc % 2] = TSNAP()
    P.barrier()
    if dbg == "x3":
        for dc in range(NCH):
            DG(lambda g, dc=dc: g.dma_start(
                out=h_dbg[dc * 128:(dc + 1) * 128, :],
                in_=AR1[:, dc * 3072:(dc + 1) * 3072]))

    stok = ln_stats(lambda dc: (AR1[:, dc * 3072: dc * 3072 + 2048], None), NOWN,
                    AR2[:, 8192:10240], AR2[:, 10240:12288])
    P.barrier()
    xln3 = lambda dc, c0, w: AR2B[:, dc * 2048 + c0: dc * 2048 + c0 + w]
    for tt in range(4):
        c0 = tt * 512
        btok = bcast_stats(c0, 512, stok)
        for dc in range(NCH):
            VV(lambda v, dc=dc, c0=c0: v.tensor_tensor(
                out=QS3[:, 0:512], in0=AR1[:, dc * 3072 + c0: dc * 3072 + c0 + 512],
                in1=QS1[:, 0:512], op=ALU.add), deps=btok)
            VV(lambda v, dc=dc, c0=c0: v.tensor_tensor(
                out=xln3(dc, c0, 512), in0=QS3[:, 0:512], in1=QS2[:, 0:512],
                op=ALU.mult))
    P.barrier()
    dwg = P.emit("sync", lambda y: y.dma_start(
        out=WS[0][:, 0:64], in_=h_wg[:, :].rearrange("(c p) m -> p c m", p=128)),
        deps=[ws_last[0]], sem="wy0")
    cwg = SS(lambda s: s.copy(out=WBF[:, 0:64], in_=WS[0][:, 0:64]),
             deps=[dwg, wbf_last[0], wbf_tlast[0]])
    ws_last[0] = cwg
    for tt in range(4):
        for dc in range(NCH):
            t = TT(lambda t_, dc=dc, tt=tt: t_.matmul(
                out=pC[0:8, 0:512], lhsT=WBF[:, dc * 8:(dc + 1) * 8],
                rhs=xln3(dc, tt * 512, 512),
                start=(dc == 0), stop=(dc == NCH - 1)),
                deps=[cwg, pfree[id(pC)]] if dc == 0 else [])
        ev = VV(lambda v, tt=tt: v.tensor_scalar(
            out=AR3[32:40, tt * 512:(tt + 1) * 512], in0=pC[0:8, 0:512],
            scalar1=BG[0:8, 0:1], scalar2=None, op0=ALU.add), deps=[t])
        pfree[id(pC)] = ev
    lg_done = SSNAP()
    for ts in range(16):
        t = TT(lambda t_, ts=ts: t_.transpose(
            out=pG[:, 0:8], in_=AR3[32:40, ts * 128:(ts + 1) * 128],
            identity=IDF[32:40, 32:40]), deps=[lg_done, pfree[id(pG)]])
        ev = SS(lambda s, ts=ts: s.copy(out=QS3[:, 128 + ts * 8: 128 + (ts + 1) * 8],
                                        in_=pG[:, 0:8]), deps=[t])
        pfree[id(pG)] = ev
        r1 = VV(lambda v, ts=ts: v.tensor_reduce(
            out=RMAX[:, 0:1], in_=QS3[:, 128 + ts * 8: 128 + (ts + 1) * 8], axis=AXX,
            op=ALU.max, negate=True), deps=[ev, SSNAP()])
        r2 = SS(lambda s, ts=ts: s.activation(
            out=QS3[:, ts * 8:(ts + 1) * 8], in_=QS3[:, 128 + ts * 8: 128 + (ts + 1) * 8],
            func=ACT.Exp, bias=RMAX[:, 0:1], accum_out=RSUM[:, 0:1]), deps=[r1])
        VV(lambda v, ts=ts: v.reciprocal(out=QS3[:, 272 + ts: 273 + ts], in_=RSUM[:, 0:1]),
           deps=[r2])
    sgate = SSNAP()
    VV(lambda v: v.tensor_scalar(out=QS3[:, 0:128], in0=QS3[:, 0:128], scalar1=1.0,
                                 scalar2=None, op0=ALU.is_ge), deps=[sgate])
    VV(lambda v: v.tensor_copy(out=QS3[:, 256:272],
                               in_=bass.AP(QS3, 0, [[1024, 128], [8, 16]])))
    for e in range(1, 8):
        ge = bass.AP(QS3, e, [[1024, 128], [8, 16]])
        VV(lambda v: v.tensor_scalar(out=QS1[:, 0:16], in0=QS3[:, 256:272],
                                     scalar1=-1.0, scalar2=1.0,
                                     op0=ALU.mult, op1=ALU.add))
        VV(lambda v, ge=ge: v.tensor_tensor(out=ge, in0=ge, in1=QS1[:, 0:16],
                                            op=ALU.mult))
        VV(lambda v, ge=ge: v.tensor_tensor(out=QS3[:, 256:272], in0=QS3[:, 256:272],
                                            in1=ge, op=ALU.add))
    for e in range(8):
        ge = bass.AP(QS3, e, [[1024, 128], [8, 16]])
        VV(lambda v, ge=ge: v.tensor_tensor(out=ge, in0=ge, in1=QS3[:, 272:288],
                                            op=ALU.mult))
    gt_done = VSNAP()
    P.barrier()

    def acc_ap(ts, c0, w):
        if ts < 6:
            return AR2[:, 8192 + ts * 1024 + c0: 8192 + ts * 1024 + c0 + w]
        if ts < 14:
            base = (ts - 6) * 3072 + 2048
            return AR1[:, base + c0: base + c0 + w]
        return AR3[:, (ts - 14) * 1024 + c0: (ts - 14) * 1024 + c0 + w]

    db2 = P.emit("sync", lambda y: y.dma_start(out=WS[1][0:8, 0:1024], in_=h_b2[:, :]),
                 deps=[ws_last[1]], sem="wy1")
    qs1g_last = None
    for ts in range(16):
        tg = TT(lambda t_, ts=ts: t_.transpose(
            out=pG[0:8, 0:128], in_=QS3[:, ts * 8:(ts + 1) * 8], identity=IDF[:, :]),
            deps=[gt_done, pfree[id(pG)]])
        eg = SS(lambda s: s.copy(out=QS1[0:8, 0:128], in_=pG[0:8, 0:128]),
                deps=[tg, qs1g_last, VSNAP()])
        pfree[id(pG)] = eg
        for oh in range(2):
            ps = pC if oh == 0 else pD
            psx = pE if oh == 0 else pF
            mb = TT(lambda t_, oh=oh, ps=ps: t_.matmul(
                out=ps[:, 0:512], lhsT=QS1[0:8, 0:128],
                rhs=WS[1][0:8, oh * 512:(oh + 1) * 512], start=True, stop=True),
                deps=[eg, db2, pfree[id(ps)]])
            qs1g_last = mb
            trs = []
            for k in range(4):
                dc = oh * 4 + k
                t = TT(lambda t_, dc=dc, ts=ts, k=k, psx=psx: t_.transpose(
                    out=psx[:, k * 128:(k + 1) * 128],
                    in_=AR1[:, dc * 3072 + ts * 128: dc * 3072 + (ts + 1) * 128],
                    identity=IDF[:, :]),
                    deps=[pfree[id(psx)]] if k == 0 else [])
                trs.append(t)
            eb2 = SS(lambda s, ps=ps: s.copy(out=QS3[:, 512:1024], in_=ps[:, 0:512]),
                     deps=[mb, qs1g_last, VSNAP()])
            av = VV(lambda v, ts=ts, oh=oh, psx=psx: v.tensor_tensor(
                out=acc_ap(ts, oh * 512, 512), in0=psx[:, 0:512], in1=QS3[:, 512:1024],
                op=ALU.add), deps=trs + [eb2])
            pfree[id(ps)] = av
            pfree[id(psx)] = av
    P.barrier()

    def w1bf_ap(dc):
        r = dc // 4
        return AR1B[:, r * 6144 + (dc % 4) * 1024: r * 6144 + (dc % 4) * 1024 + 1024]

    def w2bf_ap(ms):
        r = 2 + ms // 4
        return AR1B[:, r * 6144 + (ms % 4) * 1024: r * 6144 + (ms % 4) * 1024 + 1024]

    def h_ap(ms, tq):
        idx = ms * 4 + tq
        r = 4 + idx // 8
        return AR1B[:, r * 6144 + (idx % 8) * 512: r * 6144 + (idx % 8) * 512 + 512]

    w1_last = None
    w2_last = None

    def emit_w1_loads(e, mc):
        toks = []
        for i in range(4):
            par = i % 2
            d = P.emit("sync", lambda y, e=e, mc=mc, i=i, par=par: y.dma_start(
                out=WS[par][:, 0:2048],
                in_=h_w1[e, 256 * i:256 * (i + 1),
                         mc * 1024:(mc + 1) * 1024].rearrange(
                    "(c p) m -> p c m", p=128)),
                deps=[ws_last[par], w1_last], sem="wy" + str(par))
            dst0 = (2 * i // 4) * 6144 + ((2 * i) % 4) * 1024
            c = SS(lambda s, par=par, dst0=dst0: s.copy(
                out=AR1B[:, dst0:dst0 + 2048], in_=WS[par][:, 0:2048]), deps=[d])
            ws_last[par] = c
            toks.append(c)
        return toks

    def emit_w2_loads(e, mc):
        toks = []
        for i in range(4):
            par = i % 2
            d = P.emit("sync", lambda y, e=e, mc=mc, i=i, par=par: y.dma_start(
                out=WS[par][:, 0:2048],
                in_=h_w2[e, mc * 1024 + 256 * i: mc * 1024 + 256 * (i + 1),
                         :].rearrange("(c p) m -> p c m", p=128)),
                deps=[ws_last[par], w2_last], sem="wy" + str(par))
            dst0 = (2 + (2 * i) // 4) * 6144 + ((2 * i) % 4) * 1024
            c = VV(lambda v, par=par, dst0=dst0: v.tensor_copy(
                out=AR1B[:, dst0:dst0 + 2048], in_=WS[par][:, 0:2048]), deps=[d])
            ws_last[par] = c
            toks.append(c)
        return toks

    emcs = [(e, mc) for e in range(8) for mc in range(4)]
    c1toks = emit_w1_loads(0, 0)
    c2toks = emit_w2_loads(0, 0)
    for bi, (e, mc) in enumerate(emcs):
        h_last = None
        for ms in range(8):
            for tq in range(4):
                ps = (pA, pB, pF)[(ms * 4 + tq) % 3]
                dep0 = c1toks + [pfree[id(ps)], w2_last]
                for dc in range(NCH):
                    t = TT(lambda t_, dc=dc, ms=ms, tq=tq, ps=ps: t_.matmul(
                        out=ps[:, 0:512],
                        lhsT=w1bf_ap(dc)[:, ms * 128:(ms + 1) * 128],
                        rhs=xln3(dc, tq * 512, 512),
                        start=(dc == 0), stop=(dc == NCH - 1)),
                        deps=dep0 if dc == 0 else [])
                ev = SS(lambda s, e=e, mc=mc, ms=ms, tq=tq, ps=ps: s.activation(
                    out=h_ap(ms, tq), in_=ps[:, 0:512], func=ACT.Relu,
                    bias=B1[:, e * 32 + mc * 8 + ms: e * 32 + mc * 8 + ms + 1]),
                    deps=[t, w2_last] if (ms == 0 and tq == 0) else [t])
                pfree[id(ps)] = ev
                h_last = ev
        w1_last = h_last
        if bi + 1 < len(emcs):
            c1toks = emit_w1_loads(*emcs[bi + 1])
        psl = [pC, pD, pE]
        for ts in range(16):
            for oh in range(2):
                ps = psl[(ts * 2 + oh) % 3]
                dep0 = c2toks + [pfree[id(ps)], h_last]
                for ms in range(8):
                    t = TT(lambda t_, ms=ms, ts=ts, oh=oh, ps=ps: t_.matmul(
                        out=ps[:, 0:512],
                        lhsT=h_ap(ms, ts // 4)[:, (ts % 4) * 128:
                                               (ts % 4 + 1) * 128],
                        rhs=w2bf_ap(ms)[:, oh * 512:(oh + 1) * 512],
                        start=(ms == 0), stop=(ms == 7)),
                        deps=dep0 if ms == 0 else [])
                w2_last = t
                av = VV(lambda v, e=e, ts=ts, oh=oh, ps=ps: v.scalar_tensor_tensor(
                    out=acc_ap(ts, oh * 512, 512), in0=ps[:, 0:512],
                    scalar=QS3[:, ts * 8 + e: ts * 8 + e + 1],
                    in1=acc_ap(ts, oh * 512, 512),
                    op0=ALU.mult, op1=ALU.add), deps=[t])
                pfree[id(ps)] = av
        if bi + 1 < len(emcs):
            c2toks = emit_w2_loads(*emcs[bi + 1])
    P.barrier()

    for ts in range(16):
        DG(lambda g, ts=ts: g.dma_start(
            out=h_out[ts * 128:(ts + 1) * 128, :], in_=acc_ap(ts, 0, 1024)))
    P.emit("gpsimd", None, deps=[("gpsimd", P.cnt["gpsimd"])], inc=0)

    with nc.Block() as block:
        P.run(block, sems)
    es.close()
    return nc


_NC_CACHE = {}


def _get_nc(dbg=None):
    if dbg not in _NC_CACHE:
        _NC_CACHE[dbg] = build_nc(dbg)
    return _NC_CACHE[dbg]


def make_inputs(inputs):
    f = np.float32
    import ml_dtypes
    x = np.asarray(inputs["L0_attn_x"], dtype=f)
    import ml_dtypes as _mld
    idx = np.arange(128)
    m_su = np.where(idx[None, :] > idx[:, None], 0.0, NEG).astype(f)
    m_li = np.where(idx[None, :] <= idx[:, None], 0.0, NEG).astype(f)
    mask_own = np.concatenate([m_su, np.zeros((128, 384), f), m_li], axis=1)
    mq = np.concatenate([np.zeros((128, 512), f), mask_own], axis=1)
    common = {
        "w_q0": np.ascontiguousarray(inputs["L0_attn_Wq"].T.astype(f)),
        "w_k0": np.ascontiguousarray(inputs["L0_attn_Wk"].T.astype(f)),
        "w_v0": np.ascontiguousarray(inputs["L0_attn_Wv"].T.astype(f)),
        "w_o0": np.ascontiguousarray(inputs["L0_attn_Wo"].T.astype(f)),
        "w_q1": np.ascontiguousarray(inputs["L1_attn_Wq"].T.astype(f)),
        "w_k1": np.ascontiguousarray(inputs["L1_attn_Wk"].T.astype(f)),
        "w_v1": np.ascontiguousarray(inputs["L1_attn_Wv"].T.astype(f)),
        "w_o1": np.ascontiguousarray(inputs["L1_attn_Wo"].T.astype(f)),
        "w_f0": np.ascontiguousarray(inputs["L0_ffn_weight"].T.astype(f)),
        "w_moe1": np.ascontiguousarray(inputs["L1_ffn_W1"].astype(f)),
        "w_moe2": np.ascontiguousarray(inputs["L1_ffn_W2"].astype(f)),
        "w_gate": np.ascontiguousarray(inputs["L1_ffn_w_gate"].astype(f)),
        "b_gate": np.ascontiguousarray(inputs["L1_ffn_b_gate"].astype(f)
                                       .reshape(8, 1)),
        "b_moe1": np.ascontiguousarray(
            inputs["L1_ffn_b1"].astype(f).reshape(8, 32, 128).transpose(2, 0, 1)
            .reshape(128, 256)),
        "b_moe2": np.ascontiguousarray(inputs["L1_ffn_b2"].astype(f)),
        "b_ffn0": np.ascontiguousarray(inputs["L0_ffn_bias"].astype(f)
                                       .reshape(8, 128).T),
        "mq_bf": mq.astype(_mld.bfloat16),
        "ident_f": np.eye(128, dtype=f),
        "ident_b": np.eye(128, dtype=f).astype(ml_dtypes.bfloat16),
        "ones_c": np.ones((128, 1), f),
        "ones_r": np.ones((33, 128), f),
    }
    in_maps = []
    for c in range(8):
        b, hh = c // 2, c % 2
        xb = x[b]
        own = xb[hh * 2048:(hh + 1) * 2048]
        halo = xb[hh * 2048 - 512: hh * 2048] if hh else np.zeros((512, D), f)
        ext = xb[0:1024]
        m = dict(common)
        m["xin"] = np.ascontiguousarray(np.concatenate([halo, own, ext], axis=0).T)
        bcm = np.empty((128, 2048), f)
        for qt in range(4):
            v = mask_own[:, 0:512].copy()
            if hh == 0:
                v[:, 0:512 - 128 * qt] += NEG
            bcm[:, qt * 512:(qt + 1) * 512] = v
        m["bcm_bf"] = np.ascontiguousarray(bcm.astype(_mld.bfloat16))
        in_maps.append(m)
    return in_maps


def kernel(**inputs):
    nc = _get_nc()
    in_maps = make_inputs(inputs)
    res = run_bass_kernel_spmd(nc, in_maps, core_ids=list(range(8)))
    out = np.empty((4, 4096, D), dtype=np.float32)
    for c in range(8):
        b, hh = c // 2, c % 2
        out[b, hh * 2048:(hh + 1) * 2048] = res.results[c]["out"]
    return out


# version 19 (final)
# speedup vs baseline: 1.1510x; 1.0032x over previous
import numpy as np
from contextlib import ExitStack
import concourse.bass as bass
import concourse.mybir as mybir
from concourse.bass_utils import run_bass_kernel_spmd

F32 = mybir.dt.float32
BF16 = mybir.dt.bfloat16
F16 = mybir.dt.float16
F8 = mybir.dt.float8e4
I8 = mybir.dt.int8
I16 = mybir.dt.int16
ACT = mybir.ActivationFunctionType
ALU = mybir.AluOpType
AXX = mybir.AxisListType.X

D = 1024
NCH = 8
NOWN = 2048
NTOT = 3584
NOE = 3072
NEXT_ = 1024
NEG = -1.0e30
EPS = 1e-5


class Prog:
    ENGINES = ("tensor", "vector", "scalar", "sync", "gpsimd")
    VSEMS = ("cst", "xs0", "xs1", "xr0", "xr1", "wy0", "wy1")
    DMAS = ("sync", "gpsimd", "cst", "xs0", "xs1", "xr0", "xr1", "wy0", "wy1")

    def __init__(self):
        self.ops = {e: [] for e in self.ENGINES}
        allsems = self.ENGINES + self.VSEMS
        self.cnt = {e: 0 for e in allsems}
        self.hwm = {e: {x: 0 for x in allsems} for e in self.ENGINES}

    def emit(self, eng, fn, deps=(), selfdep=False, inc=1, sem=None):
        sem = sem or eng
        waits = []
        dd = [d for d in deps if d is not None]
        if selfdep and self.cnt[eng] > 0:
            dd.append((eng, self.cnt[eng]))
        for peng, pval in dd:
            if pval <= 0:
                continue
            if self.hwm[eng][peng] < pval:
                self.hwm[eng][peng] = pval
                waits.append((peng, pval))
        self.ops[eng].append((waits, fn, inc, sem))
        if fn is not None and inc:
            self.cnt[sem] += inc
            return (sem, self.cnt[sem])
        return None

    def barrier(self):
        snap = dict(self.cnt)
        allsems = self.ENGINES + self.VSEMS
        for e in self.ENGINES:
            self.emit(e, None, deps=[(p, snap[p]) for p in allsems if p != e],
                      inc=0)

    def run(self, block, sems):
        mult = {e: (16 if e in self.DMAS else 1) for e in self.ENGINES + self.VSEMS}

        def mk(eng_name):
            def body(eng):
                for waits, fn, inc, sem in self.ops[eng_name]:
                    for peng, pval in waits:
                        eng.wait_ge(sems[peng], pval * mult[peng])
                    if fn is not None:
                        inst = fn(eng)
                        if inc:
                            inst.then_inc(sems[sem], inc * mult[sem])
            return body
        block.tensor(mk("tensor"))
        block.vector(mk("vector"))
        block.scalar(mk("scalar"))
        block.sync(mk("sync"))
        block.gpsimd(mk("gpsimd"))


def build_nc(dbg=None):
    nc = bass.Bass("TRN2", target_bir_lowering=False)
    P = Prog()

    h_xin = nc.dram_tensor("xin", [D, NTOT], F32, kind="ExternalInput").ap()
    h_w = {}
    for nm in ("q0", "k0", "v0", "o0", "q1", "k1", "v1", "o1", "f0"):
        h_w[nm] = nc.dram_tensor("w_" + nm, [D, D], F32, kind="ExternalInput").ap()
    h_w1 = nc.dram_tensor("w_moe1", [8, D, 4096], F32, kind="ExternalInput").ap()
    h_w2 = nc.dram_tensor("w_moe2", [8, 4096, D], F32, kind="ExternalInput").ap()
    h_wg = nc.dram_tensor("w_gate", [D, 8], F32, kind="ExternalInput").ap()
    h_bg = nc.dram_tensor("b_gate", [8, 1], F32, kind="ExternalInput").ap()
    h_b1 = nc.dram_tensor("b_moe1", [128, 256], F32, kind="ExternalInput").ap()
    h_b2 = nc.dram_tensor("b_moe2", [8, D], F32, kind="ExternalInput").ap()
    h_b0 = nc.dram_tensor("b_ffn0", [128, 8], F32, kind="ExternalInput").ap()
    h_mq = nc.dram_tensor("mq_bf", [128, 1152], BF16, kind="ExternalInput").ap()
    h_bcm = nc.dram_tensor("bcm_bf", [128, 2048], BF16, kind="ExternalInput").ap()
    h_idf = nc.dram_tensor("ident_f", [128, 128], F32, kind="ExternalInput").ap()
    h_idb = nc.dram_tensor("ident_b", [128, 128], BF16, kind="ExternalInput").ap()
    h_onc = nc.dram_tensor("ones_c", [128, 1], F32, kind="ExternalInput").ap()
    h_onr = nc.dram_tensor("ones_r", [33, 128], F32, kind="ExternalInput").ap()
    h_out = nc.dram_tensor("out", [NOWN, D], F32, kind="ExternalOutput").ap()
    h_dbg = None
    if dbg is not None:
        h_dbg = nc.dram_tensor("dbgout", [D, NOE], F32, kind="ExternalOutput").ap()

    es = ExitStack()
    sbt = lambda nm, shp, dt: es.enter_context(nc.sbuf_tensor(nm, shp, dt))
    pst = lambda nm, shp: es.enter_context(nc.psum_tensor(nm, shp, F32))

    AR1 = sbt("AR1", [128, 24576], F32)
    AR2 = sbt("AR2", [128, 14336], F32)
    AR3 = sbt("AR3", [128, 3584], F32)
    WS = [sbt("ws0", [128, 2048], F32), sbt("ws1", [128, 2048], F32)]
    WBF = sbt("wbf", [128, 2048], BF16)
    MQ = sbt("mqbf", [128, 1152], BF16)
    IDF = sbt("identf", [128, 128], F32)
    IDB = sbt("identb", [128, 128], BF16)
    ONC = sbt("onesc", [128, 1], F32)
    ONR = sbt("onesr", [33, 128], F32)
    B1 = sbt("b1sb", [128, 256], F32)
    B0 = sbt("b0sb", [128, 8], F32)
    BG = sbt("bgsb", [8, 1], F32)
    QS1 = sbt("qs1", [128, 512], F32)
    QS2 = sbt("qs2", [128, 512], F32)
    QS3 = sbt("qs3", [128, 1024], F32)
    GT = QS3
    PNB = sbt("pnb", [128, 1024], BF16)
    PTB = sbt("ptb", [128, 1024], BF16)
    RMAX = sbt("rmax", [128, 8], F32)
    RSUM = sbt("rsum", [128, 4], F32)
    RINV = sbt("rinv", [128, 2], F32)
    Q8 = sbt("q8t", [128, 1024], F8)
    I8T = sbt("i8t", [128, 1024], I8)
    F16T = sbt("f16t", [128, 512], F16)
    I16T = sbt("i16t", [128, 512], I16)

    AR1B = AR1.bitcast(BF16)
    AR2B = AR2.bitcast(BF16)
    AR3B = AR3.bitcast(BF16)

    pA = pst("pA", [128, 512])
    pB = pst("pB", [128, 512])
    pC = pst("pC", [128, 512])
    pD = pst("pD", [128, 512])
    pE = pst("pE", [128, 512])
    pF = pst("pF", [128, 512])
    pG = pst("pG", [128, 128])
    pH = pst("pH", [128, 128])

    pFB = pF.bitcast(BF16)
    pDB = pD.bitcast(BF16)
    pEB = pE.bitcast(BF16)
    pGB = pG.bitcast(BF16)
    sems = {e: es.enter_context(nc.semaphore("sem_" + e))
            for e in Prog.ENGINES + Prog.VSEMS}

    TT = lambda fn, deps=(): P.emit("tensor", fn, deps=deps)
    VV = lambda fn, deps=(): P.emit("vector", fn, deps=deps, selfdep=True)
    SS = lambda fn, deps=(): P.emit("scalar", fn, deps=deps, selfdep=True)
    DY = lambda fn, deps=(): P.emit("sync", fn, deps=deps)
    DG = lambda fn, deps=(): P.emit("gpsimd", fn, deps=deps)
    TSNAP = lambda: ("tensor", P.cnt["tensor"])
    VSNAP = lambda: ("vector", P.cnt["vector"])
    SSNAP = lambda: ("scalar", P.cnt["scalar"])

    pfree = {id(p): None for p in (pA, pB, pC, pD, pE, pF, pG, pH)}
    ws_last = [None, None]
    wbf_last = [None, None]
    wbf_tlast = [None, None]

    P.emit("gpsimd", lambda g: g.dma_start(out=MQ[:], in_=h_mq[:, :]), sem="cst")
    P.emit("gpsimd", lambda g: g.dma_start(out=AR2.bitcast(BF16)[:, 19456:21504], in_=h_bcm[:, :]), sem="cst")
    P.emit("gpsimd", lambda g: g.dma_start(out=IDF[:], in_=h_idf[:, :]), sem="cst")
    P.emit("gpsimd", lambda g: g.dma_start(out=IDB[:], in_=h_idb[:, :]), sem="cst")
    P.emit("gpsimd", lambda g: g.dma_start(out=ONC[:], in_=h_onc[:, :]), sem="cst")
    P.emit("gpsimd", lambda g: g.dma_start(out=ONR[:], in_=h_onr[:, :]), sem="cst")
    P.emit("gpsimd", lambda g: g.dma_start(out=B1[:], in_=h_b1[:, :]), sem="cst")
    P.emit("gpsimd", lambda g: g.dma_start(out=B0[:], in_=h_b0[:, :]), sem="cst")
    P.emit("gpsimd", lambda g: g.dma_start(out=BG[:], in_=h_bg[:, :]), sem="cst")
    P.barrier()

    src_cons = [None, None]

    def ln_stats(src, n, accx, accsq, rows=None):
        if rows is None:
            rows = AR3
        sq = (QS1[:, 0:512], QS2[:, 0:512])
        sq_cons = [None, None]
        for dc in range(NCH):
            ap, dtok = src(dc)
            if dc == 0:
                VV(lambda v, ap=ap: v.tensor_copy(out=accx[:, 0:n], in_=ap),
                   deps=[dtok])
            else:
                VV(lambda v, ap=ap: v.tensor_tensor(out=accx[:, 0:n], in0=accx[:, 0:n],
                                                    in1=ap, op=ALU.add), deps=[dtok])
            for tt in range((n + 511) // 512):
                c0 = tt * 512
                w = min(512, n - c0)
                qp = (dc + tt) % 2
                ts = SS(lambda s, ap=ap, c0=c0, w=w, qp=qp: s.activation(
                    out=sq[qp][:, 0:w], in_=ap[:, c0:c0 + w], func=ACT.Square),
                    deps=[sq_cons[qp], dtok])
                if dc == 0:
                    av = VV(lambda v, c0=c0, w=w, qp=qp: v.tensor_copy(
                        out=accsq[:, c0:c0 + w], in_=sq[qp][:, 0:w]), deps=[ts])
                else:
                    av = VV(lambda v, c0=c0, w=w, qp=qp: v.tensor_tensor(
                        out=accsq[:, c0:c0 + w], in0=accsq[:, c0:c0 + w],
                        in1=sq[qp][:, 0:w], op=ALU.add), deps=[ts])
                sq_cons[qp] = av
            src_cons[dc % 2] = av
        vacc = VSNAP()
        evs = []
        for tt in range((n + 511) // 512):
            c0 = tt * 512
            w = min(512, n - c0)
            m1 = TT(lambda t, c0=c0, w=w: t.matmul(
                out=pE[0:1, 0:w], lhsT=ONC[:, :], rhs=accx[:, c0:c0 + w],
                start=True, stop=True), deps=[vacc, pfree[id(pE)]])
            e1 = SS(lambda s, c0=c0, w=w: s.copy(out=rows[0:1, c0:c0 + w],
                                                 in_=pE[0:1, 0:w]), deps=[m1])
            pfree[id(pE)] = e1
            m2 = TT(lambda t, c0=c0, w=w: t.matmul(
                out=pF[0:1, 0:w], lhsT=ONC[:, :], rhs=accsq[:, c0:c0 + w],
                start=True, stop=True), deps=[vacc, pfree[id(pF)]])
            e2 = SS(lambda s, c0=c0, w=w: s.copy(out=rows[32:33, c0:c0 + w],
                                                 in_=pF[0:1, 0:w]), deps=[m2])
            pfree[id(pF)] = e2
            evs += [e1, e2]
        return evs + [SSNAP()]

    def bcast_stats(c0, w, stok, rows=None, dst1=None, dst2=None):
        if rows is None:
            rows = AR3
        if dst1 is None:
            dst1 = QS1[:, 0:w]
        if dst2 is None:
            dst2 = QS2[:, 0:w]
        war = [VSNAP(), TSNAP()]
        m1 = TT(lambda t, c0=c0, w=w: t.matmul(
            out=pE[:, 0:w], lhsT=ONR[0:1, :], rhs=rows[0:1, c0:c0 + w],
            start=True, stop=True), deps=list(stok) + [pfree[id(pE)]])
        e1 = SS(lambda s, w=w, dst1=dst1: s.copy(out=dst1, in_=pE[:, 0:w]),
                deps=[m1] + war)
        pfree[id(pE)] = e1
        m2 = TT(lambda t, c0=c0, w=w: t.matmul(
            out=pF[:, 0:w], lhsT=ONR[32:33, :], rhs=rows[32:33, c0:c0 + w],
            start=True, stop=True), deps=list(stok) + [pfree[id(pF)]])
        e2 = SS(lambda s, w=w, dst2=dst2: s.copy(out=dst2, in_=pF[:, 0:w]),
                deps=[m2])
        pfree[id(pF)] = e2
        tmp = QS3[:, 512:512 + w]
        VV(lambda v, dst1=dst1: v.tensor_scalar(out=dst1, in0=dst1,
                                                scalar1=-1.0 / D, scalar2=None,
                                                op0=ALU.mult), deps=[e1])
        VV(lambda v, dst2=dst2: v.tensor_scalar(out=dst2, in0=dst2,
                                                scalar1=1.0 / D, scalar2=EPS,
                                                op0=ALU.mult, op1=ALU.add),
           deps=[e2])
        VV(lambda v, dst1=dst1, tmp=tmp: v.tensor_tensor(out=tmp, in0=dst1,
                                                         in1=dst1, op=ALU.mult))
        tv = VV(lambda v, dst2=dst2, tmp=tmp: v.tensor_tensor(
            out=dst2, in0=dst2, in1=tmp, op=ALU.subtract))
        tq = SS(lambda s, dst2=dst2: s.activation(out=dst2, in_=dst2,
                                                  func=ACT.Sqrt), deps=[tv])
        tr = VV(lambda v, dst2=dst2: v.reciprocal(out=dst2, in_=dst2), deps=[tq])
        return [tr, e1]

    def load_wslice(hfn, par, nf32, prior=(), wb0=0):
        d = P.emit("sync", lambda y, hfn=hfn, par=par, nf32=nf32: y.dma_start(
            out=WS[par][:, 0:nf32], in_=hfn()), deps=[ws_last[par]] + list(prior),
            sem="wy" + str(par))
        if nf32 <= 1024:
            wdeps = [wbf_last[wb0 // 1024], wbf_tlast[wb0 // 1024]]
        else:
            wdeps = [wbf_last[0], wbf_last[1], wbf_tlast[0], wbf_tlast[1]]
        c = SS(lambda s, par=par, nf32=nf32, wb0=wb0: s.copy(
            out=WBF[:, wb0:wb0 + nf32], in_=WS[par][:, 0:nf32]),
            deps=[d] + wdeps)
        ws_last[par] = c
        return c

    att_st = {}

    def attn_reset():
        att_st.clear()
        for p in (0, 1):
            att_st[p] = dict(exp=None, nrm=None, tr=None, pte=None, avm=None,
                             oev=None, rmx=None, rcp=None)

    def attn_s1(n, qap, kap_fn, nt, mask_a, mask_b, v_fn, o_ap, cfg):
        p = n % 2
        st = att_st[p]
        wk = nt * 128
        w0 = min(wk, 512)
        wb = wk - w0
        sA = cfg["sA"][p]
        sB = cfg["sB"][p]
        m1 = TT(lambda t: t.matmul(out=sA[:, 0:w0], lhsT=qap, rhs=kap_fn(0, w0),
                                   start=True, stop=(mask_a is None)),
                deps=[st["exp"], st["rmx"]])
        if mask_a is not None:
            m1 = TT(lambda t: t.matmul(out=sA[:, 0:w0], lhsT=IDB[:, :], rhs=mask_a,
                                       start=False, stop=True))
        m2 = None
        if wb > 0:
            m2 = TT(lambda t: t.matmul(out=sB[:, 0:wb], lhsT=qap, rhs=kap_fn(w0, wb),
                                       start=True, stop=(mask_b is None)))
            if mask_b is not None:
                m2 = TT(lambda t: t.matmul(out=sB[:, 0:wb], lhsT=IDB[:, :],
                                           rhs=mask_b, start=False, stop=True))
        r1 = VV(lambda v: v.tensor_reduce(out=RMAX[:, 4 * p:4 * p + 1],
                                          in_=sA[:, 0:w0], axis=AXX,
                                          op=ALU.max, negate=True),
                deps=[m1, st["exp"]])
        rc = r1
        rcol = 4 * p
        if wb > 0:
            r2 = VV(lambda v: v.tensor_reduce(out=RMAX[:, 4 * p + 1:4 * p + 2],
                                              in_=sB[:, 0:wb], axis=AXX,
                                              op=ALU.max, negate=True), deps=[m2])
            rc = VV(lambda v: v.tensor_tensor(out=RMAX[:, 4 * p + 2:4 * p + 3],
                                              in0=RMAX[:, 4 * p:4 * p + 1],
                                              in1=RMAX[:, 4 * p + 1:4 * p + 2],
                                              op=ALU.min))
            rcol = 4 * p + 2
        st["rmx"] = rc
        return dict(n=n, nt=nt, wk=wk, w0=w0, wb=wb, vfn=v_fn, oap=o_ap, cfg=cfg,
                    rc=rc, rcol=rcol, m1=m1, m2=m2)

    def attn_s2a(cx):
        p = cx["n"] % 2
        st = att_st[p]
        cfg = cx["cfg"]
        e1 = SS(lambda s: s.activation(out=cfg["ea"][p][:, 0:cx["w0"]],
                                       in_=cfg["sA"][p][:, 0:cx["w0"]],
                                       func=ACT.Exp,
                                       bias=RMAX[:, cx["rcol"]:cx["rcol"] + 1],
                                       accum_out=RSUM[:, 2 * p:2 * p + 1]),
                deps=[cx["rc"], st["rcp"], st["nrm"]])
        elast = e1
        if cx["wb"] > 0:
            elast = SS(lambda s: s.activation(out=cfg["eb"][p][:, 0:cx["wb"]],
                                              in_=cfg["sB"][p][:, 0:cx["wb"]],
                                              func=ACT.Exp,
                                              bias=RMAX[:, cx["rcol"]:cx["rcol"] + 1],
                                              accum_out=RSUM[:, 2 * p + 1:2 * p + 2]))
        st["exp"] = elast
        cx["elast"] = elast

    def attn_s2b(cx):
        p = cx["n"] % 2
        st = att_st[p]
        if cx["wb"] > 0:
            sadd = VV(lambda v: v.tensor_tensor(out=RSUM[:, 2 * p:2 * p + 1],
                                                in0=RSUM[:, 2 * p:2 * p + 1],
                                                in1=RSUM[:, 2 * p + 1:2 * p + 2],
                                                op=ALU.add), deps=[cx["elast"]])
        else:
            sadd = cx["elast"]
        rcp = VV(lambda v: v.reciprocal(out=RINV[:, p:p + 1],
                                        in_=RSUM[:, 2 * p:2 * p + 1]), deps=[sadd])
        st["rcp"] = rcp
        cx["rcp"] = rcp

    def attn_s3(cx):
        p = cx["n"] % 2
        st = att_st[p]
        cfg = cx["cfg"]
        pnb = cfg["pnb"][p]
        trb = cfg["trb"][p]
        nt, wk, w0, wb = cx["nt"], cx["wk"], cx["w0"], cx["wb"]
        na = SS(lambda s: s.activation(out=pnb[:, 0:w0], in_=cfg["ea"][p][:, 0:w0],
                                       func=ACT.Copy, scale=RINV[:, p:p + 1]),
                deps=[cx["rcp"], st["tr"]])
        nrm = na
        if wb > 0:
            nrm = SS(lambda s: s.activation(out=pnb[:, w0:wk],
                                            in_=cfg["eb"][p][:, 0:wb],
                                            func=ACT.Copy, scale=RINV[:, p:p + 1]))
        st["nrm"] = nrm
        tts = []
        for d_ in range(nt):
            t = TT(lambda t_, d_=d_: t_.transpose(
                out=trb[:, d_ * 128:(d_ + 1) * 128],
                in_=pnb[:, d_ * 128:(d_ + 1) * 128], identity=IDB[:, :]),
                deps=[nrm, st["pte"]] if d_ == 0 else [nrm])
            tts.append(t)
        st["tr"] = tts[-1]
        pte = VV(lambda v: v.tensor_copy(out=cfg["ptb"][p][:, 0:wk],
                                         in_=trb[:, 0:wk]),
                 deps=tts + [st["avm"]])
        st["pte"] = pte
        cx["pte"] = pte

    def attn_s4(cx):
        p = cx["n"] % 2
        st = att_st[p]
        cfg = cx["cfg"]
        ptb = cfg["ptb"][p]
        avp = cfg["avs"][p]
        avm = None
        for d_ in range(cx["nt"]):
            avm = TT(lambda t_, d_=d_: t_.matmul(
                out=avp, lhsT=cx["vfn"](d_), rhs=ptb[:, d_ * 128:(d_ + 1) * 128],
                start=(d_ == 0), stop=(d_ == cx["nt"] - 1)),
                deps=[cx["pte"], st["oev"]] if d_ == 0 else [])
        st["avm"] = avm
        oev = VV(lambda v: v.tensor_copy(out=cx["oap"], in_=avp), deps=[avm])
        st["oev"] = oev

    def attn_drive(tiles):
        ctxs = []
        N = len(tiles)
        for i in range(N + 4):
            if i >= 4:
                attn_s4(ctxs[i - 4])
            if 3 <= i < N + 3:
                attn_s3(ctxs[i - 3])
            if 2 <= i < N + 2:
                attn_s2b(ctxs[i - 2])
            if 1 <= i < N + 1:
                attn_s2a(ctxs[i - 1])
            if i < N:
                ctxs.append(attn_s1(*tiles[i]))

    xln_ap = lambda dc, c0, w: AR1B[:, dc * 3584 + c0: dc * 3584 + c0 + w]
    xbuf = [AR1[:, 14336:17920], AR1[:, 17920:21504]]

    def srcA_final(dc):
        par = dc % 2
        tk = P.emit("gpsimd", lambda g, dc=dc, par=par: g.dma_start(
            out=xbuf[par][:, :], in_=h_xin[dc * 128:(dc + 1) * 128, :]),
            deps=[src_cons[par]], sem="xs" + str(par))
        return xbuf[par], tk

    stok = ln_stats(srcA_final, NTOT, AR2[:, 0:3584], AR2[:, 3584:7168])
    P.barrier()
    MB = AR2[:, 0:3584]
    RB = AR2[:, 3584:7168]
    for tt in range(7):
        bcast_stats(tt * 512, 512, stok, dst1=MB[:, tt * 512:(tt + 1) * 512],
                    dst2=RB[:, tt * 512:(tt + 1) * 512])
    P.barrier()
    for dc in range(NCH):
        ap, tk = srcA_final(dc)
        q0 = (dc % 2) * 512
        for tt in range(7):
            c0 = tt * 512
            VV(lambda v, ap=ap, c0=c0, q0=q0: v.tensor_tensor(
                out=QS3[:, q0:q0 + 512], in0=ap[:, c0:c0 + 512],
                in1=MB[:, c0:c0 + 512], op=ALU.add), deps=[tk])
            lastap = VV(lambda v, dc=dc, c0=c0, q0=q0: v.tensor_tensor(
                out=xln_ap(dc, c0, 512), in0=QS3[:, q0:q0 + 512],
                in1=RB[:, c0:c0 + 512], op=ALU.mult))
        src_cons[dc % 2] = lastap
    P.barrier()

    def o_ap(ch):
        if ch < 6:
            return AR1B[:, 28672 + ch * 3072: 28672 + (ch + 1) * 3072]
        return AR2B[:, 13312 + (ch - 6) * 3072: 13312 + (ch - 5) * 3072]

    v_ap = lambda kt: AR3B[:, kt * 256:(kt + 1) * 256]

    for g in range(4):
        if g > 0:
            P.barrier()
        cw = load_wslice(lambda g=g: h_w["q0"][:, g * 256:(g + 1) * 256].rearrange(
            "(c p) m -> p c m", p=128), g % 2, 2048)
        for cc in range(2):
            for tt in range(6):
                ps = (pA, pB, pC, pD)[(cc * 6 + tt) % 4]
                for dc in range(NCH):
                    t = TT(lambda t_, dc=dc, cc=cc, tt=tt, ps=ps: t_.matmul(
                        out=ps[:, 0:512],
                        lhsT=WBF[:, dc * 256 + cc * 128: dc * 256 + cc * 128 + 128],
                        rhs=xln_ap(dc, 512 + tt * 512, 512),
                        start=(dc == 0), stop=(dc == NCH - 1)),
                        deps=[cw, pfree[id(ps)]] if dc == 0 else [])
                ev = SS(lambda s, cc=cc, tt=tt, ps=ps: s.activation(
                    out=AR2B[:, cc * 3072 + tt * 512: cc * 3072 + (tt + 1) * 512],
                    in_=ps[:, 0:512], func=ACT.Copy, scale=0.125), deps=[t])
                pfree[id(ps)] = ev
        wbf_last[0] = wbf_last[1] = SSNAP()
        wbf_tlast[0] = wbf_tlast[1] = TSNAP()
        cw = load_wslice(lambda g=g: h_w["k0"][:, g * 256:(g + 1) * 256].rearrange(
            "(c p) m -> p c m", p=128), (g + 1) % 2, 2048)
        for cc in range(2):
            for tt in range(7):
                ps = (pA, pB, pC, pD)[(cc * 7 + tt) % 4]
                for dc in range(NCH):
                    t = TT(lambda t_, dc=dc, cc=cc, tt=tt, ps=ps: t_.matmul(
                        out=ps[:, 0:512],
                        lhsT=WBF[:, dc * 256 + cc * 128: dc * 256 + cc * 128 + 128],
                        rhs=xln_ap(dc, tt * 512, 512),
                        start=(dc == 0), stop=(dc == NCH - 1)),
                        deps=[cw, pfree[id(ps)]] if dc == 0 else [])
                ev = SS(lambda s, cc=cc, tt=tt, ps=ps: s.copy(
                    out=AR2B[:, 6144 + cc * 3584 + tt * 512:
                             6144 + cc * 3584 + (tt + 1) * 512],
                    in_=ps[:, 0:512]), deps=[t])
                pfree[id(ps)] = ev
        wbf_last[0] = wbf_last[1] = SSNAP()
        wbf_tlast[0] = wbf_tlast[1] = TSNAP()
        cw = load_wslice(lambda g=g: h_w["v0"][:, g * 256:(g + 1) * 256].rearrange(
            "(c p) m -> p c m", p=128), g % 2, 2048)
        for kt in range(28):
            ps = (pA, pB, pC, pD)[kt % 4]
            for dc in range(NCH):
                t = TT(lambda t_, dc=dc, kt=kt, ps=ps: t_.matmul(
                    out=ps[:, 0:256], lhsT=xln_ap(dc, kt * 128, 128),
                    rhs=WBF[:, dc * 256:(dc + 1) * 256],
                    start=(dc == 0), stop=(dc == NCH - 1)),
                    deps=[cw, pfree[id(ps)]] if dc == 0 else [])
            ev = SS(lambda s, kt=kt, ps=ps: s.copy(out=v_ap(kt), in_=ps[:, 0:256]),
                    deps=[t])
            pfree[id(ps)] = ev
        wbf_last[0] = wbf_last[1] = SSNAP()
        wbf_tlast[0] = wbf_tlast[1] = TSNAP()

        proj_done = [TSNAP(), SSNAP()]
        attn_reset()
        BCMB = AR2B[:, 19456:21504]
        cfgL0 = dict(sA=(pC, pE), sB=(pA, pB),
                     ea=(QS3[:, 0:512], QS1[:, 0:512]),
                     eb=(QS3[:, 512:1024], QS2[:, 0:512]),
                     pnb=(PNB[:, 0:640], AR2B[:, 21504:22144]),
                     ptb=(PTB[:, 0:640], AR2B[:, 22144:22784]),
                     trb=(pFB[:, 0:1024], pDB[:, 0:1024]),
                     avs=(pH[0:64, 0:128], pH[64:128, 0:128]))
        tiles = []
        ntile = 0
        for hl in range(4):
            cc = hl // 2
            p0 = 64 * (hl % 2)
            ch = 2 * g + cc
            qh = lambda c0, w, cc=cc, p0=p0: AR2B[p0:p0 + 64,
                                                  cc * 3072 + c0: cc * 3072 + c0 + w]
            kh = lambda c0, w, cc=cc, p0=p0: AR2B[p0:p0 + 64,
                                                  6144 + cc * 3584 + c0:
                                                  6144 + cc * 3584 + c0 + w]
            for qt in range(16):
                ma = BCMB[:, qt * 512:(qt + 1) * 512] if qt <= 3 else MQ[:, 512:1024]
                tiles.append((ntile, qh(qt * 128, 128),
                              lambda c0, w, qt=qt, kh=kh: kh(qt * 128 + c0, w),
                              5, ma, MQ[:, 1024:1152],
                              lambda d, qt=qt, hl=hl: v_ap(qt + d)[:, hl * 64:(hl + 1) * 64],
                              o_ap(ch)[p0:p0 + 64, qt * 128:(qt + 1) * 128], cfgL0))
                ntile += 1
            for qt in range(8):
                nt = min(qt, 4) + 1
                j0 = max(0, qt - 4)
                wk = nt * 128
                if nt == 5:
                    ma, mb = MQ[:, 512:1024], MQ[:, 1024:1152]
                else:
                    ma, mb = MQ[:, 1152 - wk:1152], None
                tiles.append((ntile, qh(2048 + qt * 128, 128),
                              lambda c0, w, j0=j0, kh=kh: kh(2560 + j0 * 128 + c0, w),
                              nt, ma, mb,
                              lambda d, j0=j0, hl=hl: v_ap(20 + j0 + d)[:, hl * 64:(hl + 1) * 64],
                              o_ap(ch)[p0:p0 + 64, 2048 + qt * 128: 2048 + (qt + 1) * 128],
                              cfgL0))
                ntile += 1
        for p_ in (0, 1):
            att_st[p_]["exp"] = proj_done[1]
            att_st[p_]["rmx"] = proj_done[0]
        attn_drive(tiles)
    P.barrier()

    def o2_ap(ch):
        if ch < 4:
            return AR2B[:, ch * 3072:(ch + 1) * 3072]
        if ch < 6:
            return AR2B[:, 19456 + (ch - 4) * 3072: 19456 + (ch - 3) * 3072]
        return AR2B[:, 13312 + (ch - 6) * 3072: 13312 + (ch - 5) * 3072]

    def wo2_ap(dk):
        if dk < 3:
            return AR2B[:, 25600 + dk * 1024: 25600 + (dk + 1) * 1024]
        return AR3B[:, (dk - 3) * 1024: (dk - 2) * 1024]

    for ch in range(6):
        VV(lambda v, ch=ch: v.tensor_copy(out=o2_ap(ch), in_=o_ap(ch)))
    omove = VSNAP()
    for i in range(4):
        par = i % 2
        d = P.emit("sync", lambda y, i=i, par=par: y.dma_start(
            out=WS[par][:, 0:2048],
            in_=h_w["o0"][256 * i:256 * (i + 1), :].rearrange(
                "(c p) m -> p c m", p=128)), deps=[ws_last[par], omove],
            sem="wy" + str(par))
        c1 = SS(lambda s, i=i, par=par: s.copy(out=wo2_ap(2 * i),
                                               in_=WS[par][:, 0:1024]), deps=[d])
        c2 = SS(lambda s, i=i, par=par: s.copy(out=wo2_ap(2 * i + 1),
                                               in_=WS[par][:, 1024:2048]), deps=[d])
        ws_last[par] = c2
    P.barrier()
    xr = [QS1, QS2]
    xr_last = [None, None]
    for tt in range(6):
        for oc in range(NCH):
            ps = (pA, pB, pC, pD)[(tt * 8 + oc) % 4]
            par = oc % 2
            dx = P.emit("gpsimd", lambda g, oc=oc, tt=tt, par=par: g.dma_start(
                out=xr[par][:, 0:512],
                in_=h_xin[oc * 128:(oc + 1) * 128,
                          512 + tt * 512: 512 + (tt + 1) * 512]),
                deps=[xr_last[par]], sem="xr" + str(par))
            for dk in range(NCH):
                t = TT(lambda t_, dk=dk, oc=oc, tt=tt, ps=ps: t_.matmul(
                    out=ps[:, 0:512], lhsT=wo2_ap(dk)[:, oc * 128:(oc + 1) * 128],
                    rhs=o2_ap(dk)[:, tt * 512:(tt + 1) * 512],
                    start=(dk == 0), stop=(dk == NCH - 1)),
                    deps=[pfree[id(ps)]] if dk == 0 else [])
            av = VV(lambda v, oc=oc, tt=tt, ps=ps, par=par: v.tensor_tensor(
                out=AR1[:, oc * 3072 + tt * 512: oc * 3072 + (tt + 1) * 512],
                in0=ps[:, 0:512], in1=xr[par][:, 0:512], op=ALU.add),
                deps=[t, dx])
            pfree[id(ps)] = av
            xr_last[par] = av
    P.barrier()
    if dbg == "x1":
        for dc in range(NCH):
            DG(lambda g, dc=dc: g.dma_start(
                out=h_dbg[dc * 128:(dc + 1) * 128, :],
                in_=AR1[:, dc * 3072:(dc + 1) * 3072]))

    stok = ln_stats(lambda dc: (AR1[:, dc * 3072:(dc + 1) * 3072], None), NOE,
                    AR2[:, 0:3072], AR2[:, 3072:6144])
    P.barrier()
    xq8_ap = lambda dc, c0, w: AR2B[:, dc * 3072 + c0: dc * 3072 + c0 + w]
    for tt in range(6):
        c0 = tt * 512
        btok = bcast_stats(c0, 512, stok)
        for dc in range(NCH):
            q0 = (dc % 2) * 512
            VV(lambda v, dc=dc, c0=c0, q0=q0: v.tensor_tensor(
                out=QS3[:, q0:q0 + 512],
                in0=AR1[:, dc * 3072 + c0: dc * 3072 + c0 + 512],
                in1=QS1[:, 0:512], op=ALU.add), deps=btok)
            VV(lambda v, q0=q0: v.tensor_tensor(out=QS3[:, q0:q0 + 512],
                                                in0=QS3[:, q0:q0 + 512],
                                                in1=QS2[:, 0:512], op=ALU.mult))
            VV(lambda v, q0=q0: v.tensor_copy(out=Q8[:, q0:q0 + 512],
                                              in_=QS3[:, q0:q0 + 512]))
            VV(lambda v, q0=q0: v.tensor_copy(out=I8T[:, q0:q0 + 512],
                                              in_=Q8[:, q0:q0 + 512].bitcast(I8)))
            VV(lambda v, dc=dc, c0=c0, q0=q0: v.tensor_copy(
                out=xq8_ap(dc, c0, 512), in_=I8T[:, q0:q0 + 512].bitcast(F8)))
    P.barrier()
    vlast_d = None
    vl2_d = [None, None]
    for oc in range(NCH):
        par = oc % 2
        d = P.emit("sync", lambda y, oc=oc, par=par: y.dma_start(
            out=WS[par][:, 0:1024],
            in_=h_w["f0"][:, oc * 128:(oc + 1) * 128].rearrange(
                "(c p) m -> p c m", p=128)), deps=[ws_last[par]],
            sem="wy" + str(par))
        q1 = SS(lambda s, par=par: s.copy(out=Q8[:, 0:1024], in_=WS[par][:, 0:1024]),
                deps=[d, vlast_d, VSNAP()])
        q2 = SS(lambda s: s.copy(out=I8T[:, 0:1024], in_=Q8[:, 0:1024].bitcast(I8)))
        q3 = SS(lambda s: s.copy(out=WBF[:, 0:1024], in_=I8T[:, 0:1024].bitcast(F8)),
                deps=[wbf_last[0], wbf_tlast[0]])
        ws_last[par] = q3
        for tt in range(6):
            ps = (pA, pB, pC, pD)[(oc * 6 + tt) % 4]
            for dc in range(NCH):
                t = TT(lambda t_, dc=dc, tt=tt, ps=ps: t_.matmul(
                    out=ps[:, 0:512], lhsT=WBF[:, dc * 128:(dc + 1) * 128],
                    rhs=xq8_ap(dc, tt * 512, 512),
                    start=(dc == 0), stop=(dc == NCH - 1)),
                    deps=[q3, pfree[id(ps)]] if dc == 0 else [])
            fp_ = (oc * 6 + tt) % 2
            f16d = F16T[:, 0:512] if fp_ == 0 else I8T.bitcast(F16)[:, 0:512]
            i16d = I16T[:, 0:512] if fp_ == 0 else Q8.bitcast(I16)[:, 0:512]
            f1 = SS(lambda s, ps=ps, f16d=f16d: s.copy(out=f16d, in_=ps[:, 0:512]),
                    deps=[t, vl2_d[fp_]])
            f2 = SS(lambda s, f16d=f16d, i16d=i16d: s.copy(
                out=i16d, in_=f16d.bitcast(I16)))
            pfree[id(ps)] = f2
            vlast_d = VV(lambda v, oc=oc, tt=tt, i16d=i16d: v.scalar_tensor_tensor(
                out=AR1[:, oc * 3072 + tt * 512: oc * 3072 + (tt + 1) * 512],
                in0=i16d.bitcast(F16),
                scalar=B0[:, oc:oc + 1],
                in1=AR1[:, oc * 3072 + tt * 512: oc * 3072 + (tt + 1) * 512],
                op0=ALU.add, op1=ALU.add), deps=[f2])
            vl2_d[fp_] = vlast_d
        wbf_last[0] = TSNAP()
    P.barrier()
    if dbg == "x2":
        for dc in range(NCH):
            DG(lambda g, dc=dc: g.dma_start(
                out=h_dbg[dc * 128:(dc + 1) * 128, :],
                in_=AR1[:, dc * 3072:(dc + 1) * 3072]))

    stok = ln_stats(lambda dc: (AR1[:, dc * 3072 + 2048:(dc + 1) * 3072], None),
                    NEXT_, AR2[:, 8192:9216], AR2[:, 9216:10240])
    P.barrier()
    xln2e = lambda dc, c0, w: AR2B[:, dc * 1024 + c0: dc * 1024 + c0 + w]
    for tt in range(2):
        c0 = tt * 512
        btok = bcast_stats(c0, 512, stok)
        for dc in range(NCH):
            VV(lambda v, dc=dc, c0=c0: v.tensor_tensor(
                out=QS3[:, 0:512],
                in0=AR1[:, dc * 3072 + 2048 + c0: dc * 3072 + 2048 + c0 + 512],
                in1=QS1[:, 0:512], op=ALU.add), deps=btok)
            VV(lambda v, dc=dc, c0=c0: v.tensor_tensor(
                out=xln2e(dc, c0, 512), in0=QS3[:, 0:512], in1=QS2[:, 0:512],
                op=ALU.mult))
    P.barrier()
    k1_ap = lambda ch, c0, w: AR2B[:, 8192 + ch * 1024 + c0: 8192 + ch * 1024 + c0 + w]
    v1_ap = lambda kt: AR2B[:, 16384 + kt * 1024: 16384 + (kt + 1) * 1024]
    for ch in range(NCH):
        wb0 = (ch % 2) * 1024
        cw = load_wslice(lambda ch=ch: h_w["k1"][:, ch * 128:(ch + 1) * 128].rearrange(
            "(c p) m -> p c m", p=128), ch % 2, 1024, wb0=wb0)
        for tt in range(2):
            ps = (pA, pB, pC, pD)[(ch * 2 + tt) % 4]
            for dc in range(NCH):
                t = TT(lambda t_, dc=dc, tt=tt, ps=ps, wb0=wb0: t_.matmul(
                    out=ps[:, 0:512], lhsT=WBF[:, wb0 + dc * 128: wb0 + (dc + 1) * 128],
                    rhs=xln2e(dc, tt * 512, 512),
                    start=(dc == 0), stop=(dc == NCH - 1)),
                    deps=[cw, pfree[id(ps)]] if dc == 0 else [])
            ev = SS(lambda s, ch=ch, tt=tt, ps=ps: s.copy(
                out=k1_ap(ch, tt * 512, 512), in_=ps[:, 0:512]), deps=[t])
            pfree[id(ps)] = ev
        wbf_last[ch % 2] = SSNAP()
        wbf_tlast[ch % 2] = TSNAP()
    for vg in range(4):
        cw = load_wslice(lambda vg=vg: h_w["v1"][:, vg * 256:(vg + 1) * 256].rearrange(
            "(c p) m -> p c m", p=128), vg % 2, 2048)
        for kt in range(8):
            ps = (pA, pB, pC, pD)[(vg * 8 + kt) % 4]
            for dc in range(NCH):
                t = TT(lambda t_, dc=dc, kt=kt, ps=ps: t_.matmul(
                    out=ps[:, 0:256], lhsT=xln2e(dc, kt * 128, 128),
                    rhs=WBF[:, dc * 256:(dc + 1) * 256],
                    start=(dc == 0), stop=(dc == NCH - 1)),
                    deps=[cw, pfree[id(ps)]] if dc == 0 else [])
            ev = SS(lambda s, kt=kt, vg=vg, ps=ps: s.copy(
                out=v1_ap(kt)[:, vg * 256:(vg + 1) * 256], in_=ps[:, 0:256]),
                deps=[t])
            pfree[id(ps)] = ev
        wbf_last[0] = wbf_last[1] = SSNAP()
        wbf_tlast[0] = wbf_tlast[1] = TSNAP()
    P.barrier()
    stok = ln_stats(lambda dc: (AR1[:, dc * 3072: dc * 3072 + 2048], None), NOWN,
                    AR2[:, 0:2048], AR2[:, 2048:4096], rows=WS[0])
    P.barrier()

    def xo_ap(dc, c0, w):
        if dc < 4:
            return AR2B[:, dc * 2048 + c0: dc * 2048 + c0 + w]
        if dc < 6:
            return AR2B[:, 24576 + (dc - 4) * 2048 + c0:
                        24576 + (dc - 4) * 2048 + c0 + w]
        return AR3B[:, (dc - 6) * 2048 + c0: (dc - 6) * 2048 + c0 + w]

    for tt in range(4):
        c0 = tt * 512
        btok = bcast_stats(c0, 512, stok, rows=WS[0])
        for dc in range(NCH):
            VV(lambda v, dc=dc, c0=c0: v.tensor_tensor(
                out=QS3[:, 0:512], in0=AR1[:, dc * 3072 + c0: dc * 3072 + c0 + 512],
                in1=QS1[:, 0:512], op=ALU.add), deps=btok)
            VV(lambda v, dc=dc, c0=c0: v.tensor_tensor(
                out=xo_ap(dc, c0, 512), in0=QS3[:, 0:512], in1=QS2[:, 0:512],
                op=ALU.mult))
    ws_last[0] = TSNAP()
    P.barrier()
    q1b_ap = lambda ch, c0, w: AR1B[:, (ch * 3072 + 2048) * 2 + c0:
                                    (ch * 3072 + 2048) * 2 + c0 + w]
    for ch in range(NCH):
        _oc4 = ch
        wb0 = (ch % 2) * 1024
        cw = load_wslice(lambda ch=ch: h_w["q1"][:, ch * 128:(ch + 1) * 128].rearrange(
            "(c p) m -> p c m", p=128), ch % 2, 1024, wb0=wb0)
        for tt in range(4):
            ps = (pA, pB, pC, pD)[(_oc4 * 4 + tt) % 4]
            for dc in range(NCH):
                t = TT(lambda t_, dc=dc, tt=tt, ps=ps, wb0=wb0: t_.matmul(
                    out=ps[:, 0:512], lhsT=WBF[:, wb0 + dc * 128: wb0 + (dc + 1) * 128],
                    rhs=xo_ap(dc, tt * 512, 512),
                    start=(dc == 0), stop=(dc == NCH - 1)),
                    deps=[cw, pfree[id(ps)]] if dc == 0 else [])
            ev = SS(lambda s, ch=ch, tt=tt, ps=ps: s.activation(
                out=q1b_ap(ch, tt * 512, 512), in_=ps[:, 0:512], func=ACT.Copy,
                scale=0.125), deps=[t])
            pfree[id(ps)] = ev
        wbf_last[ch % 2] = SSNAP()
        wbf_tlast[ch % 2] = TSNAP()
    P.barrier()
    attn_reset()
    cfgL1 = dict(sA=(pC, pA), sB=(pD, pB),
                 ea=(QS3[:, 0:512], QS1[:, 0:512]),
                 eb=(QS3[:, 512:1024], QS2[:, 0:512]),
                 pnb=(PNB[:, 0:1024], AR3B[:, 4096:5120]),
                 ptb=(PTB[:, 0:1024], AR3B[:, 5120:6144]),
                 trb=(pFB[:, 0:1024], pEB[:, 0:1024]),
                 avs=(pH[0:64, 0:128], pH[64:128, 0:128]))
    tiles = []
    ntile = 0
    for h in range(16):
        ch = h // 2
        p0 = 64 * (h % 2)
        for qt in range(16):
            nt = (qt % 8) + 1
            wk = nt * 128
            if wk <= 512:
                ma, mb = MQ[:, 1152 - wk:1152], None
            else:
                ma, mb = None, MQ[:, 1152 - (wk - 512):1152]
            tiles.append((ntile,
                          q1b_ap(ch, qt * 128, 128)[p0:p0 + 64, :],
                          lambda c0, w, ch=ch, p0=p0: k1_ap(ch, c0, w)[p0:p0 + 64, :],
                          nt, ma, mb,
                          lambda d, h=h: v1_ap(d)[:, h * 64:(h + 1) * 64],
                          xo_ap(ch, qt * 128, 128)[p0:p0 + 64, :], cfgL1))
            ntile += 1
    attn_drive(tiles)
    P.barrier()
    for oc in range(NCH):
        _oc4 = oc
        wb0 = (oc % 2) * 1024
        cw = load_wslice(lambda oc=oc: h_w["o1"][:, oc * 128:(oc + 1) * 128].rearrange(
            "(c p) m -> p c m", p=128), oc % 2, 1024, wb0=wb0)
        for tt in range(4):
            ps = (pA, pB, pC, pD)[(_oc4 * 4 + tt) % 4]
            for dk in range(NCH):
                t = TT(lambda t_, dk=dk, tt=tt, ps=ps, wb0=wb0: t_.matmul(
                    out=ps[:, 0:512], lhsT=WBF[:, wb0 + dk * 128: wb0 + (dk + 1) * 128],
                    rhs=xo_ap(dk, tt * 512, 512),
                    start=(dk == 0), stop=(dk == NCH - 1)),
                    deps=[cw, pfree[id(ps)]] if dk == 0 else [])
            av = VV(lambda v, oc=oc, tt=tt, ps=ps: v.tensor_tensor(
                out=AR1[:, oc * 3072 + tt * 512: oc * 3072 + (tt + 1) * 512],
                in0=AR1[:, oc * 3072 + tt * 512: oc * 3072 + (tt + 1) * 512],
                in1=ps[:, 0:512], op=ALU.add), deps=[t])
            pfree[id(ps)] = av
        wbf_last[oc % 2] = VSNAP()
        wbf_tlast[oc % 2] = TSNAP()
    P.barrier()
    if dbg == "x3":
        for dc in range(NCH):
            DG(lambda g, dc=dc: g.dma_start(
                out=h_dbg[dc * 128:(dc + 1) * 128, :],
                in_=AR1[:, dc * 3072:(dc + 1) * 3072]))

    stok = ln_stats(lambda dc: (AR1[:, dc * 3072: dc * 3072 + 2048], None), NOWN,
                    AR2[:, 8192:10240], AR2[:, 10240:12288])
    P.barrier()
    xln3 = lambda dc, c0, w: AR2B[:, dc * 2048 + c0: dc * 2048 + c0 + w]
    for tt in range(4):
        c0 = tt * 512
        btok = bcast_stats(c0, 512, stok)
        for dc in range(NCH):
            VV(lambda v, dc=dc, c0=c0: v.tensor_tensor(
                out=QS3[:, 0:512], in0=AR1[:, dc * 3072 + c0: dc * 3072 + c0 + 512],
                in1=QS1[:, 0:512], op=ALU.add), deps=btok)
            VV(lambda v, dc=dc, c0=c0: v.tensor_tensor(
                out=xln3(dc, c0, 512), in0=QS3[:, 0:512], in1=QS2[:, 0:512],
                op=ALU.mult))
    P.barrier()
    dwg = P.emit("sync", lambda y: y.dma_start(
        out=WS[0][:, 0:64], in_=h_wg[:, :].rearrange("(c p) m -> p c m", p=128)),
        deps=[ws_last[0]], sem="wy0")
    cwg = SS(lambda s: s.copy(out=WBF[:, 0:64], in_=WS[0][:, 0:64]),
             deps=[dwg, wbf_last[0], wbf_tlast[0]])
    ws_last[0] = cwg
    for tt in range(4):
        for dc in range(NCH):
            t = TT(lambda t_, dc=dc, tt=tt: t_.matmul(
                out=pC[0:8, 0:512], lhsT=WBF[:, dc * 8:(dc + 1) * 8],
                rhs=xln3(dc, tt * 512, 512),
                start=(dc == 0), stop=(dc == NCH - 1)),
                deps=[cwg, pfree[id(pC)]] if dc == 0 else [])
        ev = VV(lambda v, tt=tt: v.tensor_scalar(
            out=AR3[32:40, tt * 512:(tt + 1) * 512], in0=pC[0:8, 0:512],
            scalar1=BG[0:8, 0:1], scalar2=None, op0=ALU.add), deps=[t])
        pfree[id(pC)] = ev
    lg_done = SSNAP()
    for ts in range(16):
        t = TT(lambda t_, ts=ts: t_.transpose(
            out=pG[:, 0:8], in_=AR3[32:40, ts * 128:(ts + 1) * 128],
            identity=IDF[32:40, 32:40]), deps=[lg_done, pfree[id(pG)]])
        ev = SS(lambda s, ts=ts: s.copy(out=QS3[:, 128 + ts * 8: 128 + (ts + 1) * 8],
                                        in_=pG[:, 0:8]), deps=[t])
        pfree[id(pG)] = ev
        r1 = VV(lambda v, ts=ts: v.tensor_reduce(
            out=RMAX[:, 0:1], in_=QS3[:, 128 + ts * 8: 128 + (ts + 1) * 8], axis=AXX,
            op=ALU.max, negate=True), deps=[ev, SSNAP()])
        r2 = SS(lambda s, ts=ts: s.activation(
            out=QS3[:, ts * 8:(ts + 1) * 8], in_=QS3[:, 128 + ts * 8: 128 + (ts + 1) * 8],
            func=ACT.Exp, bias=RMAX[:, 0:1], accum_out=RSUM[:, 0:1]), deps=[r1])
        VV(lambda v, ts=ts: v.reciprocal(out=QS3[:, 272 + ts: 273 + ts], in_=RSUM[:, 0:1]),
           deps=[r2])
    sgate = SSNAP()
    VV(lambda v: v.tensor_scalar(out=QS3[:, 0:128], in0=QS3[:, 0:128], scalar1=1.0,
                                 scalar2=None, op0=ALU.is_ge), deps=[sgate])
    VV(lambda v: v.tensor_copy(out=QS3[:, 256:272],
                               in_=bass.AP(QS3, 0, [[1024, 128], [8, 16]])))
    for e in range(1, 8):
        ge = bass.AP(QS3, e, [[1024, 128], [8, 16]])
        VV(lambda v: v.tensor_scalar(out=QS1[:, 0:16], in0=QS3[:, 256:272],
                                     scalar1=-1.0, scalar2=1.0,
                                     op0=ALU.mult, op1=ALU.add))
        VV(lambda v, ge=ge: v.tensor_tensor(out=ge, in0=ge, in1=QS1[:, 0:16],
                                            op=ALU.mult))
        VV(lambda v, ge=ge: v.tensor_tensor(out=QS3[:, 256:272], in0=QS3[:, 256:272],
                                            in1=ge, op=ALU.add))
    for e in range(8):
        ge = bass.AP(QS3, e, [[1024, 128], [8, 16]])
        VV(lambda v, ge=ge: v.tensor_tensor(out=ge, in0=ge, in1=QS3[:, 272:288],
                                            op=ALU.mult))
    gt_done = VSNAP()
    P.barrier()

    def acc_ap(ts, c0, w):
        if ts < 6:
            return AR2[:, 8192 + ts * 1024 + c0: 8192 + ts * 1024 + c0 + w]
        if ts < 14:
            base = (ts - 6) * 3072 + 2048
            return AR1[:, base + c0: base + c0 + w]
        return AR3[:, (ts - 14) * 1024 + c0: (ts - 14) * 1024 + c0 + w]

    db2 = P.emit("sync", lambda y: y.dma_start(out=WS[1][0:8, 0:1024], in_=h_b2[:, :]),
                 deps=[ws_last[1]], sem="wy1")
    qs1g_last = None
    for ts in range(16):
        tg = TT(lambda t_, ts=ts: t_.transpose(
            out=pG[0:8, 0:128], in_=QS3[:, ts * 8:(ts + 1) * 8], identity=IDF[:, :]),
            deps=[gt_done, pfree[id(pG)]])
        eg = SS(lambda s: s.copy(out=QS1[0:8, 0:128], in_=pG[0:8, 0:128]),
                deps=[tg, qs1g_last, VSNAP()])
        pfree[id(pG)] = eg
        for oh in range(2):
            ps = pC if oh == 0 else pD
            psx = pE if oh == 0 else pF
            mb = TT(lambda t_, oh=oh, ps=ps: t_.matmul(
                out=ps[:, 0:512], lhsT=QS1[0:8, 0:128],
                rhs=WS[1][0:8, oh * 512:(oh + 1) * 512], start=True, stop=True),
                deps=[eg, db2, pfree[id(ps)]])
            qs1g_last = mb
            trs = []
            for k in range(4):
                dc = oh * 4 + k
                t = TT(lambda t_, dc=dc, ts=ts, k=k, psx=psx: t_.transpose(
                    out=psx[:, k * 128:(k + 1) * 128],
                    in_=AR1[:, dc * 3072 + ts * 128: dc * 3072 + (ts + 1) * 128],
                    identity=IDF[:, :]),
                    deps=[pfree[id(psx)]] if k == 0 else [])
                trs.append(t)
            eb2 = SS(lambda s, ps=ps: s.copy(out=QS3[:, 512:1024], in_=ps[:, 0:512]),
                     deps=[mb, qs1g_last, VSNAP()])
            av = VV(lambda v, ts=ts, oh=oh, psx=psx: v.tensor_tensor(
                out=acc_ap(ts, oh * 512, 512), in0=psx[:, 0:512], in1=QS3[:, 512:1024],
                op=ALU.add), deps=trs + [eb2])
            pfree[id(ps)] = av
            pfree[id(psx)] = av
    P.barrier()

    def w1bf_ap(dc):
        r = dc // 4
        return AR1B[:, r * 6144 + (dc % 4) * 1024: r * 6144 + (dc % 4) * 1024 + 1024]

    def w2bf_ap(ms):
        r = 2 + ms // 4
        return AR1B[:, r * 6144 + (ms % 4) * 1024: r * 6144 + (ms % 4) * 1024 + 1024]

    def h_ap(ms, tq):
        idx = ms * 4 + tq
        r = 4 + idx // 8
        return AR1B[:, r * 6144 + (idx % 8) * 512: r * 6144 + (idx % 8) * 512 + 512]

    w1_last = None
    w2_last = None

    def emit_w1_loads(e, mc):
        toks = []
        for i in range(4):
            par = i % 2
            d = P.emit("sync", lambda y, e=e, mc=mc, i=i, par=par: y.dma_start(
                out=WS[par][:, 0:2048],
                in_=h_w1[e, 256 * i:256 * (i + 1),
                         mc * 1024:(mc + 1) * 1024].rearrange(
                    "(c p) m -> p c m", p=128)),
                deps=[ws_last[par], w1_last], sem="wy" + str(par))
            dst0 = (2 * i // 4) * 6144 + ((2 * i) % 4) * 1024
            c = SS(lambda s, par=par, dst0=dst0: s.copy(
                out=AR1B[:, dst0:dst0 + 2048], in_=WS[par][:, 0:2048]), deps=[d])
            ws_last[par] = c
            toks.append(c)
        return toks

    def emit_w2_loads(e, mc):
        toks = []
        for i in range(4):
            par = i % 2
            d = P.emit("sync", lambda y, e=e, mc=mc, i=i, par=par: y.dma_start(
                out=WS[par][:, 0:2048],
                in_=h_w2[e, mc * 1024 + 256 * i: mc * 1024 + 256 * (i + 1),
                         :].rearrange("(c p) m -> p c m", p=128)),
                deps=[ws_last[par], w2_last], sem="wy" + str(par))
            dst0 = (2 + (2 * i) // 4) * 6144 + ((2 * i) % 4) * 1024
            c = VV(lambda v, par=par, dst0=dst0: v.tensor_copy(
                out=AR1B[:, dst0:dst0 + 2048], in_=WS[par][:, 0:2048]), deps=[d])
            ws_last[par] = c
            toks.append(c)
        return toks

    emcs = [(e, mc) for e in range(8) for mc in range(4)]
    c1toks = emit_w1_loads(0, 0)
    c2toks = emit_w2_loads(0, 0)
    for bi, (e, mc) in enumerate(emcs):
        h_last = None
        for ms in range(8):
            for tq in range(4):
                ps = (pA, pB, pF)[(ms * 4 + tq) % 3]
                dep0 = c1toks + [pfree[id(ps)], w2_last]
                for dc in range(NCH):
                    t = TT(lambda t_, dc=dc, ms=ms, tq=tq, ps=ps: t_.matmul(
                        out=ps[:, 0:512],
                        lhsT=w1bf_ap(dc)[:, ms * 128:(ms + 1) * 128],
                        rhs=xln3(dc, tq * 512, 512),
                        start=(dc == 0), stop=(dc == NCH - 1)),
                        deps=dep0 if dc == 0 else [])
                ev = SS(lambda s, e=e, mc=mc, ms=ms, tq=tq, ps=ps: s.activation(
                    out=h_ap(ms, tq), in_=ps[:, 0:512], func=ACT.Relu,
                    bias=B1[:, e * 32 + mc * 8 + ms: e * 32 + mc * 8 + ms + 1]),
                    deps=[t, w2_last] if (ms == 0 and tq == 0) else [t])
                pfree[id(ps)] = ev
                h_last = ev
        w1_last = h_last
        if bi + 1 < len(emcs):
            c1toks = emit_w1_loads(*emcs[bi + 1])
        psl = [pC, pD, pE]
        for ts in range(16):
            for oh in range(2):
                ps = psl[(ts * 2 + oh) % 3]
                dep0 = c2toks + [pfree[id(ps)], h_last]
                for ms in range(8):
                    t = TT(lambda t_, ms=ms, ts=ts, oh=oh, ps=ps: t_.matmul(
                        out=ps[:, 0:512],
                        lhsT=h_ap(ms, ts // 4)[:, (ts % 4) * 128:
                                               (ts % 4 + 1) * 128],
                        rhs=w2bf_ap(ms)[:, oh * 512:(oh + 1) * 512],
                        start=(ms == 0), stop=(ms == 7)),
                        deps=dep0 if ms == 0 else [])
                w2_last = t
                av = VV(lambda v, e=e, ts=ts, oh=oh, ps=ps: v.scalar_tensor_tensor(
                    out=acc_ap(ts, oh * 512, 512), in0=ps[:, 0:512],
                    scalar=QS3[:, ts * 8 + e: ts * 8 + e + 1],
                    in1=acc_ap(ts, oh * 512, 512),
                    op0=ALU.mult, op1=ALU.add), deps=[t])
                pfree[id(ps)] = av
        if bi + 1 < len(emcs):
            c2toks = emit_w2_loads(*emcs[bi + 1])
    P.barrier()

    for ts in range(16):
        DG(lambda g, ts=ts: g.dma_start(
            out=h_out[ts * 128:(ts + 1) * 128, :], in_=acc_ap(ts, 0, 1024)))
    P.emit("gpsimd", None, deps=[("gpsimd", P.cnt["gpsimd"])], inc=0)

    with nc.Block() as block:
        P.run(block, sems)
    es.close()
    return nc


_NC_CACHE = {}


def _get_nc(dbg=None):
    if dbg not in _NC_CACHE:
        _NC_CACHE[dbg] = build_nc(dbg)
    return _NC_CACHE[dbg]


def make_inputs(inputs):
    f = np.float32
    import ml_dtypes
    x = np.asarray(inputs["L0_attn_x"], dtype=f)
    import ml_dtypes as _mld
    idx = np.arange(128)
    m_su = np.where(idx[None, :] > idx[:, None], 0.0, NEG).astype(f)
    m_li = np.where(idx[None, :] <= idx[:, None], 0.0, NEG).astype(f)
    mask_own = np.concatenate([m_su, np.zeros((128, 384), f), m_li], axis=1)
    mq = np.concatenate([np.zeros((128, 512), f), mask_own], axis=1)
    common = {
        "w_q0": np.ascontiguousarray(inputs["L0_attn_Wq"].T.astype(f)),
        "w_k0": np.ascontiguousarray(inputs["L0_attn_Wk"].T.astype(f)),
        "w_v0": np.ascontiguousarray(inputs["L0_attn_Wv"].T.astype(f)),
        "w_o0": np.ascontiguousarray(inputs["L0_attn_Wo"].T.astype(f)),
        "w_q1": np.ascontiguousarray(inputs["L1_attn_Wq"].T.astype(f)),
        "w_k1": np.ascontiguousarray(inputs["L1_attn_Wk"].T.astype(f)),
        "w_v1": np.ascontiguousarray(inputs["L1_attn_Wv"].T.astype(f)),
        "w_o1": np.ascontiguousarray(inputs["L1_attn_Wo"].T.astype(f)),
        "w_f0": np.ascontiguousarray(inputs["L0_ffn_weight"].T.astype(f)),
        "w_moe1": np.ascontiguousarray(inputs["L1_ffn_W1"].astype(f)),
        "w_moe2": np.ascontiguousarray(inputs["L1_ffn_W2"].astype(f)),
        "w_gate": np.ascontiguousarray(inputs["L1_ffn_w_gate"].astype(f)),
        "b_gate": np.ascontiguousarray(inputs["L1_ffn_b_gate"].astype(f)
                                       .reshape(8, 1)),
        "b_moe1": np.ascontiguousarray(
            inputs["L1_ffn_b1"].astype(f).reshape(8, 32, 128).transpose(2, 0, 1)
            .reshape(128, 256)),
        "b_moe2": np.ascontiguousarray(inputs["L1_ffn_b2"].astype(f)),
        "b_ffn0": np.ascontiguousarray(inputs["L0_ffn_bias"].astype(f)
                                       .reshape(8, 128).T),
        "mq_bf": mq.astype(_mld.bfloat16),
        "ident_f": np.eye(128, dtype=f),
        "ident_b": np.eye(128, dtype=f).astype(ml_dtypes.bfloat16),
        "ones_c": np.ones((128, 1), f),
        "ones_r": np.ones((33, 128), f),
    }
    in_maps = []
    for c in range(8):
        b, hh = c // 2, c % 2
        xb = x[b]
        own = xb[hh * 2048:(hh + 1) * 2048]
        halo = xb[hh * 2048 - 512: hh * 2048] if hh else np.zeros((512, D), f)
        ext = xb[0:1024]
        m = dict(common)
        m["xin"] = np.ascontiguousarray(np.concatenate([halo, own, ext], axis=0).T)
        bcm = np.empty((128, 2048), f)
        for qt in range(4):
            v = mask_own[:, 0:512].copy()
            if hh == 0:
                v[:, 0:512 - 128 * qt] += NEG
            bcm[:, qt * 512:(qt + 1) * 512] = v
        m["bcm_bf"] = np.ascontiguousarray(bcm.astype(_mld.bfloat16))
        in_maps.append(m)
    return in_maps


def kernel(**inputs):
    nc = _get_nc()
    in_maps = make_inputs(inputs)
    res = run_bass_kernel_spmd(nc, in_maps, core_ids=list(range(8)))
    out = np.empty((4, 4096, D), dtype=np.float32)
    for c in range(8):
        b, hh = c // 2, c % 2
        out[b, hh * 2048:(hh + 1) * 2048] = res.results[c]["out"]
    return out


# version 20
# speedup vs baseline: 1.1734x; 1.0194x over previous
import numpy as np
from contextlib import ExitStack
import concourse.bass as bass
import concourse.mybir as mybir
from concourse.bass_utils import run_bass_kernel_spmd

F32 = mybir.dt.float32
BF16 = mybir.dt.bfloat16
F16 = mybir.dt.float16
F8 = mybir.dt.float8e4
I8 = mybir.dt.int8
I16 = mybir.dt.int16
ACT = mybir.ActivationFunctionType
ALU = mybir.AluOpType
AXX = mybir.AxisListType.X

D = 1024
NCH = 8
NOWN = 2048
NTOT = 3584
NOE = 3072
NEXT_ = 1024
NEG = -1.0e30
EPS = 1e-5


class Prog:
    ENGINES = ("tensor", "vector", "scalar", "sync", "gpsimd")
    VSEMS = ("cst", "xs0", "xs1", "xr0", "xr1", "wy0", "wy1")
    DMAS = ("sync", "gpsimd", "cst", "xs0", "xs1", "xr0", "xr1", "wy0", "wy1")

    def __init__(self):
        self.ops = {e: [] for e in self.ENGINES}
        allsems = self.ENGINES + self.VSEMS
        self.cnt = {e: 0 for e in allsems}
        self.hwm = {e: {x: 0 for x in allsems} for e in self.ENGINES}

    def emit(self, eng, fn, deps=(), selfdep=False, inc=1, sem=None):
        sem = sem or eng
        waits = []
        dd = [d for d in deps if d is not None]
        if selfdep and self.cnt[eng] > 0:
            dd.append((eng, self.cnt[eng]))
        for peng, pval in dd:
            if pval <= 0:
                continue
            if self.hwm[eng][peng] < pval:
                self.hwm[eng][peng] = pval
                waits.append((peng, pval))
        self.ops[eng].append((waits, fn, inc, sem))
        if fn is not None and inc:
            self.cnt[sem] += inc
            return (sem, self.cnt[sem])
        return None

    def barrier(self):
        snap = dict(self.cnt)
        allsems = self.ENGINES + self.VSEMS
        for e in self.ENGINES:
            self.emit(e, None, deps=[(p, snap[p]) for p in allsems if p != e],
                      inc=0)

    def run(self, block, sems):
        mult = {e: (16 if e in self.DMAS else 1) for e in self.ENGINES + self.VSEMS}

        def mk(eng_name):
            def body(eng):
                for waits, fn, inc, sem in self.ops[eng_name]:
                    for peng, pval in waits:
                        eng.wait_ge(sems[peng], pval * mult[peng])
                    if fn is not None:
                        inst = fn(eng)
                        if inc:
                            inst.then_inc(sems[sem], inc * mult[sem])
            return body
        block.tensor(mk("tensor"))
        block.vector(mk("vector"))
        block.scalar(mk("scalar"))
        block.sync(mk("sync"))
        block.gpsimd(mk("gpsimd"))


def build_nc(dbg=None):
    nc = bass.Bass("TRN2", target_bir_lowering=False)
    P = Prog()

    h_xin = nc.dram_tensor("xin", [D, NTOT], F32, kind="ExternalInput").ap()
    h_w = {}
    for nm in ("q0", "k0", "v0", "o0", "q1", "k1", "v1", "o1", "f0"):
        h_w[nm] = nc.dram_tensor("w_" + nm, [D, D], F32, kind="ExternalInput").ap()
    h_w1 = nc.dram_tensor("w_moe1", [8, D, 4096], F32, kind="ExternalInput").ap()
    h_w2 = nc.dram_tensor("w_moe2", [8, 4096, D], F32, kind="ExternalInput").ap()
    h_wg = nc.dram_tensor("w_gate", [D, 8], F32, kind="ExternalInput").ap()
    h_bg = nc.dram_tensor("b_gate", [8, 1], F32, kind="ExternalInput").ap()
    h_b1 = nc.dram_tensor("b_moe1", [128, 256], F32, kind="ExternalInput").ap()
    h_b2 = nc.dram_tensor("b_moe2", [8, D], F32, kind="ExternalInput").ap()
    h_b0 = nc.dram_tensor("b_ffn0", [128, 8], F32, kind="ExternalInput").ap()
    h_mq = nc.dram_tensor("mq_bf", [128, 1152], BF16, kind="ExternalInput").ap()
    h_bcm = nc.dram_tensor("bcm_bf", [128, 2048], BF16, kind="ExternalInput").ap()
    h_idf = nc.dram_tensor("ident_f", [128, 128], F32, kind="ExternalInput").ap()
    h_idb = nc.dram_tensor("ident_b", [128, 128], BF16, kind="ExternalInput").ap()
    h_onc = nc.dram_tensor("ones_c", [128, 1], F32, kind="ExternalInput").ap()
    h_onr = nc.dram_tensor("ones_r", [33, 128], F32, kind="ExternalInput").ap()
    h_out = nc.dram_tensor("out", [NOWN, D], F32, kind="ExternalOutput").ap()
    h_dbg = None
    if dbg is not None:
        h_dbg = nc.dram_tensor("dbgout", [D, NOE], F32, kind="ExternalOutput").ap()

    es = ExitStack()
    sbt = lambda nm, shp, dt: es.enter_context(nc.sbuf_tensor(nm, shp, dt))
    pst = lambda nm, shp: es.enter_context(nc.psum_tensor(nm, shp, F32))

    AR1 = sbt("AR1", [128, 24576], F32)
    AR2 = sbt("AR2", [128, 14336], F32)
    AR3 = sbt("AR3", [128, 3584], F32)
    WS = [sbt("ws0", [128, 2048], F32), sbt("ws1", [128, 2048], F32)]
    WBF = sbt("wbf", [128, 2048], BF16)
    MQ = sbt("mqbf", [128, 1152], BF16)
    IDF = sbt("identf", [128, 128], F32)
    IDB = sbt("identb", [128, 128], BF16)
    ONC = sbt("onesc", [128, 1], F32)
    ONR = sbt("onesr", [33, 128], F32)
    B1 = sbt("b1sb", [128, 256], F32)
    B0 = sbt("b0sb", [128, 8], F32)
    BG = sbt("bgsb", [8, 1], F32)
    QS1 = sbt("qs1", [128, 512], F32)
    QS2 = sbt("qs2", [128, 512], F32)
    QS3 = sbt("qs3", [128, 1024], F32)
    GT = QS3
    PNB = sbt("pnb", [128, 1024], BF16)
    PTB = sbt("ptb", [128, 1024], BF16)
    RMAX = sbt("rmax", [128, 8], F32)
    RSUM = sbt("rsum", [128, 4], F32)
    RINV = sbt("rinv", [128, 2], F32)
    Q8 = sbt("q8t", [128, 1024], F8)
    I8T = sbt("i8t", [128, 1024], I8)
    F16T = sbt("f16t", [128, 512], F16)
    I16T = sbt("i16t", [128, 512], I16)

    AR1B = AR1.bitcast(BF16)
    AR2B = AR2.bitcast(BF16)
    AR3B = AR3.bitcast(BF16)

    pA = pst("pA", [128, 512])
    pB = pst("pB", [128, 512])
    pC = pst("pC", [128, 512])
    pD = pst("pD", [128, 512])
    pE = pst("pE", [128, 512])
    pF = pst("pF", [128, 512])
    pG = pst("pG", [128, 128])
    pH = pst("pH", [128, 128])

    pFB = pF.bitcast(BF16)
    pDB = pD.bitcast(BF16)
    pEB = pE.bitcast(BF16)
    pGB = pG.bitcast(BF16)
    sems = {e: es.enter_context(nc.semaphore("sem_" + e))
            for e in Prog.ENGINES + Prog.VSEMS}

    TT = lambda fn, deps=(): P.emit("tensor", fn, deps=deps)
    VV = lambda fn, deps=(): P.emit("vector", fn, deps=deps, selfdep=True)
    SS = lambda fn, deps=(): P.emit("scalar", fn, deps=deps, selfdep=True)
    DY = lambda fn, deps=(): P.emit("sync", fn, deps=deps)
    DG = lambda fn, deps=(): P.emit("gpsimd", fn, deps=deps)
    TSNAP = lambda: ("tensor", P.cnt["tensor"])
    VSNAP = lambda: ("vector", P.cnt["vector"])
    SSNAP = lambda: ("scalar", P.cnt["scalar"])

    pfree = {id(p): None for p in (pA, pB, pC, pD, pE, pF, pG, pH)}
    ws_last = [None, None]
    wbf_last = [None, None]
    wbf_tlast = [None, None]

    P.emit("gpsimd", lambda g: g.dma_start(out=MQ[:], in_=h_mq[:, :]), sem="cst")
    P.emit("gpsimd", lambda g: g.dma_start(out=AR2.bitcast(BF16)[:, 19456:21504], in_=h_bcm[:, :]), sem="cst")
    P.emit("gpsimd", lambda g: g.dma_start(out=IDF[:], in_=h_idf[:, :]), sem="cst")
    P.emit("gpsimd", lambda g: g.dma_start(out=IDB[:], in_=h_idb[:, :]), sem="cst")
    P.emit("gpsimd", lambda g: g.dma_start(out=ONC[:], in_=h_onc[:, :]), sem="cst")
    P.emit("gpsimd", lambda g: g.dma_start(out=ONR[:], in_=h_onr[:, :]), sem="cst")
    P.emit("gpsimd", lambda g: g.dma_start(out=B1[:], in_=h_b1[:, :]), sem="cst")
    P.emit("gpsimd", lambda g: g.dma_start(out=B0[:], in_=h_b0[:, :]), sem="cst")
    P.emit("gpsimd", lambda g: g.dma_start(out=BG[:], in_=h_bg[:, :]), sem="cst")
    P.barrier()

    src_cons = [None, None]

    def ln_stats(src, n, accx, accsq, rows=None):
        if rows is None:
            rows = AR3
        sq = (QS1[:, 0:512], QS2[:, 0:512])
        sq_cons = [None, None]
        for dc in range(NCH):
            ap, dtok = src(dc)
            if dc == 0:
                VV(lambda v, ap=ap: v.tensor_copy(out=accx[:, 0:n], in_=ap),
                   deps=[dtok])
            else:
                VV(lambda v, ap=ap: v.tensor_tensor(out=accx[:, 0:n], in0=accx[:, 0:n],
                                                    in1=ap, op=ALU.add), deps=[dtok])
            for tt in range((n + 511) // 512):
                c0 = tt * 512
                w = min(512, n - c0)
                qp = (dc + tt) % 2
                ts = SS(lambda s, ap=ap, c0=c0, w=w, qp=qp: s.activation(
                    out=sq[qp][:, 0:w], in_=ap[:, c0:c0 + w], func=ACT.Square),
                    deps=[sq_cons[qp], dtok])
                if dc == 0:
                    av = VV(lambda v, c0=c0, w=w, qp=qp: v.tensor_copy(
                        out=accsq[:, c0:c0 + w], in_=sq[qp][:, 0:w]), deps=[ts])
                else:
                    av = VV(lambda v, c0=c0, w=w, qp=qp: v.tensor_tensor(
                        out=accsq[:, c0:c0 + w], in0=accsq[:, c0:c0 + w],
                        in1=sq[qp][:, 0:w], op=ALU.add), deps=[ts])
                sq_cons[qp] = av
            src_cons[dc % 2] = av
        vacc = VSNAP()
        evs = []
        for tt in range((n + 511) // 512):
            c0 = tt * 512
            w = min(512, n - c0)
            m1 = TT(lambda t, c0=c0, w=w: t.matmul(
                out=pE[0:1, 0:w], lhsT=ONC[:, :], rhs=accx[:, c0:c0 + w],
                start=True, stop=True), deps=[vacc, pfree[id(pE)]])
            e1 = SS(lambda s, c0=c0, w=w: s.copy(out=rows[0:1, c0:c0 + w],
                                                 in_=pE[0:1, 0:w]), deps=[m1])
            pfree[id(pE)] = e1
            m2 = TT(lambda t, c0=c0, w=w: t.matmul(
                out=pF[0:1, 0:w], lhsT=ONC[:, :], rhs=accsq[:, c0:c0 + w],
                start=True, stop=True), deps=[vacc, pfree[id(pF)]])
            e2 = SS(lambda s, c0=c0, w=w: s.copy(out=rows[32:33, c0:c0 + w],
                                                 in_=pF[0:1, 0:w]), deps=[m2])
            pfree[id(pF)] = e2
            evs += [e1, e2]
        return evs + [SSNAP()]

    def bcast_stats(c0, w, stok, rows=None, dst1=None, dst2=None):
        if rows is None:
            rows = AR3
        if dst1 is None:
            dst1 = QS1[:, 0:w]
        if dst2 is None:
            dst2 = QS2[:, 0:w]
        war = [VSNAP(), TSNAP()]
        m1 = TT(lambda t, c0=c0, w=w: t.matmul(
            out=pE[:, 0:w], lhsT=ONR[0:1, :], rhs=rows[0:1, c0:c0 + w],
            start=True, stop=True), deps=list(stok) + [pfree[id(pE)]])
        e1 = SS(lambda s, w=w, dst1=dst1: s.copy(out=dst1, in_=pE[:, 0:w]),
                deps=[m1] + war)
        pfree[id(pE)] = e1
        m2 = TT(lambda t, c0=c0, w=w: t.matmul(
            out=pF[:, 0:w], lhsT=ONR[32:33, :], rhs=rows[32:33, c0:c0 + w],
            start=True, stop=True), deps=list(stok) + [pfree[id(pF)]])
        e2 = SS(lambda s, w=w, dst2=dst2: s.copy(out=dst2, in_=pF[:, 0:w]),
                deps=[m2])
        pfree[id(pF)] = e2
        tmp = QS3[:, 512:512 + w]
        VV(lambda v, dst1=dst1: v.tensor_scalar(out=dst1, in0=dst1,
                                                scalar1=-1.0 / D, scalar2=None,
                                                op0=ALU.mult), deps=[e1])
        VV(lambda v, dst2=dst2: v.tensor_scalar(out=dst2, in0=dst2,
                                                scalar1=1.0 / D, scalar2=EPS,
                                                op0=ALU.mult, op1=ALU.add),
           deps=[e2])
        VV(lambda v, dst1=dst1, tmp=tmp: v.tensor_tensor(out=tmp, in0=dst1,
                                                         in1=dst1, op=ALU.mult))
        tv = VV(lambda v, dst2=dst2, tmp=tmp: v.tensor_tensor(
            out=dst2, in0=dst2, in1=tmp, op=ALU.subtract))
        tq = SS(lambda s, dst2=dst2: s.activation(out=dst2, in_=dst2,
                                                  func=ACT.Sqrt), deps=[tv])
        tr = VV(lambda v, dst2=dst2: v.reciprocal(out=dst2, in_=dst2), deps=[tq])
        return [tr, e1]

    def load_wslice(hfn, par, nf32, prior=(), wb0=0):
        d = P.emit("sync", lambda y, hfn=hfn, par=par, nf32=nf32: y.dma_start(
            out=WS[par][:, 0:nf32], in_=hfn()), deps=[ws_last[par]] + list(prior),
            sem="wy" + str(par))
        if nf32 <= 1024:
            wdeps = [wbf_last[wb0 // 1024], wbf_tlast[wb0 // 1024]]
        else:
            wdeps = [wbf_last[0], wbf_last[1], wbf_tlast[0], wbf_tlast[1]]
        c = SS(lambda s, par=par, nf32=nf32, wb0=wb0: s.copy(
            out=WBF[:, wb0:wb0 + nf32], in_=WS[par][:, 0:nf32]),
            deps=[d] + wdeps)
        ws_last[par] = c
        return c

    att_st = {}

    def attn_reset():
        att_st.clear()
        for p in (0, 1):
            att_st[p] = dict(exp=None, nrm=None, tr=None, pte=None, avm=None,
                             oev=None, rmx=None, rcp=None)

    def attn_s1(n, qap, kap_fn, nt, mask_a, mask_b, v_fn, o_ap, cfg):
        p = n % 2
        st = att_st[p]
        wk = nt * 128
        w0 = min(wk, 512)
        wb = wk - w0
        sA = cfg["sA"][p]
        sB = cfg["sB"][p]
        m1 = TT(lambda t: t.matmul(out=sA[:, 0:w0], lhsT=qap, rhs=kap_fn(0, w0),
                                   start=True, stop=(mask_a is None)),
                deps=[st["exp"], st["rmx"]])
        if mask_a is not None:
            m1 = TT(lambda t: t.matmul(out=sA[:, 0:w0], lhsT=IDB[:, :], rhs=mask_a,
                                       start=False, stop=True))
        m2 = None
        if wb > 0:
            m2 = TT(lambda t: t.matmul(out=sB[:, 0:wb], lhsT=qap, rhs=kap_fn(w0, wb),
                                       start=True, stop=(mask_b is None)))
            if mask_b is not None:
                m2 = TT(lambda t: t.matmul(out=sB[:, 0:wb], lhsT=IDB[:, :],
                                           rhs=mask_b, start=False, stop=True))
        r1 = VV(lambda v: v.tensor_reduce(out=RMAX[:, 4 * p:4 * p + 1],
                                          in_=sA[:, 0:w0], axis=AXX,
                                          op=ALU.max, negate=True),
                deps=[m1, st["exp"]])
        rc = r1
        rcol = 4 * p
        if wb > 0:
            r2 = VV(lambda v: v.tensor_reduce(out=RMAX[:, 4 * p + 1:4 * p + 2],
                                              in_=sB[:, 0:wb], axis=AXX,
                                              op=ALU.max, negate=True), deps=[m2])
            rc = VV(lambda v: v.tensor_tensor(out=RMAX[:, 4 * p + 2:4 * p + 3],
                                              in0=RMAX[:, 4 * p:4 * p + 1],
                                              in1=RMAX[:, 4 * p + 1:4 * p + 2],
                                              op=ALU.min))
            rcol = 4 * p + 2
        st["rmx"] = rc
        return dict(n=n, nt=nt, wk=wk, w0=w0, wb=wb, vfn=v_fn, oap=o_ap, cfg=cfg,
                    rc=rc, rcol=rcol, m1=m1, m2=m2)

    def attn_s2a(cx):
        p = cx["n"] % 2
        st = att_st[p]
        cfg = cx["cfg"]
        e1 = SS(lambda s: s.activation(out=cfg["ea"][p][:, 0:cx["w0"]],
                                       in_=cfg["sA"][p][:, 0:cx["w0"]],
                                       func=ACT.Exp,
                                       bias=RMAX[:, cx["rcol"]:cx["rcol"] + 1],
                                       accum_out=RSUM[:, 2 * p:2 * p + 1]),
                deps=[cx["rc"], st["rcp"], st["nrm"]])
        elast = e1
        if cx["wb"] > 0:
            elast = SS(lambda s: s.activation(out=cfg["eb"][p][:, 0:cx["wb"]],
                                              in_=cfg["sB"][p][:, 0:cx["wb"]],
                                              func=ACT.Exp,
                                              bias=RMAX[:, cx["rcol"]:cx["rcol"] + 1],
                                              accum_out=RSUM[:, 2 * p + 1:2 * p + 2]))
        st["exp"] = elast
        cx["elast"] = elast

    def attn_s2b(cx):
        p = cx["n"] % 2
        st = att_st[p]
        if cx["wb"] > 0:
            sadd = VV(lambda v: v.tensor_tensor(out=RSUM[:, 2 * p:2 * p + 1],
                                                in0=RSUM[:, 2 * p:2 * p + 1],
                                                in1=RSUM[:, 2 * p + 1:2 * p + 2],
                                                op=ALU.add), deps=[cx["elast"]])
        else:
            sadd = cx["elast"]
        rcp = VV(lambda v: v.reciprocal(out=RINV[:, p:p + 1],
                                        in_=RSUM[:, 2 * p:2 * p + 1]), deps=[sadd])
        st["rcp"] = rcp
        cx["rcp"] = rcp

    def attn_s3(cx):
        p = cx["n"] % 2
        st = att_st[p]
        cfg = cx["cfg"]
        pnb = cfg["pnb"][p]
        trb = cfg["trb"][p]
        nt, wk, w0, wb = cx["nt"], cx["wk"], cx["w0"], cx["wb"]
        if p == 0 and wb > 0 and w0 == 512:
            nrm = SS(lambda s: s.activation(out=pnb[:, 0:wk], in_=QS3[:, 0:wk],
                                            func=ACT.Copy, scale=RINV[:, p:p + 1]),
                     deps=[cx["rcp"], st["tr"]])
        else:
            nrm = SS(lambda s: s.activation(out=pnb[:, 0:w0],
                                            in_=cfg["ea"][p][:, 0:w0],
                                            func=ACT.Copy, scale=RINV[:, p:p + 1]),
                     deps=[cx["rcp"], st["tr"]])
            if wb > 0:
                nrm = SS(lambda s: s.activation(out=pnb[:, w0:wk],
                                                in_=cfg["eb"][p][:, 0:wb],
                                                func=ACT.Copy,
                                                scale=RINV[:, p:p + 1]))
        st["nrm"] = nrm
        tts = []
        for d_ in range(nt):
            t = TT(lambda t_, d_=d_: t_.transpose(
                out=trb[:, d_ * 128:(d_ + 1) * 128],
                in_=pnb[:, d_ * 128:(d_ + 1) * 128], identity=IDB[:, :]),
                deps=[nrm, st["pte"]] if d_ == 0 else [nrm])
            tts.append(t)
        st["tr"] = tts[-1]
        pte = VV(lambda v: v.tensor_copy(out=cfg["ptb"][p][:, 0:wk],
                                         in_=trb[:, 0:wk]),
                 deps=tts + [st["avm"]])
        st["pte"] = pte
        cx["pte"] = pte

    def attn_s4(cx):
        p = cx["n"] % 2
        st = att_st[p]
        cfg = cx["cfg"]
        ptb = cfg["ptb"][p]
        avp = cfg["avs"][p]
        avm = None
        for d_ in range(cx["nt"]):
            avm = TT(lambda t_, d_=d_: t_.matmul(
                out=avp, lhsT=cx["vfn"](d_), rhs=ptb[:, d_ * 128:(d_ + 1) * 128],
                start=(d_ == 0), stop=(d_ == cx["nt"] - 1)),
                deps=[cx["pte"], st["oev"]] if d_ == 0 else [])
        st["avm"] = avm
        oev = VV(lambda v: v.tensor_copy(out=cx["oap"], in_=avp), deps=[avm])
        st["oev"] = oev

    def attn_drive(tiles):
        ctxs = []
        N = len(tiles)
        for i in range(N + 4):
            if i >= 4:
                attn_s4(ctxs[i - 4])
            if 3 <= i < N + 3:
                attn_s3(ctxs[i - 3])
            if 2 <= i < N + 2:
                attn_s2b(ctxs[i - 2])
            if 1 <= i < N + 1:
                attn_s2a(ctxs[i - 1])
            if i < N:
                ctxs.append(attn_s1(*tiles[i]))

    xln_ap = lambda dc, c0, w: AR1B[:, dc * 3584 + c0: dc * 3584 + c0 + w]
    xbuf = [AR1[:, 14336:17920], AR1[:, 17920:21504]]

    def srcA_final(dc):
        par = dc % 2
        tk = P.emit("gpsimd", lambda g, dc=dc, par=par: g.dma_start(
            out=xbuf[par][:, :], in_=h_xin[dc * 128:(dc + 1) * 128, :]),
            deps=[src_cons[par]], sem="xs" + str(par))
        return xbuf[par], tk

    stok = ln_stats(srcA_final, NTOT, AR2[:, 0:3584], AR2[:, 3584:7168])
    P.barrier()
    MB = AR2[:, 0:3584]
    RB = AR2[:, 3584:7168]
    for tt in range(7):
        bcast_stats(tt * 512, 512, stok, dst1=MB[:, tt * 512:(tt + 1) * 512],
                    dst2=RB[:, tt * 512:(tt + 1) * 512])
    P.barrier()
    for dc in range(NCH):
        ap, tk = srcA_final(dc)
        q0 = (dc % 2) * 512
        for tt in range(7):
            c0 = tt * 512
            VV(lambda v, ap=ap, c0=c0, q0=q0: v.tensor_tensor(
                out=QS3[:, q0:q0 + 512], in0=ap[:, c0:c0 + 512],
                in1=MB[:, c0:c0 + 512], op=ALU.add), deps=[tk])
            lastap = VV(lambda v, dc=dc, c0=c0, q0=q0: v.tensor_tensor(
                out=xln_ap(dc, c0, 512), in0=QS3[:, q0:q0 + 512],
                in1=RB[:, c0:c0 + 512], op=ALU.mult))
        src_cons[dc % 2] = lastap
    P.barrier()

    def o_ap(ch):
        if ch < 6:
            return AR1B[:, 28672 + ch * 3072: 28672 + (ch + 1) * 3072]
        return AR2B[:, 13312 + (ch - 6) * 3072: 13312 + (ch - 5) * 3072]

    v_ap = lambda kt: AR3B[:, kt * 256:(kt + 1) * 256]

    for g in range(4):
        if g > 0:
            P.barrier()
        cw = load_wslice(lambda g=g: h_w["q0"][:, g * 256:(g + 1) * 256].rearrange(
            "(c p) m -> p c m", p=128), g % 2, 2048)
        for cc in range(2):
            for tt in range(6):
                ps = (pA, pB, pC, pD)[(cc * 6 + tt) % 4]
                for dc in range(NCH):
                    t = TT(lambda t_, dc=dc, cc=cc, tt=tt, ps=ps: t_.matmul(
                        out=ps[:, 0:512],
                        lhsT=WBF[:, dc * 256 + cc * 128: dc * 256 + cc * 128 + 128],
                        rhs=xln_ap(dc, 512 + tt * 512, 512),
                        start=(dc == 0), stop=(dc == NCH - 1)),
                        deps=[cw, pfree[id(ps)]] if dc == 0 else [])
                ev = SS(lambda s, cc=cc, tt=tt, ps=ps: s.activation(
                    out=AR2B[:, cc * 3072 + tt * 512: cc * 3072 + (tt + 1) * 512],
                    in_=ps[:, 0:512], func=ACT.Copy, scale=0.125), deps=[t])
                pfree[id(ps)] = ev
        wbf_last[0] = wbf_last[1] = SSNAP()
        wbf_tlast[0] = wbf_tlast[1] = TSNAP()
        cw = load_wslice(lambda g=g: h_w["k0"][:, g * 256:(g + 1) * 256].rearrange(
            "(c p) m -> p c m", p=128), (g + 1) % 2, 2048)
        for cc in range(2):
            for tt in range(7):
                ps = (pA, pB, pC, pD)[(cc * 7 + tt) % 4]
                for dc in range(NCH):
                    t = TT(lambda t_, dc=dc, cc=cc, tt=tt, ps=ps: t_.matmul(
                        out=ps[:, 0:512],
                        lhsT=WBF[:, dc * 256 + cc * 128: dc * 256 + cc * 128 + 128],
                        rhs=xln_ap(dc, tt * 512, 512),
                        start=(dc == 0), stop=(dc == NCH - 1)),
                        deps=[cw, pfree[id(ps)]] if dc == 0 else [])
                ev = SS(lambda s, cc=cc, tt=tt, ps=ps: s.copy(
                    out=AR2B[:, 6144 + cc * 3584 + tt * 512:
                             6144 + cc * 3584 + (tt + 1) * 512],
                    in_=ps[:, 0:512]), deps=[t])
                pfree[id(ps)] = ev
        wbf_last[0] = wbf_last[1] = SSNAP()
        wbf_tlast[0] = wbf_tlast[1] = TSNAP()
        cw = load_wslice(lambda g=g: h_w["v0"][:, g * 256:(g + 1) * 256].rearrange(
            "(c p) m -> p c m", p=128), g % 2, 2048)
        for kt in range(28):
            ps = (pA, pB, pC, pD)[kt % 4]
            for dc in range(NCH):
                t = TT(lambda t_, dc=dc, kt=kt, ps=ps: t_.matmul(
                    out=ps[:, 0:256], lhsT=xln_ap(dc, kt * 128, 128),
                    rhs=WBF[:, dc * 256:(dc + 1) * 256],
                    start=(dc == 0), stop=(dc == NCH - 1)),
                    deps=[cw, pfree[id(ps)]] if dc == 0 else [])
            ev = SS(lambda s, kt=kt, ps=ps: s.copy(out=v_ap(kt), in_=ps[:, 0:256]),
                    deps=[t])
            pfree[id(ps)] = ev
        wbf_last[0] = wbf_last[1] = SSNAP()
        wbf_tlast[0] = wbf_tlast[1] = TSNAP()

        proj_done = [TSNAP(), SSNAP()]
        attn_reset()
        BCMB = AR2B[:, 19456:21504]
        cfgL0 = dict(sA=(pC, pE), sB=(pA, pB),
                     ea=(QS3[:, 0:512], QS1[:, 0:512]),
                     eb=(QS3[:, 512:1024], QS2[:, 0:512]),
                     pnb=(PNB[:, 0:640], AR2B[:, 21504:22144]),
                     ptb=(PTB[:, 0:640], AR2B[:, 22144:22784]),
                     trb=(pFB[:, 0:1024], pDB[:, 0:1024]),
                     avs=(pH[0:64, 0:128], pH[64:128, 0:128]))
        tiles = []
        ntile = 0
        for hl in range(4):
            cc = hl // 2
            p0 = 64 * (hl % 2)
            ch = 2 * g + cc
            qh = lambda c0, w, cc=cc, p0=p0: AR2B[p0:p0 + 64,
                                                  cc * 3072 + c0: cc * 3072 + c0 + w]
            kh = lambda c0, w, cc=cc, p0=p0: AR2B[p0:p0 + 64,
                                                  6144 + cc * 3584 + c0:
                                                  6144 + cc * 3584 + c0 + w]
            for qt in range(16):
                ma = BCMB[:, qt * 512:(qt + 1) * 512] if qt <= 3 else MQ[:, 512:1024]
                tiles.append((ntile, qh(qt * 128, 128),
                              lambda c0, w, qt=qt, kh=kh: kh(qt * 128 + c0, w),
                              5, ma, MQ[:, 1024:1152],
                              lambda d, qt=qt, hl=hl: v_ap(qt + d)[:, hl * 64:(hl + 1) * 64],
                              o_ap(ch)[p0:p0 + 64, qt * 128:(qt + 1) * 128], cfgL0))
                ntile += 1
            for qt in range(8):
                nt = min(qt, 4) + 1
                j0 = max(0, qt - 4)
                wk = nt * 128
                if nt == 5:
                    ma, mb = MQ[:, 512:1024], MQ[:, 1024:1152]
                else:
                    ma, mb = MQ[:, 1152 - wk:1152], None
                tiles.append((ntile, qh(2048 + qt * 128, 128),
                              lambda c0, w, j0=j0, kh=kh: kh(2560 + j0 * 128 + c0, w),
                              nt, ma, mb,
                              lambda d, j0=j0, hl=hl: v_ap(20 + j0 + d)[:, hl * 64:(hl + 1) * 64],
                              o_ap(ch)[p0:p0 + 64, 2048 + qt * 128: 2048 + (qt + 1) * 128],
                              cfgL0))
                ntile += 1
        for p_ in (0, 1):
            att_st[p_]["exp"] = proj_done[1]
            att_st[p_]["rmx"] = proj_done[0]
        attn_drive(tiles)
    P.barrier()

    def o2_ap(ch):
        if ch < 4:
            return AR2B[:, ch * 3072:(ch + 1) * 3072]
        if ch < 6:
            return AR2B[:, 19456 + (ch - 4) * 3072: 19456 + (ch - 3) * 3072]
        return AR2B[:, 13312 + (ch - 6) * 3072: 13312 + (ch - 5) * 3072]

    def wo2_ap(dk):
        if dk < 3:
            return AR2B[:, 25600 + dk * 1024: 25600 + (dk + 1) * 1024]
        return AR3B[:, (dk - 3) * 1024: (dk - 2) * 1024]

    for ch in range(6):
        VV(lambda v, ch=ch: v.tensor_copy(out=o2_ap(ch), in_=o_ap(ch)))
    omove = VSNAP()
    for i in range(4):
        par = i % 2
        d = P.emit("sync", lambda y, i=i, par=par: y.dma_start(
            out=WS[par][:, 0:2048],
            in_=h_w["o0"][256 * i:256 * (i + 1), :].rearrange(
                "(c p) m -> p c m", p=128)), deps=[ws_last[par], omove],
            sem="wy" + str(par))
        c1 = SS(lambda s, i=i, par=par: s.copy(out=wo2_ap(2 * i),
                                               in_=WS[par][:, 0:1024]), deps=[d])
        c2 = SS(lambda s, i=i, par=par: s.copy(out=wo2_ap(2 * i + 1),
                                               in_=WS[par][:, 1024:2048]), deps=[d])
        ws_last[par] = c2
    P.barrier()
    xr = [QS1, QS2]
    xr_last = [None, None]
    for tt in range(6):
        for oc in range(NCH):
            ps = (pA, pB, pC, pD)[(tt * 8 + oc) % 4]
            par = oc % 2
            dx = P.emit("gpsimd", lambda g, oc=oc, tt=tt, par=par: g.dma_start(
                out=xr[par][:, 0:512],
                in_=h_xin[oc * 128:(oc + 1) * 128,
                          512 + tt * 512: 512 + (tt + 1) * 512]),
                deps=[xr_last[par]], sem="xr" + str(par))
            for dk in range(NCH):
                t = TT(lambda t_, dk=dk, oc=oc, tt=tt, ps=ps: t_.matmul(
                    out=ps[:, 0:512], lhsT=wo2_ap(dk)[:, oc * 128:(oc + 1) * 128],
                    rhs=o2_ap(dk)[:, tt * 512:(tt + 1) * 512],
                    start=(dk == 0), stop=(dk == NCH - 1)),
                    deps=[pfree[id(ps)]] if dk == 0 else [])
            av = VV(lambda v, oc=oc, tt=tt, ps=ps, par=par: v.tensor_tensor(
                out=AR1[:, oc * 3072 + tt * 512: oc * 3072 + (tt + 1) * 512],
                in0=ps[:, 0:512], in1=xr[par][:, 0:512], op=ALU.add),
                deps=[t, dx])
            pfree[id(ps)] = av
            xr_last[par] = av
    P.barrier()
    if dbg == "x1":
        for dc in range(NCH):
            DG(lambda g, dc=dc: g.dma_start(
                out=h_dbg[dc * 128:(dc + 1) * 128, :],
                in_=AR1[:, dc * 3072:(dc + 1) * 3072]))

    stok = ln_stats(lambda dc: (AR1[:, dc * 3072:(dc + 1) * 3072], None), NOE,
                    AR2[:, 0:3072], AR2[:, 3072:6144])
    P.barrier()
    xq8_ap = lambda dc, c0, w: AR2B[:, dc * 3072 + c0: dc * 3072 + c0 + w]
    for tt in range(6):
        c0 = tt * 512
        btok = bcast_stats(c0, 512, stok)
        for dc in range(NCH):
            q0 = (dc % 2) * 512
            VV(lambda v, dc=dc, c0=c0, q0=q0: v.tensor_tensor(
                out=QS3[:, q0:q0 + 512],
                in0=AR1[:, dc * 3072 + c0: dc * 3072 + c0 + 512],
                in1=QS1[:, 0:512], op=ALU.add), deps=btok)
            VV(lambda v, q0=q0: v.tensor_tensor(out=QS3[:, q0:q0 + 512],
                                                in0=QS3[:, q0:q0 + 512],
                                                in1=QS2[:, 0:512], op=ALU.mult))
            VV(lambda v, q0=q0: v.tensor_copy(out=Q8[:, q0:q0 + 512],
                                              in_=QS3[:, q0:q0 + 512]))
            VV(lambda v, q0=q0: v.tensor_copy(out=I8T[:, q0:q0 + 512],
                                              in_=Q8[:, q0:q0 + 512].bitcast(I8)))
            VV(lambda v, dc=dc, c0=c0, q0=q0: v.tensor_copy(
                out=xq8_ap(dc, c0, 512), in_=I8T[:, q0:q0 + 512].bitcast(F8)))
    P.barrier()
    vlast_d = None
    vl2_d = [None, None]
    for oc in range(NCH):
        par = oc % 2
        d = P.emit("sync", lambda y, oc=oc, par=par: y.dma_start(
            out=WS[par][:, 0:1024],
            in_=h_w["f0"][:, oc * 128:(oc + 1) * 128].rearrange(
                "(c p) m -> p c m", p=128)), deps=[ws_last[par]],
            sem="wy" + str(par))
        q1 = SS(lambda s, par=par: s.copy(out=Q8[:, 0:1024], in_=WS[par][:, 0:1024]),
                deps=[d, vlast_d, VSNAP()])
        q2 = SS(lambda s: s.copy(out=I8T[:, 0:1024], in_=Q8[:, 0:1024].bitcast(I8)))
        q3 = SS(lambda s: s.copy(out=WBF[:, 0:1024], in_=I8T[:, 0:1024].bitcast(F8)),
                deps=[wbf_last[0], wbf_tlast[0]])
        ws_last[par] = q3
        for tt in range(6):
            ps = (pA, pB, pC, pD)[(oc * 6 + tt) % 4]
            for dc in range(NCH):
                t = TT(lambda t_, dc=dc, tt=tt, ps=ps: t_.matmul(
                    out=ps[:, 0:512], lhsT=WBF[:, dc * 128:(dc + 1) * 128],
                    rhs=xq8_ap(dc, tt * 512, 512),
                    start=(dc == 0), stop=(dc == NCH - 1)),
                    deps=[q3, pfree[id(ps)]] if dc == 0 else [])
            fp_ = (oc * 6 + tt) % 2
            f16d = F16T[:, 0:512] if fp_ == 0 else I8T.bitcast(F16)[:, 0:512]
            i16d = I16T[:, 0:512] if fp_ == 0 else Q8.bitcast(I16)[:, 0:512]
            f1 = SS(lambda s, ps=ps, f16d=f16d: s.copy(out=f16d, in_=ps[:, 0:512]),
                    deps=[t, vl2_d[fp_]])
            f2 = SS(lambda s, f16d=f16d, i16d=i16d: s.copy(
                out=i16d, in_=f16d.bitcast(I16)))
            pfree[id(ps)] = f2
            vlast_d = VV(lambda v, oc=oc, tt=tt, i16d=i16d: v.scalar_tensor_tensor(
                out=AR1[:, oc * 3072 + tt * 512: oc * 3072 + (tt + 1) * 512],
                in0=i16d.bitcast(F16),
                scalar=B0[:, oc:oc + 1],
                in1=AR1[:, oc * 3072 + tt * 512: oc * 3072 + (tt + 1) * 512],
                op0=ALU.add, op1=ALU.add), deps=[f2])
            vl2_d[fp_] = vlast_d
        wbf_last[0] = TSNAP()
    P.barrier()
    if dbg == "x2":
        for dc in range(NCH):
            DG(lambda g, dc=dc: g.dma_start(
                out=h_dbg[dc * 128:(dc + 1) * 128, :],
                in_=AR1[:, dc * 3072:(dc + 1) * 3072]))

    stok = ln_stats(lambda dc: (AR1[:, dc * 3072 + 2048:(dc + 1) * 3072], None),
                    NEXT_, AR2[:, 8192:9216], AR2[:, 9216:10240])
    P.barrier()
    xln2e = lambda dc, c0, w: AR2B[:, dc * 1024 + c0: dc * 1024 + c0 + w]
    for tt in range(2):
        c0 = tt * 512
        btok = bcast_stats(c0, 512, stok)
        for dc in range(NCH):
            VV(lambda v, dc=dc, c0=c0: v.tensor_tensor(
                out=QS3[:, 0:512],
                in0=AR1[:, dc * 3072 + 2048 + c0: dc * 3072 + 2048 + c0 + 512],
                in1=QS1[:, 0:512], op=ALU.add), deps=btok)
            VV(lambda v, dc=dc, c0=c0: v.tensor_tensor(
                out=xln2e(dc, c0, 512), in0=QS3[:, 0:512], in1=QS2[:, 0:512],
                op=ALU.mult))
    P.barrier()
    k1_ap = lambda ch, c0, w: AR2B[:, 8192 + ch * 1024 + c0: 8192 + ch * 1024 + c0 + w]
    v1_ap = lambda kt: AR2B[:, 16384 + kt * 1024: 16384 + (kt + 1) * 1024]
    for ch in range(NCH):
        wb0 = (ch % 2) * 1024
        cw = load_wslice(lambda ch=ch: h_w["k1"][:, ch * 128:(ch + 1) * 128].rearrange(
            "(c p) m -> p c m", p=128), ch % 2, 1024, wb0=wb0)
        for tt in range(2):
            ps = (pA, pB, pC, pD)[(ch * 2 + tt) % 4]
            for dc in range(NCH):
                t = TT(lambda t_, dc=dc, tt=tt, ps=ps, wb0=wb0: t_.matmul(
                    out=ps[:, 0:512], lhsT=WBF[:, wb0 + dc * 128: wb0 + (dc + 1) * 128],
                    rhs=xln2e(dc, tt * 512, 512),
                    start=(dc == 0), stop=(dc == NCH - 1)),
                    deps=[cw, pfree[id(ps)]] if dc == 0 else [])
            ev = SS(lambda s, ch=ch, tt=tt, ps=ps: s.copy(
                out=k1_ap(ch, tt * 512, 512), in_=ps[:, 0:512]), deps=[t])
            pfree[id(ps)] = ev
        wbf_last[ch % 2] = SSNAP()
        wbf_tlast[ch % 2] = TSNAP()
    for vg in range(4):
        cw = load_wslice(lambda vg=vg: h_w["v1"][:, vg * 256:(vg + 1) * 256].rearrange(
            "(c p) m -> p c m", p=128), vg % 2, 2048)
        for kt in range(8):
            ps = (pA, pB, pC, pD)[(vg * 8 + kt) % 4]
            for dc in range(NCH):
                t = TT(lambda t_, dc=dc, kt=kt, ps=ps: t_.matmul(
                    out=ps[:, 0:256], lhsT=xln2e(dc, kt * 128, 128),
                    rhs=WBF[:, dc * 256:(dc + 1) * 256],
                    start=(dc == 0), stop=(dc == NCH - 1)),
                    deps=[cw, pfree[id(ps)]] if dc == 0 else [])
            ev = SS(lambda s, kt=kt, vg=vg, ps=ps: s.copy(
                out=v1_ap(kt)[:, vg * 256:(vg + 1) * 256], in_=ps[:, 0:256]),
                deps=[t])
            pfree[id(ps)] = ev
        wbf_last[0] = wbf_last[1] = SSNAP()
        wbf_tlast[0] = wbf_tlast[1] = TSNAP()
    P.barrier()
    stok = ln_stats(lambda dc: (AR1[:, dc * 3072: dc * 3072 + 2048], None), NOWN,
                    AR2[:, 0:2048], AR2[:, 2048:4096], rows=WS[0])
    P.barrier()

    def xo_ap(dc, c0, w):
        if dc < 4:
            return AR2B[:, dc * 2048 + c0: dc * 2048 + c0 + w]
        if dc < 6:
            return AR2B[:, 24576 + (dc - 4) * 2048 + c0:
                        24576 + (dc - 4) * 2048 + c0 + w]
        return AR3B[:, (dc - 6) * 2048 + c0: (dc - 6) * 2048 + c0 + w]

    for tt in range(4):
        c0 = tt * 512
        btok = bcast_stats(c0, 512, stok, rows=WS[0])
        for dc in range(NCH):
            VV(lambda v, dc=dc, c0=c0: v.tensor_tensor(
                out=QS3[:, 0:512], in0=AR1[:, dc * 3072 + c0: dc * 3072 + c0 + 512],
                in1=QS1[:, 0:512], op=ALU.add), deps=btok)
            VV(lambda v, dc=dc, c0=c0: v.tensor_tensor(
                out=xo_ap(dc, c0, 512), in0=QS3[:, 0:512], in1=QS2[:, 0:512],
                op=ALU.mult))
    ws_last[0] = TSNAP()
    P.barrier()
    q1b_ap = lambda ch, c0, w: AR1B[:, (ch * 3072 + 2048) * 2 + c0:
                                    (ch * 3072 + 2048) * 2 + c0 + w]
    for ch in range(NCH):
        _oc4 = ch
        wb0 = (ch % 2) * 1024
        cw = load_wslice(lambda ch=ch: h_w["q1"][:, ch * 128:(ch + 1) * 128].rearrange(
            "(c p) m -> p c m", p=128), ch % 2, 1024, wb0=wb0)
        for tt in range(4):
            ps = (pA, pB, pC, pD)[(_oc4 * 4 + tt) % 4]
            for dc in range(NCH):
                t = TT(lambda t_, dc=dc, tt=tt, ps=ps, wb0=wb0: t_.matmul(
                    out=ps[:, 0:512], lhsT=WBF[:, wb0 + dc * 128: wb0 + (dc + 1) * 128],
                    rhs=xo_ap(dc, tt * 512, 512),
                    start=(dc == 0), stop=(dc == NCH - 1)),
                    deps=[cw, pfree[id(ps)]] if dc == 0 else [])
            ev = SS(lambda s, ch=ch, tt=tt, ps=ps: s.activation(
                out=q1b_ap(ch, tt * 512, 512), in_=ps[:, 0:512], func=ACT.Copy,
                scale=0.125), deps=[t])
            pfree[id(ps)] = ev
        wbf_last[ch % 2] = SSNAP()
        wbf_tlast[ch % 2] = TSNAP()
    P.barrier()
    attn_reset()
    cfgL1 = dict(sA=(pC, pA), sB=(pD, pB),
                 ea=(QS3[:, 0:512], QS1[:, 0:512]),
                 eb=(QS3[:, 512:1024], QS2[:, 0:512]),
                 pnb=(PNB[:, 0:1024], AR3B[:, 4096:5120]),
                 ptb=(PTB[:, 0:1024], AR3B[:, 5120:6144]),
                 trb=(pFB[:, 0:1024], pEB[:, 0:1024]),
                 avs=(pH[0:64, 0:128], pH[64:128, 0:128]))
    tiles = []
    ntile = 0
    for h in range(16):
        ch = h // 2
        p0 = 64 * (h % 2)
        for qt in range(16):
            nt = (qt % 8) + 1
            wk = nt * 128
            if wk <= 512:
                ma, mb = MQ[:, 1152 - wk:1152], None
            else:
                ma, mb = None, MQ[:, 1152 - (wk - 512):1152]
            tiles.append((ntile,
                          q1b_ap(ch, qt * 128, 128)[p0:p0 + 64, :],
                          lambda c0, w, ch=ch, p0=p0: k1_ap(ch, c0, w)[p0:p0 + 64, :],
                          nt, ma, mb,
                          lambda d, h=h: v1_ap(d)[:, h * 64:(h + 1) * 64],
                          xo_ap(ch, qt * 128, 128)[p0:p0 + 64, :], cfgL1))
            ntile += 1
    attn_drive(tiles)
    P.barrier()
    for oc in range(NCH):
        _oc4 = oc
        wb0 = (oc % 2) * 1024
        cw = load_wslice(lambda oc=oc: h_w["o1"][:, oc * 128:(oc + 1) * 128].rearrange(
            "(c p) m -> p c m", p=128), oc % 2, 1024, wb0=wb0)
        for tt in range(4):
            ps = (pA, pB, pC, pD)[(_oc4 * 4 + tt) % 4]
            for dk in range(NCH):
                t = TT(lambda t_, dk=dk, tt=tt, ps=ps, wb0=wb0: t_.matmul(
                    out=ps[:, 0:512], lhsT=WBF[:, wb0 + dk * 128: wb0 + (dk + 1) * 128],
                    rhs=xo_ap(dk, tt * 512, 512),
                    start=(dk == 0), stop=(dk == NCH - 1)),
                    deps=[cw, pfree[id(ps)]] if dk == 0 else [])
            av = VV(lambda v, oc=oc, tt=tt, ps=ps: v.tensor_tensor(
                out=AR1[:, oc * 3072 + tt * 512: oc * 3072 + (tt + 1) * 512],
                in0=AR1[:, oc * 3072 + tt * 512: oc * 3072 + (tt + 1) * 512],
                in1=ps[:, 0:512], op=ALU.add), deps=[t])
            pfree[id(ps)] = av
        wbf_last[oc % 2] = VSNAP()
        wbf_tlast[oc % 2] = TSNAP()
    P.barrier()
    if dbg == "x3":
        for dc in range(NCH):
            DG(lambda g, dc=dc: g.dma_start(
                out=h_dbg[dc * 128:(dc + 1) * 128, :],
                in_=AR1[:, dc * 3072:(dc + 1) * 3072]))

    stok = ln_stats(lambda dc: (AR1[:, dc * 3072: dc * 3072 + 2048], None), NOWN,
                    AR2[:, 8192:10240], AR2[:, 10240:12288])
    P.barrier()
    xln3 = lambda dc, c0, w: AR2B[:, dc * 2048 + c0: dc * 2048 + c0 + w]
    for tt in range(4):
        c0 = tt * 512
        btok = bcast_stats(c0, 512, stok)
        for dc in range(NCH):
            VV(lambda v, dc=dc, c0=c0: v.tensor_tensor(
                out=QS3[:, 0:512], in0=AR1[:, dc * 3072 + c0: dc * 3072 + c0 + 512],
                in1=QS1[:, 0:512], op=ALU.add), deps=btok)
            VV(lambda v, dc=dc, c0=c0: v.tensor_tensor(
                out=xln3(dc, c0, 512), in0=QS3[:, 0:512], in1=QS2[:, 0:512],
                op=ALU.mult))
    P.barrier()
    dwg = P.emit("sync", lambda y: y.dma_start(
        out=WS[0][:, 0:64], in_=h_wg[:, :].rearrange("(c p) m -> p c m", p=128)),
        deps=[ws_last[0]], sem="wy0")
    cwg = SS(lambda s: s.copy(out=WBF[:, 0:64], in_=WS[0][:, 0:64]),
             deps=[dwg, wbf_last[0], wbf_tlast[0]])
    ws_last[0] = cwg
    for tt in range(4):
        for dc in range(NCH):
            t = TT(lambda t_, dc=dc, tt=tt: t_.matmul(
                out=pC[0:8, 0:512], lhsT=WBF[:, dc * 8:(dc + 1) * 8],
                rhs=xln3(dc, tt * 512, 512),
                start=(dc == 0), stop=(dc == NCH - 1)),
                deps=[cwg, pfree[id(pC)]] if dc == 0 else [])
        ev = VV(lambda v, tt=tt: v.tensor_scalar(
            out=AR3[32:40, tt * 512:(tt + 1) * 512], in0=pC[0:8, 0:512],
            scalar1=BG[0:8, 0:1], scalar2=None, op0=ALU.add), deps=[t])
        pfree[id(pC)] = ev
    lg_done = SSNAP()
    for ts in range(16):
        t = TT(lambda t_, ts=ts: t_.transpose(
            out=pG[:, 0:8], in_=AR3[32:40, ts * 128:(ts + 1) * 128],
            identity=IDF[32:40, 32:40]), deps=[lg_done, pfree[id(pG)]])
        ev = SS(lambda s, ts=ts: s.copy(out=QS3[:, 128 + ts * 8: 128 + (ts + 1) * 8],
                                        in_=pG[:, 0:8]), deps=[t])
        pfree[id(pG)] = ev
        r1 = VV(lambda v, ts=ts: v.tensor_reduce(
            out=RMAX[:, 0:1], in_=QS3[:, 128 + ts * 8: 128 + (ts + 1) * 8], axis=AXX,
            op=ALU.max, negate=True), deps=[ev, SSNAP()])
        r2 = SS(lambda s, ts=ts: s.activation(
            out=QS3[:, ts * 8:(ts + 1) * 8], in_=QS3[:, 128 + ts * 8: 128 + (ts + 1) * 8],
            func=ACT.Exp, bias=RMAX[:, 0:1], accum_out=RSUM[:, 0:1]), deps=[r1])
        VV(lambda v, ts=ts: v.reciprocal(out=QS3[:, 272 + ts: 273 + ts], in_=RSUM[:, 0:1]),
           deps=[r2])
    sgate = SSNAP()
    VV(lambda v: v.tensor_scalar(out=QS3[:, 0:128], in0=QS3[:, 0:128], scalar1=1.0,
                                 scalar2=None, op0=ALU.is_ge), deps=[sgate])
    VV(lambda v: v.tensor_copy(out=QS3[:, 256:272],
                               in_=bass.AP(QS3, 0, [[1024, 128], [8, 16]])))
    for e in range(1, 8):
        ge = bass.AP(QS3, e, [[1024, 128], [8, 16]])
        VV(lambda v: v.tensor_scalar(out=QS1[:, 0:16], in0=QS3[:, 256:272],
                                     scalar1=-1.0, scalar2=1.0,
                                     op0=ALU.mult, op1=ALU.add))
        VV(lambda v, ge=ge: v.tensor_tensor(out=ge, in0=ge, in1=QS1[:, 0:16],
                                            op=ALU.mult))
        VV(lambda v, ge=ge: v.tensor_tensor(out=QS3[:, 256:272], in0=QS3[:, 256:272],
                                            in1=ge, op=ALU.add))
    for e in range(8):
        ge = bass.AP(QS3, e, [[1024, 128], [8, 16]])
        VV(lambda v, ge=ge: v.tensor_tensor(out=ge, in0=ge, in1=QS3[:, 272:288],
                                            op=ALU.mult))
    gt_done = VSNAP()
    P.barrier()

    def acc_ap(ts, c0, w):
        if ts < 6:
            return AR2[:, 8192 + ts * 1024 + c0: 8192 + ts * 1024 + c0 + w]
        if ts < 14:
            base = (ts - 6) * 3072 + 2048
            return AR1[:, base + c0: base + c0 + w]
        return AR3[:, (ts - 14) * 1024 + c0: (ts - 14) * 1024 + c0 + w]

    db2 = P.emit("sync", lambda y: y.dma_start(out=WS[1][0:8, 0:1024], in_=h_b2[:, :]),
                 deps=[ws_last[1]], sem="wy1")
    qs1g_last = None
    for ts in range(16):
        tg = TT(lambda t_, ts=ts: t_.transpose(
            out=pG[0:8, 0:128], in_=QS3[:, ts * 8:(ts + 1) * 8], identity=IDF[:, :]),
            deps=[gt_done, pfree[id(pG)]])
        eg = SS(lambda s: s.copy(out=QS1[0:8, 0:128], in_=pG[0:8, 0:128]),
                deps=[tg, qs1g_last, VSNAP()])
        pfree[id(pG)] = eg
        for oh in range(2):
            ps = pC if oh == 0 else pD
            psx = pE if oh == 0 else pF
            mb = TT(lambda t_, oh=oh, ps=ps: t_.matmul(
                out=ps[:, 0:512], lhsT=QS1[0:8, 0:128],
                rhs=WS[1][0:8, oh * 512:(oh + 1) * 512], start=True, stop=True),
                deps=[eg, db2, pfree[id(ps)]])
            qs1g_last = mb
            trs = []
            for k in range(4):
                dc = oh * 4 + k
                t = TT(lambda t_, dc=dc, ts=ts, k=k, psx=psx: t_.transpose(
                    out=psx[:, k * 128:(k + 1) * 128],
                    in_=AR1[:, dc * 3072 + ts * 128: dc * 3072 + (ts + 1) * 128],
                    identity=IDF[:, :]),
                    deps=[pfree[id(psx)]] if k == 0 else [])
                trs.append(t)
            eb2 = SS(lambda s, ps=ps: s.copy(out=QS3[:, 512:1024], in_=ps[:, 0:512]),
                     deps=[mb, qs1g_last, VSNAP()])
            av = VV(lambda v, ts=ts, oh=oh, psx=psx: v.tensor_tensor(
                out=acc_ap(ts, oh * 512, 512), in0=psx[:, 0:512], in1=QS3[:, 512:1024],
                op=ALU.add), deps=trs + [eb2])
            pfree[id(ps)] = av
            pfree[id(psx)] = av
    P.barrier()

    def w1bf_ap(dc):
        r = dc // 4
        return AR1B[:, r * 6144 + (dc % 4) * 1024: r * 6144 + (dc % 4) * 1024 + 1024]

    def w2bf_ap(ms):
        r = 2 + ms // 4
        return AR1B[:, r * 6144 + (ms % 4) * 1024: r * 6144 + (ms % 4) * 1024 + 1024]

    def h_ap(ms, tq):
        idx = ms * 4 + tq
        r = 4 + idx // 8
        return AR1B[:, r * 6144 + (idx % 8) * 512: r * 6144 + (idx % 8) * 512 + 512]

    w1_last = None
    w2_last = None

    def emit_w1_loads(e, mc):
        toks = []
        for i in range(4):
            par = i % 2
            d = P.emit("sync", lambda y, e=e, mc=mc, i=i, par=par: y.dma_start(
                out=WS[par][:, 0:2048],
                in_=h_w1[e, 256 * i:256 * (i + 1),
                         mc * 1024:(mc + 1) * 1024].rearrange(
                    "(c p) m -> p c m", p=128)),
                deps=[ws_last[par], w1_last], sem="wy" + str(par))
            dst0 = (2 * i // 4) * 6144 + ((2 * i) % 4) * 1024
            c = SS(lambda s, par=par, dst0=dst0: s.copy(
                out=AR1B[:, dst0:dst0 + 2048], in_=WS[par][:, 0:2048]), deps=[d])
            ws_last[par] = c
            toks.append(c)
        return toks

    def emit_w2_loads(e, mc):
        toks = []
        for i in range(4):
            par = i % 2
            d = P.emit("sync", lambda y, e=e, mc=mc, i=i, par=par: y.dma_start(
                out=WS[par][:, 0:2048],
                in_=h_w2[e, mc * 1024 + 256 * i: mc * 1024 + 256 * (i + 1),
                         :].rearrange("(c p) m -> p c m", p=128)),
                deps=[ws_last[par], w2_last], sem="wy" + str(par))
            dst0 = (2 + (2 * i) // 4) * 6144 + ((2 * i) % 4) * 1024
            c = VV(lambda v, par=par, dst0=dst0: v.tensor_copy(
                out=AR1B[:, dst0:dst0 + 2048], in_=WS[par][:, 0:2048]), deps=[d])
            ws_last[par] = c
            toks.append(c)
        return toks

    emcs = [(e, mc) for e in range(8) for mc in range(4)]
    c1toks = emit_w1_loads(0, 0)
    c2toks = emit_w2_loads(0, 0)
    for bi, (e, mc) in enumerate(emcs):
        h_last = None
        for ms in range(8):
            for tq in range(4):
                ps = (pA, pB, pF)[(ms * 4 + tq) % 3]
                dep0 = c1toks + [pfree[id(ps)], w2_last]
                for dc in range(NCH):
                    t = TT(lambda t_, dc=dc, ms=ms, tq=tq, ps=ps: t_.matmul(
                        out=ps[:, 0:512],
                        lhsT=w1bf_ap(dc)[:, ms * 128:(ms + 1) * 128],
                        rhs=xln3(dc, tq * 512, 512),
                        start=(dc == 0), stop=(dc == NCH - 1)),
                        deps=dep0 if dc == 0 else [])
                ev = SS(lambda s, e=e, mc=mc, ms=ms, tq=tq, ps=ps: s.activation(
                    out=h_ap(ms, tq), in_=ps[:, 0:512], func=ACT.Relu,
                    bias=B1[:, e * 32 + mc * 8 + ms: e * 32 + mc * 8 + ms + 1]),
                    deps=[t, w2_last] if (ms == 0 and tq == 0) else [t])
                pfree[id(ps)] = ev
                h_last = ev
        w1_last = h_last
        if bi + 1 < len(emcs):
            c1toks = emit_w1_loads(*emcs[bi + 1])
        psl = [pC, pD, pE]
        for ts in range(16):
            for oh in range(2):
                ps = psl[(ts * 2 + oh) % 3]
                dep0 = c2toks + [pfree[id(ps)], h_last]
                for ms in range(8):
                    t = TT(lambda t_, ms=ms, ts=ts, oh=oh, ps=ps: t_.matmul(
                        out=ps[:, 0:512],
                        lhsT=h_ap(ms, ts // 4)[:, (ts % 4) * 128:
                                               (ts % 4 + 1) * 128],
                        rhs=w2bf_ap(ms)[:, oh * 512:(oh + 1) * 512],
                        start=(ms == 0), stop=(ms == 7)),
                        deps=dep0 if ms == 0 else [])
                w2_last = t
                av = VV(lambda v, e=e, ts=ts, oh=oh, ps=ps: v.scalar_tensor_tensor(
                    out=acc_ap(ts, oh * 512, 512), in0=ps[:, 0:512],
                    scalar=QS3[:, ts * 8 + e: ts * 8 + e + 1],
                    in1=acc_ap(ts, oh * 512, 512),
                    op0=ALU.mult, op1=ALU.add), deps=[t])
                pfree[id(ps)] = av
        if bi + 1 < len(emcs):
            c2toks = emit_w2_loads(*emcs[bi + 1])
    P.barrier()

    for ts in range(16):
        DG(lambda g, ts=ts: g.dma_start(
            out=h_out[ts * 128:(ts + 1) * 128, :], in_=acc_ap(ts, 0, 1024)))
    P.emit("gpsimd", None, deps=[("gpsimd", P.cnt["gpsimd"])], inc=0)

    with nc.Block() as block:
        P.run(block, sems)
    es.close()
    return nc


_NC_CACHE = {}


def _get_nc(dbg=None):
    if dbg not in _NC_CACHE:
        _NC_CACHE[dbg] = build_nc(dbg)
    return _NC_CACHE[dbg]


def make_inputs(inputs):
    f = np.float32
    import ml_dtypes
    x = np.asarray(inputs["L0_attn_x"], dtype=f)
    import ml_dtypes as _mld
    idx = np.arange(128)
    m_su = np.where(idx[None, :] > idx[:, None], 0.0, NEG).astype(f)
    m_li = np.where(idx[None, :] <= idx[:, None], 0.0, NEG).astype(f)
    mask_own = np.concatenate([m_su, np.zeros((128, 384), f), m_li], axis=1)
    mq = np.concatenate([np.zeros((128, 512), f), mask_own], axis=1)
    common = {
        "w_q0": np.ascontiguousarray(inputs["L0_attn_Wq"].T.astype(f)),
        "w_k0": np.ascontiguousarray(inputs["L0_attn_Wk"].T.astype(f)),
        "w_v0": np.ascontiguousarray(inputs["L0_attn_Wv"].T.astype(f)),
        "w_o0": np.ascontiguousarray(inputs["L0_attn_Wo"].T.astype(f)),
        "w_q1": np.ascontiguousarray(inputs["L1_attn_Wq"].T.astype(f)),
        "w_k1": np.ascontiguousarray(inputs["L1_attn_Wk"].T.astype(f)),
        "w_v1": np.ascontiguousarray(inputs["L1_attn_Wv"].T.astype(f)),
        "w_o1": np.ascontiguousarray(inputs["L1_attn_Wo"].T.astype(f)),
        "w_f0": np.ascontiguousarray(inputs["L0_ffn_weight"].T.astype(f)),
        "w_moe1": np.ascontiguousarray(inputs["L1_ffn_W1"].astype(f)),
        "w_moe2": np.ascontiguousarray(inputs["L1_ffn_W2"].astype(f)),
        "w_gate": np.ascontiguousarray(inputs["L1_ffn_w_gate"].astype(f)),
        "b_gate": np.ascontiguousarray(inputs["L1_ffn_b_gate"].astype(f)
                                       .reshape(8, 1)),
        "b_moe1": np.ascontiguousarray(
            inputs["L1_ffn_b1"].astype(f).reshape(8, 32, 128).transpose(2, 0, 1)
            .reshape(128, 256)),
        "b_moe2": np.ascontiguousarray(inputs["L1_ffn_b2"].astype(f)),
        "b_ffn0": np.ascontiguousarray(inputs["L0_ffn_bias"].astype(f)
                                       .reshape(8, 128).T),
        "mq_bf": mq.astype(_mld.bfloat16),
        "ident_f": np.eye(128, dtype=f),
        "ident_b": np.eye(128, dtype=f).astype(ml_dtypes.bfloat16),
        "ones_c": np.ones((128, 1), f),
        "ones_r": np.ones((33, 128), f),
    }
    in_maps = []
    for c in range(8):
        b, hh = c // 2, c % 2
        xb = x[b]
        own = xb[hh * 2048:(hh + 1) * 2048]
        halo = xb[hh * 2048 - 512: hh * 2048] if hh else np.zeros((512, D), f)
        ext = xb[0:1024]
        m = dict(common)
        m["xin"] = np.ascontiguousarray(np.concatenate([halo, own, ext], axis=0).T)
        bcm = np.empty((128, 2048), f)
        for qt in range(4):
            v = mask_own[:, 0:512].copy()
            if hh == 0:
                v[:, 0:512 - 128 * qt] += NEG
            bcm[:, qt * 512:(qt + 1) * 512] = v
        m["bcm_bf"] = np.ascontiguousarray(bcm.astype(_mld.bfloat16))
        in_maps.append(m)
    return in_maps


def kernel(**inputs):
    nc = _get_nc()
    in_maps = make_inputs(inputs)
    res = run_bass_kernel_spmd(nc, in_maps, core_ids=list(range(8)))
    out = np.empty((4, 4096, D), dtype=np.float32)
    for c in range(8):
        b, hh = c // 2, c % 2
        out[b, hh * 2048:(hh + 1) * 2048] = res.results[c]["out"]
    return out
